# version 19 (final)
# baseline (speedup 1.0000x reference)
import numpy as np
import ml_dtypes
import concourse.bass as bass
import concourse.mybir as mybir
from concourse.bass_utils import run_bass_kernel_spmd

F32 = mybir.dt.float32
BF16 = mybir.dt.bfloat16
I32 = mybir.dt.int32
U8 = mybir.dt.uint8
AF = mybir.ActivationFunctionType
ALU = mybir.AluOpType
INVSQD = 1.0 / float(np.float32(np.sqrt(np.float64(128.0))))
BF16NP = ml_dtypes.bfloat16

ENGS = ("tensor", "vector", "scalar", "gpsimd", "sync")
NEG = -1.0e30



def _ref_rope_tables(cfg):
    T = cfg["T"]
    D = 128
    e = np.arange(0, D, 2).astype(np.float32) / np.float32(D)
    pw = np.power(np.float64(cfg["BASE"]), e.astype(np.float64)).astype(np.float32)
    inv = (np.float32(1.0) / pw).astype(np.float32)
    fr = (np.arange(T).astype(np.float32)[:, None] * inv[None, :]).astype(np.float32)
    try:
        import ctypes, ctypes.util
        libm = ctypes.CDLL(ctypes.util.find_library("m") or "libm.so.6")
        libm.cosf.restype = ctypes.c_float
        libm.cosf.argtypes = [ctypes.c_float]
        libm.sinf.restype = ctypes.c_float
        libm.sinf.argtypes = [ctypes.c_float]
        fl = fr.ravel()
        cos = np.fromiter((libm.cosf(ctypes.c_float(float(v))) for v in fl),
                          np.float32, fl.size).reshape(fr.shape)
        sin = np.fromiter((libm.sinf(ctypes.c_float(float(v))) for v in fl),
                          np.float32, fl.size).reshape(fr.shape)
        return cos, sin
    except Exception:
        return (np.cos(fr.astype(np.float64)).astype(np.float32),
                np.sin(fr.astype(np.float64)).astype(np.float32))


def full_cfg():
    return dict(C=2048, T=2048, H=16, HID=5632, CHUNK=512, MQ=30.0,
                HIDCH=4, EPS=1e-5, BASE=10000.0)


class Ref:
    __slots__ = ("eng", "idx")

    def __init__(self, eng, idx):
        self.eng = eng
        self.idx = idx


NRING = 12


class Em:

    def __init__(self):
        self.events = {e: [] for e in ENGS}
        self.ndma = {e: 0 for e in ENGS}
        self.dma_idx = {e: [] for e in ENGS}

    def ev(self, eng, fn, deps=(), dma=False):
        idx = len(self.events[eng])
        d = [r for r in deps if r is not None]
        slot = None
        if dma:
            slot = self.ndma[eng]
            self.ndma[eng] += 1
            self.dma_idx[eng].append(idx)
        self.events[eng].append((d, fn, dma, slot))
        return Ref(eng, idx)

    def dma(self, eng, out, in_, deps=()):
        def fn(E, out=out, in_=in_):
            return E.dma_start(out=out, in_=in_)
        return self.ev(eng, fn, deps, dma=True)

    def barrier(self):
        refs = [Ref(e, len(self.events[e]) - 1) for e in ENGS if self.events[e]]
        for e in ENGS:
            for i in self.dma_idx[e][-NRING:]:
                refs.append(Ref(e, i))
        return refs

    def finalize(self):
        marked = {e: set() for e in ENGS}
        for eng in ENGS:
            for deps, _, _, _ in self.events[eng]:
                for r in deps:
                    marked[r.eng].add(r.idx)
        for eng in ENGS:
            for i, (_, _, dma, _) in enumerate(self.events[eng]):
                if dma:
                    marked[eng].add(i)
        rank = {e: {} for e in ENGS}
        for eng in ENGS:
            v = 0
            for i in sorted(marked[eng]):
                if self.events[eng][i][2]:
                    continue
                v += 1
                rank[eng][i] = v
        self.rank = rank
        self.marked = marked

    def _dep_key(self, r):
        deps, fn, dma, slot = self.events[r.eng][r.idx]
        if dma:
            return (r.eng, slot % NRING), 16 * (slot // NRING + 1)
        return (r.eng, None), self.rank[r.eng][r.idx]

    def replay(self, eng, E, sems, rings):
        lastw = {}
        for i, (deps, fn, dma, slot) in enumerate(self.events[eng]):
            for r in deps:
                key, v = self._dep_key(r)
                assert v is not None, f"dep on unmarked {r.eng}:{r.idx}"
                if lastw.get(key, 0) >= v:
                    continue
                lastw[key] = v
                sem = sems[key[0]] if key[1] is None else rings[key[0]][key[1]]
                E.wait_ge(sem, v)
            if dma:
                ring_sem = rings[eng][slot % NRING]
                rnd = slot // NRING
                key = (eng, slot % NRING)
                if rnd > 0 and lastw.get(key, 0) < 16 * rnd:
                    lastw[key] = 16 * rnd
                    E.wait_ge(ring_sem, 16 * rnd)
                inst = fn(E)
                inst.then_inc(ring_sem, 16)
            else:
                inst = fn(E)
                if i in self.marked[eng]:
                    inst.then_inc(sems[eng], 1)


def align64(n):
    return ((n + 63) // 64) * 64


def build_nc(cfg):
    C = cfg["C"]; T = cfg["T"]; H = cfg["H"]; HID = cfg["HID"]
    CH = cfg["CHUNK"]; MQ = cfg["MQ"]; HIDCH = cfg["HIDCH"]; EPS = cfg["EPS"]
    CT = C // 128
    TK = T // 128
    NB = TK // 2
    TQ = NB * 128
    QB = NB * 129
    HT = HID // 128
    QTPC = CH // 128
    NCH = TQ // CH
    KTPC = 2 * QTPC
    NTC = T // CH
    NQC = 4 if QB % 4 == 0 else 2
    QCH = QB // NQC
    NHC = (HT + HIDCH - 1) // HIDCH
    HLF = 64
    assert C % 128 == 0 and T % 128 == 0 and TQ % CH == 0 and QB % NQC == 0
    assert NTC <= 4
    assert (T.bit_length() - 1) % 2 == 1, "need odd Hillis-Steele step count"

    nc = bass.Bass("TRN2", target_bir_lowering=False)

    dr = {}

    def din(name, shape, dt):
        dr[name] = nc.dram_tensor(name, list(shape), dt, kind="ExternalInput").ap()

    def dint(name, shape, dt):
        dr[name] = nc.dram_tensor(name, list(shape), dt, kind="Internal").ap()

    din("xkT", (C, T), F32)
    din("xqT", (C, QB), F32)
    din("xqresT", (C, TQ), F32)
    din("wqT", (H, C, 128), F32)
    din("wkT", (H, C, 128), F32)
    din("wvT", (H, C, 128), F32)
    din("wsqT", (CT, 128, H), F32)
    din("wspT", (CT, 128, 3 * H), F32)
    din("bfpack", (3 * H, 1), F32)
    din("woT4", (CT, CT, 128, 128), F32)
    din("wgT", (HT, C, 128), I32)
    din("wuT", (HT, C, 128), I32)
    din("wdT", (HT, 128, C), I32)
    din("sg", (128, HT), F32)
    din("sup", (128, HT), F32)
    din("sdn", (128, CT), F32)
    din("cosK", (128, T), F32)
    din("sinK2", (128, T), F32)
    din("cosQ", (128, TQ), F32)
    din("sinQ2", (128, TQ), F32)
    din("selq", (TK, 128, TQ), BF16)
    din("masks", (TK, 128, CH), U8)
    din("ident", (128, 128), F32)
    din("onescf", (128, 1), F32)
    din("onesrf", (1, 128), F32)
    din("onescb", (128, 1), BF16)
    din("onesrb", (1, 128), BF16)

    outT = nc.dram_tensor("outT", [C, TQ], F32, kind="ExternalOutput").ap()

    dint("xl_sp", (CT, 128, T), BF16)
    dint("xnqh_sp", (CT, 128, TQ), BF16)
    dint("xnql_sp", (CT, 128, TQ), BF16)
    dint("qh_sp", (H, 128, TQ), BF16)
    dint("ql_sp", (H, 128, TQ), BF16)
    dint("oh_sp", (H, 128, TQ), BF16)
    dint("ol_sp", (H, 128, TQ), BF16)

    S = {}
    OFF = {}
    _probe = nc.alloc_sbuf_tensor("_base_probe", [128, 16], mybir.dt.uint8)
    BASE0 = nc.lookup_mloc(_probe).addr + 64
    LIMIT = nc.SBUF_PARTITION_SIZE_BYTES
    cur = [BASE0]

    def sb(name, shape, dt, at=None):
        nbytes = int(np.prod(shape[1:])) * mybir.dt.size(dt)
        if at is None:
            at = cur[0]
            cur[0] = at + align64(nbytes)
        S[name] = nc.alloc_sbuf_tensor_at(name, list(shape), dt, offset=at)
        OFF[name] = at
        return at + align64(nbytes)

    sb("ident", (128, 128), F32)
    sb("onescf", (128, 1), F32)
    sb("onesrf", (1, 128), F32)
    sb("onescb", (128, 1), BF16)
    sb("onesrb", (1, 128), BF16)
    sb("bfpack", (3 * H, 1), F32)
    sb("sdn", (128, CT), F32)
    sb("sg", (128, HT), F32)
    sb("sup", (128, HT), F32)
    B_BASE = cur[0]

    sb("negcumT", (128, H * TK), F32)
    sb("cumbf", (128, H * TK), BF16)
    sb("biasQ", (H, TQ), BF16)
    sb("wsp_f", (128, CT * 3 * H), F32)
    sb("wsp_h", (128, CT * 3 * H), BF16)
    sb("wsp_l", (128, CT * 3 * H), BF16)
    sb("gates", (3 * H, T), F32)
    sb("cosK", (128, T), F32)
    sb("sinK2", (128, T), F32)
    sb("masks", (128, TK * CH), U8)
    sb("rowbuf", (1, max(T, 4 * QCH, 4 * CH)), F32)
    sb("bhrow", (1, 2 * TQ), BF16)
    R1 = cur[0]
    sb("cosQ", (128, TQ), F32)
    sb("sinQ2", (128, TQ), F32)
    R1_END = cur[0]
    e_off = sb("ef", (128, 2 * CH), F32, at=R1)
    e_off = sb("eh", (128, 2 * CH), BF16, at=e_off)
    e_off = sb("el", (128, 2 * CH), BF16, at=e_off)
    assert e_off <= R1_END
    A2_BASE = cur[0]
    sb("xnh", (128, CT * T), BF16)
    sb("xlp", (128, 2 * T), BF16)
    sb("stage", (128, T), F32)
    sb("wv_h", (128, CT * 128), BF16)
    sb("wv_l", (128, CT * 128), BF16)
    sb("wk_h", (128, CT * 128), BF16)
    sb("wk_l", (128, CT * 128), BF16)
    sb("vtm_h", (128, TK * 128), BF16)
    sb("vtm_l", (128, TK * 128), BF16)
    sb("y_k", (128, T + 4), F32)
    sb("dK", (128, T + 4), F32)
    sb("K_h", (128, T), BF16)
    sb("K_l", (128, T), BF16)
    sb("qt_h", (128, TQ), BF16)
    sb("qt_l", (128, TQ), BF16)
    sb("o_h", (128, 2 * CH), BF16)
    sb("o_l", (128, 2 * CH), BF16)
    sb("invlbc", (128, CH), F32)
    sb("krotb", (128, CH), F32)
    sb("rch2", (128, CH), F32)
    sb("invlrow", (1, CH), F32)
    A_END = cur[0]
    nc._dbg_a_end = (A_END, LIMIT)
    assert A_END <= LIMIT, f"A-phase SBUF overflow: {A_END}"

    sb("cumA", (H, T), F32, at=OFF["vtm_h"])
    sb("cumB", (H, T), F32, at=OFF["K_h"])
    sb("selp", (128, 2 * TQ), BF16, at=OFF["o_h"])
    sb("rbc", (128, T), F32, at=OFF["vtm_h"])
    sb("mrbc", (128, T), F32, at=OFF["y_k"])
    sb("x2b", (128, T), F32, at=OFF["dK"])
    sb("lotile", (128, 2 * T), BF16, at=OFF["K_h"])

    p = OFF["xnh"]
    p = sb("xq_st", (128, 2 * QB), F32, at=p)
    p = sb("y_q", (128, QB), F32, at=p)
    p = sb("xnq_h", (128, CT * QB), BF16, at=p)
    p = sb("xnq_l", (128, CT * QB), BF16, at=p)
    p = sb("dQ", (128, TQ), F32, at=p)
    p = sb("qrope", (128, TQ), F32, at=p)
    p = sb("rbcq", (128, QB), F32, at=p)
    p = sb("mrbcq", (128, QB), F32, at=p)
    p = sb("wq_st", (128, CT * 128), F32, at=p)
    p = sb("wq_h", (128, CT * 128), BF16, at=p)
    p = sb("wq_l", (128, CT * 128), BF16, at=p)
    p = sb("wsq_f", (128, CT * H), F32, at=p)
    p = sb("wsq_h", (128, CT * H), BF16, at=p)
    p = sb("wsq_l", (128, CT * H), BF16, at=p)
    p = sb("pq", (H, QB), F32, at=p)
    p = sb("pqrow", (1, TQ), F32, at=p)
    p = sb("rch", (128, CH), F32, at=p)
    assert p <= LIMIT, f"A1 SBUF overflow: {p}"

    p = B_BASE
    p = sb("bxnq_h", (128, CT * TQ), BF16, at=p)
    p = sb("bxnq_l", (128, CT * TQ), BF16, at=p)
    p = sb("g_acc", (128, CT * TQ), F32, at=p)
    B_AFTER_G = p
    p = sb("su_h", (128, HIDCH * TQ), BF16, at=p)
    p = sb("su_l", (128, HIDCH * TQ), BF16, at=p)
    p = sb("wint", (128, CT * 128), I32, at=p)
    p = sb("wbf", (128, 4 * CT * 128), BF16, at=p)
    p = sb("wdc", (128, HIDCH * CT * 128), BF16, at=p)
    p = sb("su_f", (128, CH), F32, at=p)
    p = sb("gsil", (128, 2 * CH), F32, at=p)
    assert p <= LIMIT, f"B1 SBUF overflow: {p}"
    p = sb("ob_h", (128, H * TQ), BF16, at=B_BASE)
    p = sb("ob_l", (128, H * TQ), BF16, at=p)
    assert p <= B_AFTER_G - (B_AFTER_G - B_BASE) + (B_AFTER_G - B_BASE), "ob overlay"
    p = B_AFTER_G
    p = sb("wo_h", (128, 2 * CT * 128), BF16, at=p)
    p = sb("wo_l", (128, 2 * CT * 128), BF16, at=p)
    p = sb("wo_f", (128, 2 * CT * 128), F32, at=p)
    p = sb("xres", (128, 2 * CH), F32, at=p)
    p = sb("outsb", (128, 2 * CH), F32, at=p)
    p = sb("gsil2", (128, 2 * CH), F32, at=p)
    assert p <= LIMIT, f"B2 SBUF overflow: {p}"

    PS = []
    for i in range(8):
        PS.append(nc.alloc_psum_tensor(f"bank{i}", [128, max(CH, 512)], F32))

    E = Em()

    def mm(out, lhsT, rhs, start, stop, deps=()):
        return E.ev("tensor", (lambda Eng, out=out, lhsT=lhsT, rhs=rhs,
                               start=start, stop=stop:
                               Eng.matmul(out, lhsT, rhs, start=start, stop=stop)), deps)

    def tr(out, in_, ident, deps=()):
        return E.ev("tensor", (lambda Eng, out=out, in_=in_, ident=ident:
                               Eng.transpose(out, in_, ident)), deps)

    def vop(fn, deps=()):
        return E.ev("vector", fn, deps)

    def sop(fn, deps=()):
        return E.ev("scalar", fn, deps)

    def gop(fn, deps=()):
        return E.ev("gpsimd", fn, deps)

    ld = {}
    for nm in ("ident", "onescf", "onesrf", "onescb", "onesrb", "bfpack",
               "sdn", "sg", "sup", "cosK", "sinK2", "cosQ", "sinQ2"):
        ld[nm] = E.dma("sync", S[nm][:], dr[nm])
    ld["masks"] = E.dma("sync", S["masks"][:].rearrange("p (a d) -> p a d", d=CH),
                        dr["masks"].rearrange("a p d -> p a d"))
    ld["wsp"] = E.dma("sync", S["wsp_f"][:].rearrange("p (a d) -> p a d", d=3 * H),
                      dr["wspT"].rearrange("a p d -> p a d"))
    ld["wsq"] = E.dma("sync", S["wsq_f"][:].rearrange("p (a d) -> p a d", d=H),
                      dr["wsqT"].rearrange("a p d -> p a d"))
    wsp_h = vop(lambda V: V.tensor_copy(S["wsp_h"][:], S["wsp_f"][:]), [ld["wsp"]])
    wsp_l = vop(lambda V: V.tensor_tensor(S["wsp_l"][:], S["wsp_f"][:], S["wsp_h"][:], ALU.subtract), [wsp_h])
    wsq_h = vop(lambda V: V.tensor_copy(S["wsq_h"][:], S["wsq_f"][:]), [ld["wsq"]])
    wsq_l = vop(lambda V: V.tensor_tensor(S["wsq_l"][:], S["wsq_f"][:], S["wsq_h"][:], ALU.subtract), [wsq_h])

    def ln_stream(src_dr, ncols, nchunk, csz, stage, x2a, t2a, rbca, mrbca,
                  rows, hi_dst, lo_sink, deps0):
        rm = rows
        stat_last = []
        prev = list(deps0)
        x2_prev = []
        for ct in range(CT):
            d = E.dma("sync", stage, src_dr[128 * ct:128 * (ct + 1), :], deps=prev)
            x2 = vop(lambda V: V.tensor_tensor(x2a, stage, stage, ALU.mult),
                     [d] + stat_last[-1:] + x2_prev)
            x2_prev = [x2]
            ms = []
            for q in range(nchunk):
                sl = slice(csz * q, csz * (q + 1))
                ms.append(mm(PS[q][0:1, 0:csz], S["onescf"][:], stage[:, sl],
                             start=(ct == 0), stop=(ct == CT - 1), deps=[d]))
                ms.append(mm(PS[4 + q][0:1, 0:csz], S["onescf"][:], x2a[:, sl],
                             start=(ct == 0), stop=(ct == CT - 1), deps=[x2]))
            stat_last = ms
            prev = [ms[-1], x2]
        bcd = []
        prev_mm = []
        for q in range(nchunk):
            sl = slice(csz * q, csz * (q + 1))
            s0 = rm[:, 0:csz]
            s1 = rm[:, csz:2 * csz]
            s2 = rm[:, 2 * csz:3 * csz]
            s3 = rm[:, 3 * csz:4 * csz]
            e1 = vop(lambda V, q=q, s0=s0: V.tensor_scalar_mul(s0, PS[q][0:1, 0:csz], 1.0 / C), stat_last + prev_mm)
            e2 = vop(lambda V, q=q, s1=s1: V.tensor_scalar_mul(s1, PS[4 + q][0:1, 0:csz], 1.0 / C), stat_last + prev_mm)
            a = vop(lambda V, s0=s0, s3=s3: V.tensor_tensor(s3, s0, s0, ALU.mult), [e1])
            a = vop(lambda V, s1=s1, s3=s3: V.tensor_tensor(s1, s1, s3, ALU.subtract), [e2, a])
            a = vop(lambda V, s1=s1, s3=s3: V.tensor_scalar_add(s3, s1, float(EPS)), [a])
            a = sop(lambda Sc, s3=s3: Sc.activation(s3, s3, AF.Sqrt), [a])
            a = vop(lambda V, s1=s1, s3=s3: V.reciprocal(s1, s3), [a])
            b = vop(lambda V, s0=s0, s1=s1, s2=s2: V.tensor_tensor(s2, s0, s1, ALU.mult), [a])
            m1 = mm(PS[q][:, 0:csz], S["onesrf"][:], s1, start=True, stop=True, deps=[a, e1] + bcd)
            v1 = vop(lambda V, q=q, sl=sl: V.tensor_copy(rbca[:, sl], PS[q][:, 0:csz]), [m1])
            m2 = mm(PS[4 + q][:, 0:csz], S["onesrf"][:], s2, start=True, stop=True, deps=[b] + bcd)
            v2 = vop(lambda V, q=q, sl=sl: V.tensor_copy(mrbca[:, sl], PS[4 + q][:, 0:csz]), [m2])
            bcd += [v1, v2]
            prev_mm = [m1, m2]
        bcd = bcd[-2 * nchunk:]
        tiles = []
        prev = []
        for ct in range(CT):
            d = E.dma("sync", stage, src_dr[128 * ct:128 * (ct + 1), :], deps=bcd + prev)
            t1 = vop(lambda V: V.tensor_tensor(x2a, stage, rbca, ALU.mult), [d])
            hd = hi_dst(ct)
            hi = vop(lambda V, hd=hd: V.tensor_tensor(hd, x2a, mrbca, ALU.subtract), [t1])
            t2 = vop(lambda V: V.tensor_tensor(t2a, x2a, mrbca, ALU.subtract), [t1])
            refs = lo_sink(ct, hi, t2)
            prev = [t2, hi] + refs
            tiles.append((hi, refs))
        return tiles

    a1_rows = S["rowbuf"][0:1, 0:4 * QCH]
    stage1 = S["xq_st"][:, 0:QB]
    t2a1 = S["xq_st"][:, QB:2 * QB]

    def a1_hi(ct):
        return S["xnq_h"][:, ct * QB:(ct + 1) * QB]

    def a1_lo(ct, hi, t2):
        lo = vop(lambda V, ct=ct: V.tensor_tensor(
            S["xnq_l"][:, ct * QB:(ct + 1) * QB], t2a1,
            S["xnq_h"][:, ct * QB:(ct + 1) * QB], ALU.subtract), [t2, hi])
        return [lo]

    a1_tiles = ln_stream(dr["xqT"], QB, NQC, QCH, stage1, S["y_q"][:, 0:QB],
                         t2a1, S["rbcq"][:], S["mrbcq"][:], a1_rows, a1_hi, a1_lo, [])

    pq_ev = None
    for q in range(NQC):
        last = None
        for ct in range(CT):
            wsl = slice(ct * H, (ct + 1) * H)
            xsl = slice(ct * QB + QCH * q, ct * QB + QCH * (q + 1))
            d0 = [wsq_h, a1_tiles[ct][0]] + ([pq_ev] if pq_ev else [])
            mm(PS[7][0:H, 0:QCH], S["wsq_h"][:, wsl], S["xnq_h"][:, xsl],
               start=(ct == 0), stop=False, deps=d0)
            mm(PS[7][0:H, 0:QCH], S["wsq_h"][:, wsl], S["xnq_l"][:, xsl],
               start=False, stop=False, deps=[a1_tiles[ct][1][0]])
            last = mm(PS[7][0:H, 0:QCH], S["wsq_l"][:, wsl], S["xnq_h"][:, xsl],
                      start=False, stop=(ct == CT - 1), deps=[wsq_l])
        pq_ev = sop(lambda Sc, q=q: Sc.activation(S["pq"][:, QCH * q:QCH * (q + 1)],
                                                  PS[7][0:H, 0:QCH], AF.Sigmoid), [last])

    QV = S["y_q"][:].rearrange("p (b c) -> p b c", c=129)
    DQV = S["dQ"][:].rearrange("p (b c) -> p b c", c=128)
    q_spills = []
    wq_free = []
    vd_prev = None
    vrow_d = {}
    km_prev = {}
    rope_prev = []
    evy_hist = {}
    for h in range(H):
        dW = E.dma("sync", S["wq_st"][:].rearrange("p (a d) -> p a d", d=128),
                   dr["wqT"][h].rearrange("(a p) d -> p a d", p=128), deps=wq_free)
        wh = vop(lambda V: V.tensor_copy(S["wq_h"][:], S["wq_st"][:]), [dW])
        wl = vop(lambda V: V.tensor_tensor(S["wq_l"][:], S["wq_st"][:], S["wq_h"][:], ALU.subtract), [wh])
        wq_free = [wl]
        PQV = S["pq"][:].rearrange("p (b c) -> p b c", c=129)
        vrow = E.dma("scalar", S["pqrow"][:].rearrange("p (b c) -> p b c", c=128),
                     PQV[h:h + 1, :, 1:129], deps=[pq_ev] + km_prev.get(h - 1, []))
        ev_y = []
        for q in range(NQC):
            pp = PS[4 + q]
            last = None
            for ct in range(CT):
                wsl = slice(ct * 128, (ct + 1) * 128)
                xsl = slice(ct * QB + QCH * q, ct * QB + QCH * (q + 1))
                d0 = [wh] + (evy_hist.get((h - 1, q), []) if ct == 0 else [])
                if ct == 0 and h == 0:
                    d0.append(pq_ev)
                if ct == 0 and q == 0 and vd_prev is not None:
                    d0.append(vd_prev)
                mm(pp[:, 0:QCH], S["wq_h"][:, wsl], S["xnq_h"][:, xsl], start=(ct == 0), stop=False, deps=d0)
                mm(pp[:, 0:QCH], S["wq_h"][:, wsl], S["xnq_l"][:, xsl], start=False, stop=False)
                last = mm(pp[:, 0:QCH], S["wq_l"][:, wsl], S["xnq_h"][:, xsl], start=False, stop=(ct == CT - 1), deps=[wl])
            ev_y.append(vop(lambda V, q=q, pp=pp: V.tensor_scalar_mul(S["y_q"][:, QCH * q:QCH * (q + 1)], pp[:, 0:QCH], INVSQD),
                            [last] + ([vd_prev] if vd_prev is not None else [])))
            evy_hist[(h, q)] = [ev_y[-1]]
        bcms = {}
        for q2 in range(NCH):
            sl = slice(CH * q2, CH * (q2 + 1))
            bcms[q2] = mm(PS[q2 % 2][:, 0:CH], S["onesrf"][:], S["pqrow"][0:1, sl], start=True, stop=True,
                          deps=[vrow] + km_prev.get((h - 1, q2), []))
        vd = vop(lambda V: V.tensor_tensor(DQV, QV[:, :, 1:129], QV[:, :, 0:128], ALU.subtract), ev_y)
        vd_prev = vd
        ev_q = []
        kms = []
        for q2 in range(NCH):
            sl = slice(CH * q2, CH * (q2 + 1))
            km = vop(lambda V, sl=sl, q2=q2: V.tensor_tensor(S["rch"][:, 0:CH], PS[q2 % 2][:, 0:CH], S["dQ"][:, sl], ALU.mult),
                     [bcms[q2], vd] + ev_q[-1:])
            km_prev[(h, q2)] = [km]
            nblk = CH // 128
            yo = QV[:, q2 * nblk:(q2 + 1) * nblk, 1:129]
            ev_q.append(vop(lambda V, sl=sl, yo=yo: V.tensor_tensor(
                S["qrope"][:, sl].rearrange("p (b c) -> p b c", c=128), yo,
                S["rch"][:, 0:CH].rearrange("p (b c) -> p b c", c=128), ALU.subtract), [km]))
            kms.append(km)
        km_prev[h] = kms
        ev_sp = []
        for q2 in range(NCH):
            sl = slice(CH * q2, CH * (q2 + 1))
            c1 = vop(lambda V, sl=sl: V.tensor_copy(S["krotb"][0:HLF, 0:CH], S["qrope"][:, sl][HLF:128, :]),
                     ev_q + ev_sp[-1:])
            c2 = vop(lambda V, sl=sl: V.tensor_copy(S["krotb"][HLF:128, 0:CH], S["qrope"][:, sl][0:HLF, :]), [c1])
            t1 = vop(lambda V, sl=sl: V.tensor_tensor(S["rch"][:, 0:CH], S["qrope"][:, sl], S["cosQ"][:, sl], ALU.mult),
                     [c2, ld["cosQ"]])
            t2 = vop(lambda V, sl=sl: V.tensor_tensor(S["krotb"][:, 0:CH], S["krotb"][:, 0:CH], S["sinQ2"][:, sl], ALU.mult),
                     [t1, ld["sinQ2"]])
            hi = vop(lambda V, sl=sl: V.tensor_tensor(S["qt_h"][:, sl], S["rch"][:, 0:CH], S["krotb"][:, 0:CH], ALU.add),
                     [t2] + (q_spills if q2 == 0 else []))
            t3 = vop(lambda V, sl=sl: V.tensor_tensor(S["rch"][:, 0:CH], S["rch"][:, 0:CH], S["krotb"][:, 0:CH], ALU.add), [hi])
            lo = vop(lambda V, sl=sl: V.tensor_tensor(S["qt_l"][:, sl], S["rch"][:, 0:CH], S["qt_h"][:, sl], ALU.subtract), [t3])
            ev_sp += [lo, hi]
        s1 = E.dma("scalar", dr["qh_sp"][h], S["qt_h"][:], deps=ev_sp + q_spills[-2:])
        s2 = E.dma("scalar", dr["ql_sp"][h], S["qt_l"][:], deps=ev_sp)
        q_spills = [s1, s2]
    xnq_sp = []
    for ct in range(CT):
        for nm, spn in (("xnq_h", "xnqh_sp"), ("xnq_l", "xnql_sp")):
            src = S[nm][:, ct * QB:(ct + 1) * QB].rearrange("p (b c) -> p b c", c=129)[:, :, 1:129]
            xnq_sp.append(E.dma("scalar", dr[spn][ct].rearrange("p (b c) -> p b c", c=128), src,
                                deps=q_spills if ct == 0 else []))
    A1_END = E.barrier()

    a2_rows = S["rowbuf"][0:1, 0:4 * CH]
    gate_last = []
    xl_sp_refs = []

    def a2_hi(ct):
        return S["xnh"][:, ct * T:(ct + 1) * T]

    def a2_lo(ct, hi, t2):
        lob = S["lotile"][:, (ct % 2) * T:(ct % 2) * T + T]
        lo = vop(lambda V, lob=lob, ct=ct: V.tensor_tensor(
            lob, S["stage"][:], S["xnh"][:, ct * T:(ct + 1) * T], ALU.subtract),
            [t2, hi] + (xl_sp_refs[-2:-1] if len(xl_sp_refs) >= 2 else []))
        spr = E.dma("scalar", dr["xl_sp"][ct], lob, deps=[lo])
        xl_sp_refs.append(spr)
        gm = []
        for q in range(NTC):
            sl = slice(CH * q, CH * (q + 1))
            wsl = slice(ct * 3 * H, (ct + 1) * 3 * H)
            gm.append(mm(PS[q][0:3 * H, 0:CH], S["wsp_h"][:, wsl],
                         S["xnh"][:, ct * T + CH * q:ct * T + CH * (q + 1)],
                         start=(ct == 0), stop=False, deps=[hi, wsp_h]))
            gm.append(mm(PS[q][0:3 * H, 0:CH], S["wsp_h"][:, wsl], lob[:, sl],
                         start=False, stop=False, deps=[lo]))
            gm.append(mm(PS[q][0:3 * H, 0:CH], S["wsp_l"][:, wsl],
                         S["xnh"][:, ct * T + CH * q:ct * T + CH * (q + 1)],
                         start=False, stop=(ct == CT - 1), deps=[wsp_l]))
        gate_last[:] = gm
        return [lo, spr, gm[-1]]

    ln_stream(dr["xkT"], T, NTC, CH, S["stage"][:], S["x2b"][:], S["stage"][:],
              S["rbc"][:], S["mrbc"][:], a2_rows, a2_hi, a2_lo, A1_END)
    gate_sig = []
    for q in range(NTC):
        gate_sig.append(sop(lambda Sc, q=q: Sc.activation(
            S["gates"][:, CH * q:CH * (q + 1)], PS[q][0:3 * H, 0:CH],
            AF.Sigmoid, bias=S["bfpack"][:], scale=1.0), gate_last + [ld["bfpack"]]))
    ln_ev = sop(lambda Sc: Sc.activation(S["gates"][0:H, :], S["gates"][0:H, :], AF.Ln), gate_sig)
    cpy = vop(lambda V: V.tensor_copy(S["cumB"][:], S["gates"][0:H, :]), [ln_ev, xl_sp_refs[-1]])
    cur_row = 1
    hs = cpy
    sh = 1
    while sh < T:
        srcb = S["cumB"][:] if cur_row == 1 else S["cumA"][:]
        dstb = S["cumA"][:] if cur_row == 1 else S["cumB"][:]
        h1 = vop(lambda V, srcb=srcb, dstb=dstb, sh=sh: V.tensor_tensor(
            dstb[:, sh:T], srcb[:, sh:T], srcb[:, 0:T - sh], ALU.add), [hs])
        hs = vop(lambda V, srcb=srcb, dstb=dstb, sh=sh: V.tensor_copy(dstb[:, 0:sh], srcb[:, 0:sh]), [h1])
        cur_row = 1 - cur_row
        sh *= 2
    assert cur_row == 0
    CUMR = S["cumA"][:]
    nct = [hs]
    for kt in range(TK):
        t0 = tr(PS[7][:, 0:H], CUMR[:, 128 * kt:128 * (kt + 1)], S["ident"][0:H, 0:H],
                deps=[hs] + nct[-2:])
        v1 = vop(lambda V, kt=kt: V.tensor_scalar_mul(S["negcumT"][:, H * kt:H * (kt + 1)], PS[7][:, 0:H], -1.0), [t0])
        v2 = vop(lambda V, kt=kt: V.tensor_copy(S["cumbf"][:, H * kt:H * (kt + 1)], PS[7][:, 0:H]), [t0])
        nct += [v1, v2]
    bq = []
    seld_hist = []
    for q2 in range(NCH):
        last = None
        for kt in range(TK):
            buf = kt % 2
            dsel = E.dma("sync", S["selp"][:, buf * TQ:buf * TQ + TQ], dr["selq"][kt],
                         deps=(seld_hist[-2:-1] if len(seld_hist) >= 2 else []))
            last = mm(PS[6][0:H, 0:CH], S["cumbf"][:, H * kt:H * (kt + 1)],
                      S["selp"][:, buf * TQ + CH * q2:buf * TQ + CH * (q2 + 1)],
                      start=(kt == 0), stop=(kt == TK - 1), deps=[dsel] + nct[-2:] + bq)
            seld_hist.append(last)
        bq = [vop(lambda V, q2=q2: V.tensor_scalar_add(
            S["biasQ"][:, CH * q2:CH * (q2 + 1)], PS[6][0:H, 0:CH], -float(MQ)), [last])]
    biasQ_done = bq[0]

    assert NCH == 2 and NTC == 4
    zcol = vop(lambda V: V.memset(S["y_k"][:, 0:1], 0.0), [biasQ_done])
    zcol2 = vop(lambda V: V.memset(S["dK"][:, 0:1], 0.0), [zcol])
    o_spills = []
    st = {}
    st["sc_last"] = [biasQ_done]
    st["sbank"] = {}
    st["pbank"] = {}
    st["obank"] = []
    st["lbank"] = []
    st["wfree"] = [zcol2]
    st["stagefree"] = [biasQ_done]
    st["ykfree"] = {}
    st["ropece"] = {}
    st["vgate"] = {}
    st["kgate"] = {}
    st["pvbc"] = {}
    st["pkbc"] = {}
    st["E"] = {}
    st["qld"] = {}
    st["bh"] = {}
    st["oev"] = []
    st["elast"] = {}
    st["xl_rd"] = {}
    st["trs"] = {}
    st["xln"] = 0

    def emit_wq_loads(h):
        dWv = E.dma("sync", S["stage"][:, 0:CT * 128].rearrange("p (a d) -> p a d", d=128),
                    dr["wvT"][h].rearrange("(a p) d -> p a d", p=128), deps=st["stagefree"])
        wvh = vop(lambda V: V.tensor_copy(S["wv_h"][:], S["stage"][:, 0:CT * 128]), [dWv] + st["wfree"])
        wvl = vop(lambda V: V.tensor_tensor(S["wv_l"][:], S["stage"][:, 0:CT * 128], S["wv_h"][:], ALU.subtract), [wvh])
        dWk = E.dma("sync", S["stage"][:, 0:CT * 128].rearrange("p (a d) -> p a d", d=128),
                    dr["wkT"][h].rearrange("(a p) d -> p a d", p=128), deps=[wvl])
        wkh = vop(lambda V: V.tensor_copy(S["wk_h"][:], S["stage"][:, 0:CT * 128]), [dWk])
        wkl = vop(lambda V: V.tensor_tensor(S["wk_l"][:], S["stage"][:, 0:CT * 128], S["wk_h"][:], ALU.subtract), [wkh])
        st["stagefree"] = [wkl]
        st.setdefault("W", {})[h] = (wvh, wvl, wkh, wkl)
        par = h % 2
        bh = E.dma("scalar", S["bhrow"][:, par * TQ:(par + 1) * TQ], S["biasQ"][h:h + 1, :],
                   deps=[biasQ_done] + st["sc_last"])
        st["bh"][h] = [bh]

    def emit_qload(h):
        dQh = E.dma("scalar", S["qt_h"][:], dr["qh_sp"][h], deps=st["sc_last"] + ([q_spills[0]] if h == 0 else []))
        dQl = E.dma("scalar", S["qt_l"][:], dr["ql_sp"][h], deps=([q_spills[1]] if h == 0 else []))
        st["qld"][h] = [dQh, dQl]

    def _proj_ct(h, qp, ct):
        wvh, wvl, wkh, wkl = st["W"][h]
        W2 = 2 * CH
        li = st["xln"]
        st["xln"] = li + 1
        buf = li % 4
        xlb = S["xlp"][:, buf * W2:(buf + 1) * W2]
        dxl = E.dma("sync", xlb, dr["xl_sp"][ct][:, qp * W2:(qp + 1) * W2],
                    deps=[xl_sp_refs[ct]] + st["xl_rd"].get(li - 4, []))
        wsl = slice(ct * 128, (ct + 1) * 128)
        refs = []
        for qq in range(2):
            q = 2 * qp + qq
            xsl = slice(ct * T + CH * q, ct * T + CH * (q + 1))
            lsl = slice(CH * qq, CH * (qq + 1))
            d0v = [wvh] + (st["pbank"].pop(4 + qq, []) if ct == 0 else [])
            d0k = [wkh] + (st["pbank"].pop(6 + qq, []) if ct == 0 else [])
            mm(PS[4 + qq][:, 0:CH], S["wv_h"][:, wsl], S["xnh"][:, xsl], start=(ct == 0), stop=False, deps=d0v)
            mm(PS[4 + qq][:, 0:CH], S["wv_l"][:, wsl], S["xnh"][:, xsl], start=False, stop=False, deps=[wvl])
            r1 = mm(PS[4 + qq][:, 0:CH], S["wv_h"][:, wsl], xlb[:, lsl], start=False, stop=(ct == CT - 1), deps=[dxl])
            mm(PS[6 + qq][:, 0:CH], S["wk_h"][:, wsl], S["xnh"][:, xsl], start=(ct == 0), stop=False, deps=d0k)
            mm(PS[6 + qq][:, 0:CH], S["wk_l"][:, wsl], S["xnh"][:, xsl], start=False, stop=False, deps=[wkl])
            r2 = mm(PS[6 + qq][:, 0:CH], S["wk_h"][:, wsl], xlb[:, lsl], start=False, stop=(ct == CT - 1), deps=[dxl])
            refs += [r1, r2]
        st["xl_rd"][li] = refs

    def _proj_evacs(h, qp):
        vev = []
        kev = []
        for qq in range(2):
            q = 2 * qp + qq
            lastr = st["xl_rd"][st["xln"] - 1]
            yw = st["ykfree"].pop((h - 1, q), [])
            ve = vop(lambda V, q=q: V.tensor_copy(S["y_k"][:, 1 + CH * q:1 + CH * (q + 1)], PS[4 + (q % 2)][:, 0:CH]),
                     lastr + yw + [zcol])
            ke = vop(lambda V, q=q: V.tensor_copy(S["dK"][:, 1 + CH * q:1 + CH * (q + 1)], PS[6 + (q % 2)][:, 0:CH]),
                     lastr + st["ropece"].get(h - 1, []) + [zcol2])
            vev.append(ve)
            kev.append(ke)
            st["pbank"][4 + qq] = [ve]
            st["pbank"][6 + qq] = [ke]
        st[("vev", h, qp)] = vev
        st[("kev", h, qp)] = kev

    def _pvbc(h):
        refs = {}
        mms = []
        for q in range(NTC):
            sl = slice(CH * q, CH * (q + 1))
            dep0 = st["grow_ld"] + st["pbank"].pop(7 - q, [])
            bm = mm(PS[7 - q][:, 0:CH], S["onesrf"][:], S["rowbuf"][0:1, sl], start=True, stop=True, deps=dep0)
            refs[q] = bm
            mms.append(bm)
        st["pvbc"][h] = refs
        st["grow_free"] = mms

    def make_proj_closures(h):
        cl = []
        for qp in range(2):
            for ct in range(CT):
                cl.append(lambda qp=qp, ct=ct: _proj_ct(h, qp, ct))
            cl.append(lambda qp=qp: _proj_evacs(h, qp))
        cl.append(lambda: _pvbc(h))
        cl.append(lambda: emit_growload(h, "k"))
        return cl

    def emit_growload(h, which):
        row = (2 * H + h) if which == "v" else (H + h)
        gv = E.dma("scalar", S["rowbuf"][0:1, 0:T], S["gates"][row:row + 1, :],
                   deps=gate_sig + st.get("grow_free", []))
        st["grow_ld"] = [gv]
    def emit_growload(h, which):
        row = (2 * H + h) if which == "v" else (H + h)
        gv = E.dma("scalar", S["rowbuf"][0:1, 0:T], S["gates"][row:row + 1, :],
                   deps=gate_sig + st.get("grow_free", []))
        st["grow_ld"] = [gv]

    def emit_pvbc(h):
        refs = {}
        mms = []
        for q in range(NTC):
            sl = slice(CH * q, CH * (q + 1))
            dep0 = st["grow_ld"] + st["pbank"].pop(7 - q, [])
            bm = mm(PS[7 - q][:, 0:CH], S["onesrf"][:], S["rowbuf"][0:1, sl], start=True, stop=True, deps=dep0)
            refs[q] = bm
            mms.append(bm)
        st["pvbc"][h] = refs
        st["grow_free"] = mms

    def emit_vgate(h):
        vg = {}
        prev = []
        for q in range(NTC - 1, -1, -1):
            sl = slice(CH * q, CH * (q + 1))
            dch = vop(lambda V, sl=sl: V.tensor_tensor(S["krotb"][:, 0:CH], S["y_k"][:, 1 + sl.start:1 + sl.stop],
                                                       S["y_k"][:, sl.start:sl.stop], ALU.subtract),
                      [st[("vev", h, q // 2)][q % 2], st["pvbc"][h][q]] + prev + st["ropece"].get(h - 1, []))
            km = vop(lambda V, q=q: V.tensor_tensor(S["rch2"][:, 0:CH], PS[7 - q][:, 0:CH], S["krotb"][:, 0:CH], ALU.mult), [dch])
            vgq = vop(lambda V, sl=sl: V.tensor_tensor(S["y_k"][:, 1 + sl.start:1 + sl.stop],
                                                       S["y_k"][:, 1 + sl.start:1 + sl.stop], S["rch2"][:, 0:CH], ALU.subtract), [km])
            vg[q] = vgq
            prev = [vgq]
            st["pbank"][7 - q] = [km]
        st["vgate"][h] = vg

    def emit_pkbc(h):
        refs = {}
        mms = []
        for q in range(NTC):
            sl = slice(CH * q, CH * (q + 1))
            bm = mm(PS[7 - q][:, 0:CH], S["onesrf"][:], S["rowbuf"][0:1, sl], start=True, stop=True,
                    deps=st["grow_ld"] + st["pbank"].pop(7 - q, []))
            refs[q] = bm
            mms.append(bm)
        st["pkbc"][h] = refs
        st["grow_free"] = mms

    def emit_transp(h):
        trs = {}
        prev = {}
        for stt_ in range(TK):
            bank = PS[stt_ % 2]
            t0 = tr(bank[:, 0:128], S["y_k"][:, 1 + 128 * stt_:1 + 128 * (stt_ + 1)], S["ident"][:],
                    deps=[st["vgate"][h][stt_ // (TK // NTC)]] + st["sbank"].pop(stt_ % 2, []) + prev.get(stt_ % 2, []))
            hi = sop(lambda Sc, stt_=stt_, bank=bank: Sc.copy(S["vtm_h"][:, 128 * stt_:128 * (stt_ + 1)], bank[:, 0:128]), [t0])
            lo = vop(lambda V, stt_=stt_, bank=bank: V.tensor_tensor(
                S["vtm_l"][:, 128 * stt_:128 * (stt_ + 1)], bank[:, 0:128],
                S["vtm_h"][:, 128 * stt_:128 * (stt_ + 1)], ALU.subtract), [hi])
            prev[stt_ % 2] = [lo]
            st["ykfree"][(h, stt_ // (TK // NTC))] = [t0]
            trs[stt_] = [t0, hi, lo]
        st["sbank"][0] = prev.get(0, [])
        st["sbank"][1] = prev.get(1, [])
        st["trs"][h] = [trs[TK - 1][2], trs[TK - 2][2]]

    def emit_kgate_rope(h):
        kg = {}
        prev = []
        for q in range(NTC - 1, -1, -1):
            sl = slice(CH * q, CH * (q + 1))
            dch = vop(lambda V, sl=sl: V.tensor_tensor(S["krotb"][:, 0:CH], S["dK"][:, 1 + sl.start:1 + sl.stop],
                                                       S["dK"][:, sl.start:sl.stop], ALU.subtract),
                      [st[("kev", h, q // 2)][q % 2], st["pkbc"][h][q]] + prev)
            km = vop(lambda V, q=q: V.tensor_tensor(S["rch2"][:, 0:CH], PS[7 - q][:, 0:CH], S["krotb"][:, 0:CH], ALU.mult), [dch])
            kgq = vop(lambda V, sl=sl: V.tensor_tensor(S["dK"][:, 1 + sl.start:1 + sl.stop],
                                                       S["dK"][:, 1 + sl.start:1 + sl.stop], S["rch2"][:, 0:CH], ALU.subtract), [km])
            kg[q] = kgq
            prev = [kgq]
            st["pbank"][7 - q] = [km]
        rope_refs = []
        sc_prev = st["sc_last"]
        for q in range(NTC):
            sl = slice(CH * q, CH * (q + 1))
            c1 = vop(lambda V, sl=sl: V.tensor_copy(S["krotb"][0:HLF, 0:CH], S["dK"][:, 1 + sl.start:1 + sl.stop][HLF:128, :]),
                     [kg[0]] + rope_refs[-1:])
            c2 = vop(lambda V, sl=sl: V.tensor_copy(S["krotb"][HLF:128, 0:CH], S["dK"][:, 1 + sl.start:1 + sl.stop][0:HLF, :]), [c1])
            t1 = vop(lambda V, sl=sl: V.tensor_tensor(S["rch2"][:, 0:CH], S["dK"][:, 1 + sl.start:1 + sl.stop],
                                                      S["cosK"][:, sl], ALU.mult), [c2])
            hi = vop(lambda V, sl=sl: V.scalar_tensor_tensor(S["K_h"][:, sl], S["krotb"][:, 0:CH], 1.0,
                                                             S["sinK2"][:, sl], ALU.mult, ALU.mult), [t1]) if False else None
            t2 = vop(lambda V, sl=sl: V.tensor_tensor(S["krotb"][:, 0:CH], S["krotb"][:, 0:CH], S["sinK2"][:, sl], ALU.mult), [t1])
            hi = vop(lambda V, sl=sl: V.tensor_tensor(S["K_h"][:, sl], S["rch2"][:, 0:CH], S["krotb"][:, 0:CH], ALU.add),
                     [t2] + sc_prev)
            t3 = vop(lambda V, sl=sl: V.tensor_tensor(S["rch2"][:, 0:CH], S["rch2"][:, 0:CH], S["krotb"][:, 0:CH], ALU.add), [hi])
            lo = vop(lambda V, sl=sl: V.tensor_tensor(S["K_l"][:, sl], S["rch2"][:, 0:CH], S["K_h"][:, sl], ALU.subtract), [t3])
            rope_refs = [lo]
            sc_prev = []
        st["ropece"][h] = rope_refs
        st["kgate"][h] = [kg[0]]

    def emit_scores(h, cc, fillers):
        nkt = KTPC * (cc + 1)
        csl = slice(CH * cc, CH * (cc + 1))
        par = h % 2
        l_mms = []
        last_pv = None
        ex_prev = {}
        pend = {}
        first_pv_dep = st["obank"]

        def emit_pv(kt):
            nonlocal last_pv
            ksl, ehs, els, eh_ev, el_ev = pend.pop(kt)
            m1 = mm(PS[2][:, 0:CH], S["vtm_h"][:, ksl], ehs, start=(kt == 0), stop=False,
                    deps=[eh_ev, st["trs"][h][0]] + (first_pv_dep if kt == 0 else []))
            mm(PS[2][:, 0:CH], S["vtm_l"][:, ksl], ehs, start=False, stop=False, deps=[st["trs"][h][1]])
            last_pv = mm(PS[2][:, 0:CH], S["vtm_h"][:, ksl], els, start=False, stop=(kt == nkt - 1), deps=[el_ev])
            l1 = mm(PS[3][0:1, 0:CH], S["onescb"][:], ehs, start=(kt == 0), stop=False,
                    deps=[eh_ev] + (st["lbank"] if kt == 0 else []))
            l2 = mm(PS[3][0:1, 0:CH], S["onescb"][:], els, start=False, stop=(kt == nkt - 1), deps=[el_ev])
            l_mms.append(l2)
            st["elast"][kt % 2] = [el_ev, last_pv if kt == nkt - 1 else m1]

        for kt in range(nkt):
            if fillers:
                fillers.pop(0)()
            bank = PS[kt % 2]
            ksl = slice(128 * kt, 128 * (kt + 1))
            d0 = st["ropece"][h] + st["qld"][h] + ex_prev.get(kt - 2, []) + st["sbank"].pop(kt % 2, [])
            mm(bank[:, 0:CH], S["K_h"][:, ksl], S["qt_h"][:, csl], start=True, stop=False, deps=d0)
            mm(bank[:, 0:CH], S["K_h"][:, ksl], S["qt_l"][:, csl], start=False, stop=False)
            mm(bank[:, 0:CH], S["K_l"][:, ksl], S["qt_h"][:, csl], start=False, stop=False)
            sc_mm = mm(bank[:, 0:CH], S["onesrb"][:], S["bhrow"][:, par * TQ + csl.start:par * TQ + csl.stop],
                       start=False, stop=True, deps=st["bh"][h])
            ebuf = kt % 2
            efs = S["ef"][:, ebuf * CH:(ebuf + 1) * CH]
            ehs = S["eh"][:, ebuf * CH:(ebuf + 1) * CH]
            els = S["el"][:, ebuf * CH:(ebuf + 1) * CH]
            bias_ap = S["negcumT"][:, H * kt + h:H * kt + h + 1]
            eb = st["elast"].get(ebuf, [])
            if kt >= KTPC * cc:
                mk = vop(lambda V, kt=kt, bank=bank, efs=efs: V.scalar_tensor_tensor(
                    efs, S["masks"][:, CH * kt:CH * (kt + 1)], NEG, bank[:, 0:CH], ALU.mult, ALU.add),
                    [sc_mm, ld["masks"]] + ex_prev.get(kt - 2, []) + eb)
                ex = sop(lambda Sc, efs=efs, bias_ap=bias_ap: Sc.activation(
                    efs, efs, AF.Exp, bias=bias_ap, scale=1.0), [mk] + eb)
                ex_psum_reader = mk
            else:
                ex = sop(lambda Sc, efs=efs, bank=bank, bias_ap=bias_ap: Sc.activation(
                    efs, bank[:, 0:CH], AF.Exp, bias=bias_ap, scale=1.0),
                    [sc_mm] + ex_prev.get(kt - 2, []) + eb)
                ex_psum_reader = ex
            eh_ev = sop(lambda Sc, ehs=ehs, efs=efs: Sc.copy(ehs, efs), [ex])
            el_ev = vop(lambda V, els=els, efs=efs, ehs=ehs: V.tensor_tensor(els, efs, ehs, ALU.subtract), [eh_ev])
            ex_prev[kt] = [ex_psum_reader]
            pend[kt] = (ksl, ehs, els, eh_ev, el_ev)
            if kt >= 1:
                emit_pv(kt - 1)
        emit_pv(nkt - 1)
        st["sc_last"] = [last_pv, l_mms[-1]]
        rin = vop(lambda V: V.reciprocal(S["invlrow"][:], PS[3][0:1, 0:CH]), l_mms[-1:])
        ibm = mm(PS[3][:, 0:CH], S["onesrf"][:], S["invlrow"][:], start=True, stop=True, deps=[rin])
        ibc = vop(lambda V: V.tensor_copy(S["invlbc"][:], PS[3][:, 0:CH]), [ibm])
        st["lbank"] = [ibc]
        st["sbank"][nkt % 2] = [ex_prev[nkt - 1][0]] if False else st["sbank"].get(nkt % 2, [])
        ob = cc % 2
        ohs = S["o_h"][:, ob * CH:(ob + 1) * CH]
        ols = S["o_l"][:, ob * CH:(ob + 1) * CH]
        hi = vop(lambda V, ohs=ohs: V.tensor_tensor(ohs, PS[2][:, 0:CH], S["invlbc"][:], ALU.mult),
                 [ibc, last_pv] + o_spills[-4:-2])
        tt_ = vop(lambda V: V.tensor_tensor(S["rch2"][:, 0:CH], PS[2][:, 0:CH], S["invlbc"][:], ALU.mult), [hi])
        lo = vop(lambda V, ohs=ohs, ols=ols: V.tensor_tensor(ols, S["rch2"][:, 0:CH], ohs, ALU.subtract), [tt_])
        st["obank"] = [hi, tt_]
        o_spills.append(E.dma("scalar", dr["oh_sp"][h][:, csl], ohs, deps=[hi]))
        o_spills.append(E.dma("scalar", dr["ol_sp"][h][:, csl], ols, deps=[lo]))

    st["grow_ld"] = []
    NPRE = 9
    emit_growload(0, "v")
    emit_wq_loads(0)
    carry = make_proj_closures(0)
    for it in range(H):
        if it >= 1:
            emit_scores(it - 1, 0, carry)
            emit_scores(it - 1, 1, carry)
        while carry:
            carry.pop(0)()
        emit_vgate(it)
        emit_pkbc(it)
        if it + 1 < H:
            emit_wq_loads(it + 1)
        emit_transp(it)
        emit_kgate_rope(it)
        emit_qload(it)
        if it + 1 < H:
            emit_growload(it + 1, "v")
            carry = make_proj_closures(it + 1)
            for _ in range(min(NPRE, len(carry))):
                carry.pop(0)()
        else:
            carry = []
    emit_scores(H - 1, 0, carry)
    emit_scores(H - 1, 1, carry)
    A2_END = E.barrier()

    B0_END = A2_END

    bx = []
    for ct in range(CT):
        bx.append(E.dma("sync", S["bxnq_h"][:, ct * TQ:(ct + 1) * TQ], dr["xnqh_sp"][ct], deps=(B0_END if ct == 0 else [])))
        bx.append(E.dma("sync", S["bxnq_l"][:, ct * TQ:(ct + 1) * TQ], dr["xnql_sp"][ct], deps=[]))
    conv_prev = {}
    sil_prev = []
    su_prev = []
    suf_prev = []
    last_down_mm = []
    ga_all = {}
    dbank_last = {}
    wd_conv = {}
    for hc in range(NHC):
        tiles = list(range(hc * HIDCH, min(HT, (hc + 1) * HIDCH)))
        nj = len(tiles)
        su_refs = {}
        dprev = []
        for j, ht in enumerate(tiles):
            gslot = S["wbf"][:, (j % 2) * CT * 128:(j % 2 + 1) * CT * 128]
            uslot = S["wbf"][:, (2 + j % 2) * CT * 128:(3 + j % 2) * CT * 128]
            dg = E.dma("sync", S["wint"][:].rearrange("p (a d) -> p a d", d=128),
                       dr["wgT"][ht].rearrange("(a p) d -> p a d", p=128),
                       deps=conv_prev.get("w", []) + (bx if hc == 0 and j == 0 else []))
            cg = vop(lambda V, gslot=gslot: V.tensor_scalar_add(gslot, S["wint"][:], -128.0),
                     [dg] + conv_prev.get(("g", j % 2), []) + (B0_END if hc == 0 and j == 0 else []))
            du = E.dma("sync", S["wint"][:].rearrange("p (a d) -> p a d", d=128),
                       dr["wuT"][ht].rearrange("(a p) d -> p a d", p=128), deps=[cg])
            cu = vop(lambda V, uslot=uslot: V.tensor_scalar_add(uslot, S["wint"][:], -128.0),
                     [du] + conv_prev.get(("u", j % 2), []))
            dd = E.dma("sync", S["wint"][:], dr["wdT"][ht], deps=[cu])
            wds = S["wdc"][:, j * CT * 128:(j + 1) * CT * 128]
            cd = vop(lambda V, wds=wds: V.tensor_scalar_add(wds, S["wint"][:], -128.0),
                     [dd] + wd_conv.get((hc - 1, j), []))
            wd_conv[(hc, j)] = [cd]
            conv_prev["w"] = [cd]
            g_sil = {}
            g_last = {}
            for tc in range(NCH):
                bank = PS[4 + tc % 2]
                last = None
                for ct in range(CT):
                    wsl = slice(ct * 128, (ct + 1) * 128)
                    xsl = slice(ct * TQ + CH * tc, ct * TQ + CH * (tc + 1))
                    d0 = [cg, bx[2 * ct], bx[2 * ct + 1]] + sil_prev
                    mm(bank[:, 0:CH], gslot[:, wsl], S["bxnq_h"][:, xsl], start=(ct == 0), stop=False,
                       deps=d0 if ct == 0 else [cg])
                    last = mm(bank[:, 0:CH], gslot[:, wsl], S["bxnq_l"][:, xsl], start=False, stop=(ct == CT - 1))
                gs = sop(lambda Sc, tc=tc, bank=bank, ht=ht: Sc.activation(
                    S["gsil"][:, CH * (tc % 2):CH * (tc % 2 + 1)], bank[:, 0:CH], AF.Silu,
                    scale=S["sg"][:, ht:ht + 1], bias=0.0), [last, ld["sg"]] + su_prev)
                g_sil[tc] = gs
                g_last[tc] = last
            sil_prev = [g_sil[NCH - 1]]
            conv_prev[("g", j % 2)] = [g_last[NCH - 1]]
            u_last = None
            for tc in range(NCH):
                bank = PS[6 + tc % 2]
                last = None
                for ct in range(CT):
                    wsl = slice(ct * 128, (ct + 1) * 128)
                    xsl = slice(ct * TQ + CH * tc, ct * TQ + CH * (tc + 1))
                    mm(bank[:, 0:CH], uslot[:, wsl], S["bxnq_h"][:, xsl], start=(ct == 0), stop=False,
                       deps=[cu] + (su_prev if ct == 0 else []))
                    last = mm(bank[:, 0:CH], uslot[:, wsl], S["bxnq_l"][:, xsl], start=False, stop=(ct == CT - 1))
                suf = vop(lambda V, tc=tc, bank=bank, ht=ht: V.scalar_tensor_tensor(
                    S["su_f"][:], bank[:, 0:CH], S["sup"][:, ht:ht + 1],
                    S["gsil"][:, CH * (tc % 2):CH * (tc % 2 + 1)], ALU.mult, ALU.mult),
                    [last, g_sil[tc], ld["sup"]] + suf_prev)
                shs = S["su_h"][:, j * TQ + CH * tc:j * TQ + CH * (tc + 1)]
                sls = S["su_l"][:, j * TQ + CH * tc:j * TQ + CH * (tc + 1)]
                sh1 = sop(lambda Sc, shs=shs: Sc.copy(shs, S["su_f"][:]), [suf] + last_down_mm)
                sl1 = vop(lambda V, sls=sls, shs=shs: V.tensor_tensor(sls, S["su_f"][:], shs, ALU.subtract), [sh1])
                su_refs[(j, tc)] = [sh1, sl1]
                suf_prev = [sh1, sl1]
                u_last = last
            su_prev = su_refs[(j, NCH - 1)]
            conv_prev[("u", j % 2)] = [u_last]
        for ct in range(CT):
            for tc in range(NCH):
                bank = PS[tc % 2]
                last = None
                for j in range(nj):
                    wsl = slice(j * CT * 128 + ct * 128, j * CT * 128 + (ct + 1) * 128)
                    ssl = slice(j * TQ + CH * tc, j * TQ + CH * (tc + 1))
                    d0 = su_refs[(j, tc)] + [wd_conv[(hc, j)][0]] + ((ga_all.get((hc - 1, ct, tc), []) + dbank_last.get(tc % 2, [])) if j == 0 else [])
                    mm(bank[:, 0:CH], S["wdc"][:, wsl], S["su_h"][:, ssl], start=(j == 0), stop=False, deps=d0)
                    last = mm(bank[:, 0:CH], S["wdc"][:, wsl], S["su_l"][:, ssl], start=False, stop=(j == nj - 1))
                gsl = slice(ct * TQ + CH * tc, ct * TQ + CH * (tc + 1))
                if hc == 0:
                    ga = vop(lambda V, gsl=gsl, bank=bank: V.tensor_copy(S["g_acc"][:, gsl], bank[:, 0:CH]), [last])
                else:
                    ga = vop(lambda V, gsl=gsl, bank=bank: V.tensor_tensor(
                        S["g_acc"][:, gsl], S["g_acc"][:, gsl], bank[:, 0:CH], ALU.add),
                        [last] + ga_all.get((hc - 1, ct, tc), []))
                ga_all[(hc, ct, tc)] = [ga]
                dbank_last[tc % 2] = [ga]
                last_down_mm = [last]
    B1_END = E.barrier()
    ob_ld = []
    for hh in range(H):
        ob_ld.append(E.dma("scalar", S["ob_h"][:, hh * TQ:(hh + 1) * TQ], dr["oh_sp"][hh],
                           deps=(list(suf_prev) + [o_spills[-1]]) if hh == 0 else []))
        ob_ld.append(E.dma("scalar", S["ob_l"][:, hh * TQ:(hh + 1) * TQ], dr["ol_sp"][hh], deps=[]))

    st_done = []
    v1_hist = []
    v2_hist = []
    wo_rd = {}
    wsplit = {}
    def emit_wosplit(ct):
        buf = ct % 2
        wfs = S["wo_f"][:, buf * CT * 128:(buf + 1) * CT * 128]
        hcols = CT * 128 // 2
        d0 = (B1_END if ct < 2 else []) + wsplit.get(ct - 2, [])
        dwo_a = E.dma("sync", wfs[:, 0:hcols].rearrange("p (a d) -> p a d", d=128),
                      dr["woT4"][ct, 0:CT // 2].rearrange("a p d -> p a d"), deps=d0)
        dwo_b = E.dma("gpsimd", wfs[:, hcols:2 * hcols].rearrange("p (a d) -> p a d", d=128),
                      dr["woT4"][ct, CT // 2:CT].rearrange("a p d -> p a d"), deps=d0)
        whs = S["wo_h"][:, buf * CT * 128:(buf + 1) * CT * 128]
        wls = S["wo_l"][:, buf * CT * 128:(buf + 1) * CT * 128]
        whi = vop(lambda V, whs=whs, wfs=wfs: V.tensor_copy(whs, wfs),
                  [dwo_a, dwo_b] + wo_rd.get(ct - 2, []) + (B1_END if ct < 2 else []))
        wlo = vop(lambda V, whs=whs, wls=wls, wfs=wfs: V.tensor_tensor(wls, wfs, whs, ALU.subtract), [whi])
        wsplit[ct] = [wlo]
        wo_rd[ct] = [wlo]
        return whs, wls, whi, wlo

    wosp = {0: emit_wosplit(0)}
    for ct in range(CT):
        if ct + 1 < CT:
            wosp[ct + 1] = emit_wosplit(ct + 1)
        whs, wls, whi, wlo = wosp[ct]
        for cc in range(NCH):
            csl = slice(CH * cc, CH * (cc + 1))
            bank = PS[2 + (2 * ct + cc) % 2]
            last = None
            for cp in range(CT):
                wsl = slice(cp * 128, (cp + 1) * 128)
                osl = slice(cp * TQ + CH * cc, cp * TQ + CH * (cc + 1))
                d0 = [whi, ob_ld[2 * cp], ob_ld[2 * cp + 1]] + ((v1_hist[-2:-1] if len(v1_hist) >= 2 else []) if cp == 0 else [])
                mm(bank[:, 0:CH], whs[:, wsl], S["ob_h"][:, osl], start=(cp == 0), stop=False, deps=d0)
                mm(bank[:, 0:CH], wls[:, wsl], S["ob_h"][:, osl], start=False, stop=False, deps=[wlo])
                last = mm(bank[:, 0:CH], whs[:, wsl], S["ob_l"][:, osl], start=False, stop=(cp == CT - 1))
            wo_rd[ct] = wo_rd[ct] + [last]
            gb = sop(lambda Sc, ct=ct, cc=cc, b2f=(2 * ct + cc) % 2: Sc.activation(
                S["gsil2"][:, b2f * CH:(b2f + 1) * CH],
                S["g_acc"][:, ct * TQ + CH * cc:ct * TQ + CH * (cc + 1)],
                AF.Silu, scale=S["sdn"][:, ct:ct + 1], bias=0.0),
                [ld["sdn"]] + (B1_END if ct == 0 and cc == 0 else []) + (v1_hist[-2:-1] if len(v1_hist) >= 2 else []))
            b2f = (2 * ct + cc) % 2
            dx = E.dma("scalar", S["xres"][:, b2f * CH:(b2f + 1) * CH],
                       dr["xqresT"][128 * ct:128 * (ct + 1), csl],
                       deps=(v2_hist[-2:-1] if len(v2_hist) >= 2 else []))
            v1 = vop(lambda V, b2f=b2f, bank=bank: V.tensor_tensor(
                S["outsb"][:, b2f * CH:(b2f + 1) * CH], bank[:, 0:CH],
                S["gsil2"][:, b2f * CH:(b2f + 1) * CH], ALU.mult),
                [last, gb] + (st_done[-2:-1] if len(st_done) >= 2 else []))
            v2 = vop(lambda V, b2f=b2f: V.tensor_tensor(
                S["outsb"][:, b2f * CH:(b2f + 1) * CH], S["outsb"][:, b2f * CH:(b2f + 1) * CH],
                S["xres"][:, b2f * CH:(b2f + 1) * CH], ALU.add), [v1, dx])
            st_ = E.dma("sync", outT[128 * ct:128 * (ct + 1), csl], S["outsb"][:, b2f * CH:(b2f + 1) * CH], deps=[v2])
            st_done.append(st_)
            v1_hist.append(v1)
            v2_hist.append(v2)
    tail = E.barrier() + st_done[-1:]
    E.ev("sync", lambda Eng: Eng.nop(), tail)

    E.finalize()

    import contextlib
    with contextlib.ExitStack() as ctx:
        sems = {}
        rings = {}
        for e in ENGS:
            sems[e] = ctx.enter_context(nc.semaphore(f"sm_{e}"))
            if E.ndma[e]:
                rings[e] = [ctx.enter_context(nc.semaphore(f"dq_{e}_{k}"))
                            for k in range(min(NRING, E.ndma[e]))]
        block = ctx.enter_context(nc.Block())

        @block.tensor
        def _(Eng):
            E.replay("tensor", Eng, sems, rings)

        @block.vector
        def _(Eng):
            E.replay("vector", Eng, sems, rings)

        @block.scalar
        def _(Eng):
            E.replay("scalar", Eng, sems, rings)

        @block.gpsimd
        def _(Eng):
            E.replay("gpsimd", Eng, sems, rings)

        @block.sync
        def _(Eng):
            E.replay("sync", Eng, sems, rings)

    return nc



def _prep_shared(cfg, W):
    C = cfg["C"]; T = cfg["T"]; H = cfg["H"]; HID = cfg["HID"]
    CT = C // 128
    HT = HID // 128
    f32 = np.float32
    sh = {}
    sh["wqT"] = np.ascontiguousarray(W["Wq"].T.reshape(C, H, 128).transpose(1, 0, 2)).astype(f32)
    sh["wkT"] = np.ascontiguousarray(W["Wk"].T.reshape(C, H, 128).transpose(1, 0, 2)).astype(f32)
    sh["wvT"] = np.ascontiguousarray(W["Wv"].T.reshape(C, H, 128).transpose(1, 0, 2)).astype(f32)
    sh["wsqT"] = np.ascontiguousarray(W["Wsq"].T.reshape(CT, 128, H)).astype(f32)
    wsp = np.concatenate([W["Wf"].T, W["Wsk"].T, W["Wsv"].T], axis=1)
    sh["wspT"] = np.ascontiguousarray(wsp.reshape(CT, 128, 3 * H)).astype(f32)
    bfp = np.zeros((3 * H, 1), f32)
    bfp[0:H, 0] = W["bf"]
    sh["bfpack"] = bfp
    WoT = W["Wo"].T
    sh["woT4"] = np.ascontiguousarray(WoT.reshape(CT, 128, CT, 128).transpose(2, 0, 1, 3)).astype(f32)
    sh["wgT"] = np.ascontiguousarray(W["wq_gate"].T.reshape(C, HT, 128).transpose(1, 0, 2)).astype(np.int32)
    sh["wuT"] = np.ascontiguousarray(W["wq_up"].T.reshape(C, HT, 128).transpose(1, 0, 2)).astype(np.int32)
    sh["wdT"] = np.ascontiguousarray(W["wq_down"].T.reshape(HT, 128, C)).astype(np.int32)
    sh["sg"] = np.ascontiguousarray(W["s_gate"].reshape(HT, 128).T).astype(f32)
    sh["sup"] = np.ascontiguousarray(W["s_up"].reshape(HT, 128).T).astype(f32)
    sh["sdn"] = np.ascontiguousarray(W["s_down"].reshape(CT, 128).T).astype(f32)
    cos, sin = _ref_rope_tables(cfg)
    cos = cos.T
    sin = sin.T
    sh["cosK"] = np.concatenate([cos, cos], axis=0).astype(f32)
    sh["sinK2"] = np.concatenate([sin, -sin], axis=0).astype(f32)
    sh["ident"] = np.eye(128, dtype=f32)
    sh["onescf"] = np.ones((128, 1), f32)
    sh["onesrf"] = np.ones((1, 128), f32)
    sh["onescb"] = np.ones((128, 1), BF16NP)
    sh["onesrb"] = np.ones((1, 128), BF16NP)
    return sh


def _prep_parity(cfg, h):
    T = cfg["T"]; CH = cfg["CHUNK"]
    TK = T // 128
    NB = TK // 2
    TQ = NB * 128
    D = 128
    i = np.arange(TQ)
    tok = 128 * (2 * (i // 128) + h) + (i % 128)
    cosT, sinT = _ref_rope_tables(cfg)
    cos = cosT[tok].T
    sin = sinT[tok].T
    out = {}
    out["cosQ"] = np.concatenate([cos, cos], axis=0).astype(np.float32)
    out["sinQ2"] = np.concatenate([sin, -sin], axis=0).astype(np.float32)
    sel = np.zeros((T, TQ), np.float32)
    sel[tok, np.arange(TQ)] = 1.0
    out["selq"] = np.ascontiguousarray(sel.reshape(TK, 128, TQ)).astype(BF16NP)
    masks = np.zeros((TK, 128, CH), np.uint8)
    for kt in range(TK):
        cc = kt // (2 * (CH // 128))
        tl = np.arange(CH)
        ql = CH * cc + tl
        g = 128 * (2 * (ql // 128) + h) + (ql % 128)
        srow = 128 * kt + np.arange(128)
        masks[kt] = (srow[:, None] > g[None, :]).astype(np.uint8)
    out["masks"] = masks
    return out


def _host_prep(cfg, inputs):
    C = cfg["C"]; T = cfg["T"]
    TK = T // 128
    NB = TK // 2
    TQ = NB * 128
    x = np.asarray(inputs["L0_attn_x"], np.float32)
    W = dict(
        Wq=np.asarray(inputs["L0_attn_Wq"], np.float32),
        Wsq=np.asarray(inputs["L0_attn_Wsq"], np.float32),
        Wk=np.asarray(inputs["L0_attn_Wk"], np.float32),
        Wsk=np.asarray(inputs["L0_attn_Wsk"], np.float32),
        Wv=np.asarray(inputs["L0_attn_Wv"], np.float32),
        Wsv=np.asarray(inputs["L0_attn_Wsv"], np.float32),
        Wf=np.asarray(inputs["L0_attn_Wf"], np.float32),
        bf=np.asarray(inputs["L0_attn_bf"], np.float32),
        Wo=np.asarray(inputs["L0_attn_Wo"], np.float32),
        wq_gate=np.asarray(inputs["L0_ffn_wq_gate"], np.int32),
        s_gate=np.asarray(inputs["L0_ffn_s_gate"], np.float32),
        wq_up=np.asarray(inputs["L0_ffn_wq_up"], np.int32),
        s_up=np.asarray(inputs["L0_ffn_s_up"], np.float32),
        wq_down=np.asarray(inputs["L0_ffn_wq_down"], np.int32),
        s_down=np.asarray(inputs["L0_ffn_s_down"], np.float32),
    )
    sh = _prep_shared(cfg, W)
    par = [_prep_parity(cfg, 0), _prep_parity(cfg, 1)]
    ncores = 2 * x.shape[0]
    in_maps = []
    own_tok = []
    for c in range(ncores):
        b, h = c // 2, c % 2
        xkT = np.ascontiguousarray(x[b].T)
        i = np.arange(TQ)
        tok = 128 * (2 * (i // 128) + h) + (i % 128)
        own_tok.append(tok)
        cols = np.empty((NB, 129), np.int64)
        for j in range(NB):
            g0 = 128 * (2 * j + h)
            cols[j] = np.arange(g0 - 1, g0 + 128)
        colsf = cols.reshape(-1)
        take = np.clip(colsf, 0, T - 1)
        xqT = xkT[:, take].copy()
        zm = colsf < 0
        if zm.any():
            xqT[:, zm] = 0.0
        m = dict(sh)
        m.update(par[h])
        m["xkT"] = xkT
        m["xqT"] = np.ascontiguousarray(xqT)
        m["xqresT"] = np.ascontiguousarray(xkT[:, tok])
        in_maps.append(m)
    return in_maps, own_tok, x.shape


def kernel(**inputs):
    cfg = full_cfg()
    in_maps, own_tok, xshape = _host_prep(cfg, inputs)
    nc = build_nc(cfg)
    res = run_bass_kernel_spmd(nc, in_maps, core_ids=list(range(len(in_maps))))
    B, T, C = xshape
    out = np.empty((B, T, C), np.float32)
    for c in range(len(in_maps)):
        b = c // 2
        outT = np.asarray(res.results[c]["outT"])
        out[b, own_tok[c], :] = outT.T
    return out


# version 21
# speedup vs baseline: 1.0335x; 1.0335x over previous
import numpy as np
import ml_dtypes
import concourse.bass as bass
import concourse.mybir as mybir
from concourse.bass_utils import run_bass_kernel_spmd

F32 = mybir.dt.float32
BF16 = mybir.dt.bfloat16
I32 = mybir.dt.int32
U8 = mybir.dt.uint8
AF = mybir.ActivationFunctionType
ALU = mybir.AluOpType
INVSQD = 1.0 / float(np.float32(np.sqrt(np.float64(128.0))))
BF16NP = ml_dtypes.bfloat16

ENGS = ("tensor", "vector", "scalar", "gpsimd", "sync")
NEG = -1.0e30



def _ref_rope_tables(cfg):
    T = cfg["T"]
    D = 128
    e = np.arange(0, D, 2).astype(np.float32) / np.float32(D)
    pw = np.power(np.float64(cfg["BASE"]), e.astype(np.float64)).astype(np.float32)
    inv = (np.float32(1.0) / pw).astype(np.float32)
    fr = (np.arange(T).astype(np.float32)[:, None] * inv[None, :]).astype(np.float32)
    try:
        import ctypes, ctypes.util
        libm = ctypes.CDLL(ctypes.util.find_library("m") or "libm.so.6")
        libm.cosf.restype = ctypes.c_float
        libm.cosf.argtypes = [ctypes.c_float]
        libm.sinf.restype = ctypes.c_float
        libm.sinf.argtypes = [ctypes.c_float]
        fl = fr.ravel()
        cos = np.fromiter((libm.cosf(ctypes.c_float(float(v))) for v in fl),
                          np.float32, fl.size).reshape(fr.shape)
        sin = np.fromiter((libm.sinf(ctypes.c_float(float(v))) for v in fl),
                          np.float32, fl.size).reshape(fr.shape)
        return cos, sin
    except Exception:
        return (np.cos(fr.astype(np.float64)).astype(np.float32),
                np.sin(fr.astype(np.float64)).astype(np.float32))


def full_cfg():
    return dict(C=2048, T=2048, H=16, HID=5632, CHUNK=512, MQ=30.0,
                HIDCH=4, EPS=1e-5, BASE=10000.0)


class Ref:
    __slots__ = ("eng", "idx")

    def __init__(self, eng, idx):
        self.eng = eng
        self.idx = idx


NRING = 12


class Em:

    def __init__(self):
        self.events = {e: [] for e in ENGS}
        self.ndma = {e: 0 for e in ENGS}
        self.dma_idx = {e: [] for e in ENGS}

    def ev(self, eng, fn, deps=(), dma=False):
        idx = len(self.events[eng])
        d = [r for r in deps if r is not None]
        slot = None
        if dma:
            slot = self.ndma[eng]
            self.ndma[eng] += 1
            self.dma_idx[eng].append(idx)
        self.events[eng].append((d, fn, dma, slot))
        return Ref(eng, idx)

    def dma(self, eng, out, in_, deps=()):
        def fn(E, out=out, in_=in_):
            return E.dma_start(out=out, in_=in_)
        return self.ev(eng, fn, deps, dma=True)

    def barrier(self):
        refs = [Ref(e, len(self.events[e]) - 1) for e in ENGS if self.events[e]]
        for e in ENGS:
            for i in self.dma_idx[e][-NRING:]:
                refs.append(Ref(e, i))
        return refs

    def finalize(self):
        marked = {e: set() for e in ENGS}
        for eng in ENGS:
            for deps, _, _, _ in self.events[eng]:
                for r in deps:
                    marked[r.eng].add(r.idx)
        for eng in ENGS:
            for i, (_, _, dma, _) in enumerate(self.events[eng]):
                if dma:
                    marked[eng].add(i)
        rank = {e: {} for e in ENGS}
        for eng in ENGS:
            v = 0
            for i in sorted(marked[eng]):
                if self.events[eng][i][2]:
                    continue
                v += 1
                rank[eng][i] = v
        self.rank = rank
        self.marked = marked

    def _dep_key(self, r):
        deps, fn, dma, slot = self.events[r.eng][r.idx]
        if dma:
            return (r.eng, slot % NRING), 16 * (slot // NRING + 1)
        return (r.eng, None), self.rank[r.eng][r.idx]

    def replay(self, eng, E, sems, rings):
        lastw = {}
        for i, (deps, fn, dma, slot) in enumerate(self.events[eng]):
            for r in deps:
                key, v = self._dep_key(r)
                assert v is not None, f"dep on unmarked {r.eng}:{r.idx}"
                if lastw.get(key, 0) >= v:
                    continue
                lastw[key] = v
                sem = sems[key[0]] if key[1] is None else rings[key[0]][key[1]]
                E.wait_ge(sem, v)
            if dma:
                ring_sem = rings[eng][slot % NRING]
                rnd = slot // NRING
                key = (eng, slot % NRING)
                if rnd > 0 and lastw.get(key, 0) < 16 * rnd:
                    lastw[key] = 16 * rnd
                    E.wait_ge(ring_sem, 16 * rnd)
                inst = fn(E)
                inst.then_inc(ring_sem, 16)
            else:
                inst = fn(E)
                if i in self.marked[eng]:
                    inst.then_inc(sems[eng], 1)


def align64(n):
    return ((n + 63) // 64) * 64


def build_nc(cfg):
    C = cfg["C"]; T = cfg["T"]; H = cfg["H"]; HID = cfg["HID"]
    CH = cfg["CHUNK"]; MQ = cfg["MQ"]; HIDCH = cfg["HIDCH"]; EPS = cfg["EPS"]
    CT = C // 128
    TK = T // 128
    NB = TK // 2
    TQ = NB * 128
    QB = NB * 129
    HT = HID // 128
    QTPC = CH // 128
    NCH = TQ // CH
    KTPC = 2 * QTPC
    NTC = T // CH
    NQC = 4 if QB % 4 == 0 else 2
    QCH = QB // NQC
    NHC = (HT + HIDCH - 1) // HIDCH
    HLF = 64
    assert C % 128 == 0 and T % 128 == 0 and TQ % CH == 0 and QB % NQC == 0
    assert NTC <= 4
    assert (T.bit_length() - 1) % 2 == 1, "need odd Hillis-Steele step count"

    nc = bass.Bass("TRN2", target_bir_lowering=False)

    dr = {}

    def din(name, shape, dt):
        dr[name] = nc.dram_tensor(name, list(shape), dt, kind="ExternalInput").ap()

    def dint(name, shape, dt):
        dr[name] = nc.dram_tensor(name, list(shape), dt, kind="Internal").ap()

    din("xkT", (C, T), F32)
    din("xqT", (C, QB), F32)
    din("xqresT", (C, TQ), F32)
    din("wqT", (H, C, 128), F32)
    din("wkT", (H, C, 128), F32)
    din("wvT", (H, C, 128), F32)
    din("wsqT", (CT, 128, H), F32)
    din("wspT", (CT, 128, 3 * H), F32)
    din("bfpack", (3 * H, 1), F32)
    din("woT4", (CT, CT, 128, 128), F32)
    din("wgT", (HT, C, 128), I32)
    din("wuT", (HT, C, 128), I32)
    din("wdT", (HT, 128, C), I32)
    din("sg", (128, HT), F32)
    din("sup", (128, HT), F32)
    din("sdn", (128, CT), F32)
    din("cosK", (128, T), F32)
    din("sinK2", (128, T), F32)
    din("cosQ", (128, TQ), F32)
    din("sinQ2", (128, TQ), F32)
    din("selq", (TK, 128, TQ), BF16)
    din("masks", (TK, 128, CH), U8)
    din("ident", (128, 128), F32)
    din("onescf", (128, 1), F32)
    din("onesrf", (1, 128), F32)
    din("onescb", (128, 1), BF16)
    din("onesrb", (1, 128), BF16)

    outT = nc.dram_tensor("outT", [C, TQ], F32, kind="ExternalOutput").ap()

    dint("xl_sp", (CT, 128, T), BF16)
    dint("xnqh_sp", (CT, 128, TQ), BF16)
    dint("xnql_sp", (CT, 128, TQ), BF16)
    dint("qh_sp", (H, 128, TQ), BF16)
    dint("ql_sp", (H, 128, TQ), BF16)
    dint("oh_sp", (H, 128, TQ), BF16)
    dint("ol_sp", (H, 128, TQ), BF16)

    S = {}
    OFF = {}
    _probe = nc.alloc_sbuf_tensor("_base_probe", [128, 16], mybir.dt.uint8)
    BASE0 = nc.lookup_mloc(_probe).addr + 64
    LIMIT = nc.SBUF_PARTITION_SIZE_BYTES
    cur = [BASE0]

    def sb(name, shape, dt, at=None):
        nbytes = int(np.prod(shape[1:])) * mybir.dt.size(dt)
        if at is None:
            at = cur[0]
            cur[0] = at + align64(nbytes)
        S[name] = nc.alloc_sbuf_tensor_at(name, list(shape), dt, offset=at)
        OFF[name] = at
        return at + align64(nbytes)

    sb("ident", (128, 128), F32)
    sb("onescf", (128, 1), F32)
    sb("onesrf", (1, 128), F32)
    sb("onescb", (128, 1), BF16)
    sb("onesrb", (1, 128), BF16)
    sb("bfpack", (3 * H, 1), F32)
    sb("sdn", (128, CT), F32)
    sb("sg", (128, HT), F32)
    sb("sup", (128, HT), F32)
    B_BASE = cur[0]

    sb("negcumT", (128, H * TK), F32)
    sb("cumbf", (128, H * TK), BF16)
    sb("biasQ", (H, TQ), BF16)
    sb("wsp_f", (128, CT * 3 * H), F32)
    sb("wsp_h", (128, CT * 3 * H), BF16)
    sb("wsp_l", (128, CT * 3 * H), BF16)
    sb("gates", (3 * H, T), F32)
    sb("cosK", (128, T), F32)
    sb("sinK2", (128, T), F32)
    sb("masks", (128, TK * CH), U8)
    sb("rowbuf", (1, max(T, 4 * QCH, 4 * CH)), F32)
    sb("bhrow", (1, 2 * TQ), BF16)
    R1 = cur[0]
    sb("cosQ", (128, TQ), F32)
    sb("sinQ2", (128, TQ), F32)
    R1_END = cur[0]
    e_off = sb("ef", (128, 2 * CH), F32, at=R1)
    e_off = sb("eh", (128, 2 * CH), BF16, at=e_off)
    e_off = sb("el", (128, 2 * CH), BF16, at=e_off)
    assert e_off <= R1_END
    A2_BASE = cur[0]
    sb("xnh", (128, CT * T), BF16)
    sb("xlp", (128, 2 * T), BF16)
    sb("stage", (128, T), F32)
    sb("wv_h", (128, CT * 128), BF16)
    sb("wv_l", (128, CT * 128), BF16)
    sb("wk_h", (128, CT * 128), BF16)
    sb("wk_l", (128, CT * 128), BF16)
    sb("vtm_h", (128, TK * 128), BF16)
    sb("vtm_l", (128, TK * 128), BF16)
    sb("y_k", (128, T + 4), F32)
    sb("dK", (128, T + 4), F32)
    sb("K_h", (128, T), BF16)
    sb("K_l", (128, T), BF16)
    sb("qt_h", (128, TQ), BF16)
    sb("qt_l", (128, TQ), BF16)
    sb("o_h", (128, 2 * CH), BF16)
    sb("o_l", (128, 2 * CH), BF16)
    sb("invlbc", (128, CH), F32)
    sb("krotb", (128, CH), F32)
    sb("rch2", (128, CH), F32)
    sb("invlrow", (1, CH), F32)
    A_END = cur[0]
    nc._dbg_a_end = (A_END, LIMIT)
    assert A_END <= LIMIT, f"A-phase SBUF overflow: {A_END}"

    sb("cumA", (H, T), F32, at=OFF["vtm_h"])
    sb("cumB", (H, T), F32, at=OFF["K_h"])
    sb("selp", (128, 2 * TQ), BF16, at=OFF["o_h"])
    sb("rbc", (128, T), F32, at=OFF["vtm_h"])
    sb("mrbc", (128, T), F32, at=OFF["y_k"])
    sb("x2b", (128, T), F32, at=OFF["dK"])
    sb("lotile", (128, 2 * T), BF16, at=OFF["K_h"])
    sb("stage2", (128, T), F32, at=R1)

    p = OFF["xnh"]
    p = sb("xq_st", (128, 2 * QB), F32, at=p)
    p = sb("y_q", (128, QB), F32, at=p)
    p = sb("xnq_h", (128, CT * QB), BF16, at=p)
    p = sb("xnq_l", (128, CT * QB), BF16, at=p)
    p = sb("dQ", (128, TQ), F32, at=p)
    p = sb("qrope", (128, TQ), F32, at=p)
    p = sb("rbcq", (128, QB), F32, at=p)
    p = sb("mrbcq", (128, QB), F32, at=p)
    p = sb("wq_st", (128, CT * 128), F32, at=p)
    p = sb("wq_h", (128, CT * 128), BF16, at=p)
    p = sb("wq_l", (128, CT * 128), BF16, at=p)
    p = sb("wsq_f", (128, CT * H), F32, at=p)
    p = sb("wsq_h", (128, CT * H), BF16, at=p)
    p = sb("wsq_l", (128, CT * H), BF16, at=p)
    p = sb("pq", (H, QB), F32, at=p)
    p = sb("pqrow", (1, TQ), F32, at=p)
    p = sb("rch", (128, CH), F32, at=p)
    assert p <= LIMIT, f"A1 SBUF overflow: {p}"

    p = B_BASE
    p = sb("bxnq_h", (128, CT * TQ), BF16, at=p)
    p = sb("bxnq_l", (128, CT * TQ), BF16, at=p)
    p = sb("g_acc", (128, CT * TQ), F32, at=p)
    B_AFTER_G = p
    p = sb("su_h", (128, HIDCH * TQ), BF16, at=p)
    p = sb("su_l", (128, HIDCH * TQ), BF16, at=p)
    p = sb("wint", (128, CT * 128), I32, at=p)
    p = sb("wbf", (128, 4 * CT * 128), BF16, at=p)
    p = sb("wdc", (128, HIDCH * CT * 128), BF16, at=p)
    p = sb("su_f", (128, CH), F32, at=p)
    p = sb("gsil", (128, 2 * CH), F32, at=p)
    assert p <= LIMIT, f"B1 SBUF overflow: {p}"
    p = sb("ob_h", (128, H * TQ), BF16, at=B_BASE)
    p = sb("ob_l", (128, H * TQ), BF16, at=p)
    assert p <= B_AFTER_G - (B_AFTER_G - B_BASE) + (B_AFTER_G - B_BASE), "ob overlay"
    p = B_AFTER_G
    p = sb("wo_h", (128, 2 * CT * 128), BF16, at=p)
    p = sb("wo_l", (128, 2 * CT * 128), BF16, at=p)
    p = sb("wo_f", (128, 2 * CT * 128), F32, at=p)
    p = sb("xres", (128, 2 * CH), F32, at=p)
    p = sb("outsb", (128, 2 * CH), F32, at=p)
    p = sb("gsil2", (128, 2 * CH), F32, at=p)
    assert p <= LIMIT, f"B2 SBUF overflow: {p}"

    PS = []
    for i in range(8):
        PS.append(nc.alloc_psum_tensor(f"bank{i}", [128, max(CH, 512)], F32))

    E = Em()

    def mm(out, lhsT, rhs, start, stop, deps=()):
        return E.ev("tensor", (lambda Eng, out=out, lhsT=lhsT, rhs=rhs,
                               start=start, stop=stop:
                               Eng.matmul(out, lhsT, rhs, start=start, stop=stop)), deps)

    def tr(out, in_, ident, deps=()):
        return E.ev("tensor", (lambda Eng, out=out, in_=in_, ident=ident:
                               Eng.transpose(out, in_, ident)), deps)

    def vop(fn, deps=()):
        return E.ev("vector", fn, deps)

    def sop(fn, deps=()):
        return E.ev("scalar", fn, deps)

    def gop(fn, deps=()):
        return E.ev("gpsimd", fn, deps)

    ld = {}
    for nm in ("ident", "onescf", "onesrf", "onescb", "onesrb", "bfpack",
               "sdn", "sg", "sup", "cosK", "sinK2", "cosQ", "sinQ2"):
        ld[nm] = E.dma("sync", S[nm][:], dr[nm])
    ld["masks"] = E.dma("sync", S["masks"][:].rearrange("p (a d) -> p a d", d=CH),
                        dr["masks"].rearrange("a p d -> p a d"))
    ld["wsp"] = E.dma("sync", S["wsp_f"][:].rearrange("p (a d) -> p a d", d=3 * H),
                      dr["wspT"].rearrange("a p d -> p a d"))
    ld["wsq"] = E.dma("sync", S["wsq_f"][:].rearrange("p (a d) -> p a d", d=H),
                      dr["wsqT"].rearrange("a p d -> p a d"))
    wsp_h = vop(lambda V: V.tensor_copy(S["wsp_h"][:], S["wsp_f"][:]), [ld["wsp"]])
    wsp_l = vop(lambda V: V.tensor_tensor(S["wsp_l"][:], S["wsp_f"][:], S["wsp_h"][:], ALU.subtract), [wsp_h])
    wsq_h = vop(lambda V: V.tensor_copy(S["wsq_h"][:], S["wsq_f"][:]), [ld["wsq"]])
    wsq_l = vop(lambda V: V.tensor_tensor(S["wsq_l"][:], S["wsq_f"][:], S["wsq_h"][:], ALU.subtract), [wsq_h])

    def ln_stream(src_dr, ncols, nchunk, csz, stages, x2a, t2as, rbca, mrbca,
                  rows, hi_dst, lo_sink, deps0):
        rm = rows
        L = len(stages)
        stat_last = []
        x2_prev = []
        slotdep = {}
        for ct in range(CT):
            stage = stages[ct % L]
            d = E.dma("sync", stage, src_dr[128 * ct:128 * (ct + 1), :],
                      deps=(list(deps0) if ct < L else []) + slotdep.get(ct - L, []))
            x2 = vop(lambda V, stage=stage: V.tensor_tensor(x2a, stage, stage, ALU.mult),
                     [d] + stat_last[-1:] + x2_prev)
            x2_prev = [x2]
            ms = []
            for q in range(nchunk):
                sl = slice(csz * q, csz * (q + 1))
                ms.append(mm(PS[q][0:1, 0:csz], S["onescf"][:], stage[:, sl],
                             start=(ct == 0), stop=(ct == CT - 1), deps=[d]))
                ms.append(mm(PS[4 + q][0:1, 0:csz], S["onescf"][:], x2a[:, sl],
                             start=(ct == 0), stop=(ct == CT - 1), deps=[x2]))
            stat_last = ms
            slotdep[ct] = [ms[-2], ms[-1], x2]
        bcd = []
        prev_mm = []
        for q in range(nchunk):
            sl = slice(csz * q, csz * (q + 1))
            s0 = rm[:, 0:csz]
            s1 = rm[:, csz:2 * csz]
            s2 = rm[:, 2 * csz:3 * csz]
            s3 = rm[:, 3 * csz:4 * csz]
            e1 = vop(lambda V, q=q, s0=s0: V.tensor_scalar_mul(s0, PS[q][0:1, 0:csz], 1.0 / C), stat_last + prev_mm)
            e2 = vop(lambda V, q=q, s1=s1: V.tensor_scalar_mul(s1, PS[4 + q][0:1, 0:csz], 1.0 / C), stat_last + prev_mm)
            a = vop(lambda V, s0=s0, s3=s3: V.tensor_tensor(s3, s0, s0, ALU.mult), [e1])
            a = vop(lambda V, s1=s1, s3=s3: V.tensor_tensor(s1, s1, s3, ALU.subtract), [e2, a])
            a = vop(lambda V, s1=s1, s3=s3: V.tensor_scalar_add(s3, s1, float(EPS)), [a])
            a = sop(lambda Sc, s3=s3: Sc.activation(s3, s3, AF.Sqrt), [a])
            a = vop(lambda V, s1=s1, s3=s3: V.reciprocal(s1, s3), [a])
            b = vop(lambda V, s0=s0, s1=s1, s2=s2: V.tensor_tensor(s2, s0, s1, ALU.mult), [a])
            m1 = mm(PS[q][:, 0:csz], S["onesrf"][:], s1, start=True, stop=True, deps=[a, e1] + bcd)
            v1 = vop(lambda V, q=q, sl=sl: V.tensor_copy(rbca[:, sl], PS[q][:, 0:csz]), [m1])
            m2 = mm(PS[4 + q][:, 0:csz], S["onesrf"][:], s2, start=True, stop=True, deps=[b] + bcd)
            v2 = vop(lambda V, q=q, sl=sl: V.tensor_copy(mrbca[:, sl], PS[4 + q][:, 0:csz]), [m2])
            bcd += [v1, v2]
            prev_mm = [m1, m2]
        bcd = bcd[-2 * nchunk:]
        tiles = []
        prevv = []
        slot2 = {}
        for ct in range(CT):
            stage = stages[ct % L]
            t2a = t2as[ct % L]
            d = E.dma("sync", stage, src_dr[128 * ct:128 * (ct + 1), :],
                      deps=(bcd if ct < L else []) + slot2.get(ct - L, []))
            t1 = vop(lambda V, stage=stage: V.tensor_tensor(x2a, stage, rbca, ALU.mult), [d] + prevv)
            hd = hi_dst(ct)
            hi = vop(lambda V, hd=hd: V.tensor_tensor(hd, x2a, mrbca, ALU.subtract), [t1])
            t2 = vop(lambda V, t2a=t2a: V.tensor_tensor(t2a, x2a, mrbca, ALU.subtract), [t1])
            refs = lo_sink(ct, hi, t2, t2a)
            prevv = [t2, hi] + refs
            slot2[ct] = [t1, t2] + refs
            tiles.append((hi, refs))
        return tiles

    a1_rows = S["rowbuf"][0:1, 0:4 * QCH]
    stage1 = S["xq_st"][:, 0:QB]
    t2a1 = S["xq_st"][:, QB:2 * QB]

    def a1_hi(ct):
        return S["xnq_h"][:, ct * QB:(ct + 1) * QB]

    def a1_lo(ct, hi, t2, t2buf):
        lo = vop(lambda V, ct=ct, t2buf=t2buf: V.tensor_tensor(
            S["xnq_l"][:, ct * QB:(ct + 1) * QB], t2buf,
            S["xnq_h"][:, ct * QB:(ct + 1) * QB], ALU.subtract), [t2, hi])
        return [lo]

    a1_tiles = ln_stream(dr["xqT"], QB, NQC, QCH, [stage1], S["y_q"][:, 0:QB],
                         [t2a1], S["rbcq"][:], S["mrbcq"][:], a1_rows, a1_hi, a1_lo, [])

    pq_ev = None
    for q in range(NQC):
        last = None
        for ct in range(CT):
            wsl = slice(ct * H, (ct + 1) * H)
            xsl = slice(ct * QB + QCH * q, ct * QB + QCH * (q + 1))
            d0 = [wsq_h, a1_tiles[ct][0]] + ([pq_ev] if pq_ev else [])
            mm(PS[7][0:H, 0:QCH], S["wsq_h"][:, wsl], S["xnq_h"][:, xsl],
               start=(ct == 0), stop=False, deps=d0)
            mm(PS[7][0:H, 0:QCH], S["wsq_h"][:, wsl], S["xnq_l"][:, xsl],
               start=False, stop=False, deps=[a1_tiles[ct][1][0]])
            last = mm(PS[7][0:H, 0:QCH], S["wsq_l"][:, wsl], S["xnq_h"][:, xsl],
                      start=False, stop=(ct == CT - 1), deps=[wsq_l])
        pq_ev = sop(lambda Sc, q=q: Sc.activation(S["pq"][:, QCH * q:QCH * (q + 1)],
                                                  PS[7][0:H, 0:QCH], AF.Sigmoid), [last])

    QV = S["y_q"][:].rearrange("p (b c) -> p b c", c=129)
    DQV = S["dQ"][:].rearrange("p (b c) -> p b c", c=128)
    q_spills = []
    wq_free = []
    vd_prev = None
    vrow_d = {}
    km_prev = {}
    rope_prev = []
    evy_hist = {}
    for h in range(H):
        dW = E.dma("sync", S["wq_st"][:].rearrange("p (a d) -> p a d", d=128),
                   dr["wqT"][h].rearrange("(a p) d -> p a d", p=128), deps=wq_free)
        wh = vop(lambda V: V.tensor_copy(S["wq_h"][:], S["wq_st"][:]), [dW])
        wl = vop(lambda V: V.tensor_tensor(S["wq_l"][:], S["wq_st"][:], S["wq_h"][:], ALU.subtract), [wh])
        wq_free = [wl]
        PQV = S["pq"][:].rearrange("p (b c) -> p b c", c=129)
        vrow = E.dma("scalar", S["pqrow"][:].rearrange("p (b c) -> p b c", c=128),
                     PQV[h:h + 1, :, 1:129], deps=[pq_ev] + km_prev.get(h - 1, []))
        ev_y = []
        for q in range(NQC):
            pp = PS[4 + q]
            last = None
            for ct in range(CT):
                wsl = slice(ct * 128, (ct + 1) * 128)
                xsl = slice(ct * QB + QCH * q, ct * QB + QCH * (q + 1))
                d0 = [wh] + (evy_hist.get((h - 1, q), []) if ct == 0 else [])
                if ct == 0 and h == 0:
                    d0.append(pq_ev)
                if ct == 0 and q == 0 and vd_prev is not None:
                    d0.append(vd_prev)
                mm(pp[:, 0:QCH], S["wq_h"][:, wsl], S["xnq_h"][:, xsl], start=(ct == 0), stop=False, deps=d0)
                mm(pp[:, 0:QCH], S["wq_h"][:, wsl], S["xnq_l"][:, xsl], start=False, stop=False)
                last = mm(pp[:, 0:QCH], S["wq_l"][:, wsl], S["xnq_h"][:, xsl], start=False, stop=(ct == CT - 1), deps=[wl])
            ev_y.append(vop(lambda V, q=q, pp=pp: V.tensor_scalar_mul(S["y_q"][:, QCH * q:QCH * (q + 1)], pp[:, 0:QCH], INVSQD),
                            [last] + ([vd_prev] if vd_prev is not None else [])))
            evy_hist[(h, q)] = [ev_y[-1]]
        bcms = {}
        for q2 in range(NCH):
            sl = slice(CH * q2, CH * (q2 + 1))
            bcms[q2] = mm(PS[q2 % 2][:, 0:CH], S["onesrf"][:], S["pqrow"][0:1, sl], start=True, stop=True,
                          deps=[vrow] + km_prev.get((h - 1, q2), []))
        vd = vop(lambda V: V.tensor_tensor(DQV, QV[:, :, 1:129], QV[:, :, 0:128], ALU.subtract), ev_y)
        vd_prev = vd
        ev_q = []
        kms = []
        for q2 in range(NCH):
            sl = slice(CH * q2, CH * (q2 + 1))
            km = vop(lambda V, sl=sl, q2=q2: V.tensor_tensor(S["rch"][:, 0:CH], PS[q2 % 2][:, 0:CH], S["dQ"][:, sl], ALU.mult),
                     [bcms[q2], vd] + ev_q[-1:])
            km_prev[(h, q2)] = [km]
            nblk = CH // 128
            yo = QV[:, q2 * nblk:(q2 + 1) * nblk, 1:129]
            ev_q.append(vop(lambda V, sl=sl, yo=yo: V.tensor_tensor(
                S["qrope"][:, sl].rearrange("p (b c) -> p b c", c=128), yo,
                S["rch"][:, 0:CH].rearrange("p (b c) -> p b c", c=128), ALU.subtract), [km]))
            kms.append(km)
        km_prev[h] = kms
        ev_sp = []
        for q2 in range(NCH):
            sl = slice(CH * q2, CH * (q2 + 1))
            c1 = vop(lambda V, sl=sl: V.tensor_copy(S["krotb"][0:HLF, 0:CH], S["qrope"][:, sl][HLF:128, :]),
                     ev_q + ev_sp[-1:])
            c2 = vop(lambda V, sl=sl: V.tensor_copy(S["krotb"][HLF:128, 0:CH], S["qrope"][:, sl][0:HLF, :]), [c1])
            t1 = vop(lambda V, sl=sl: V.tensor_tensor(S["rch"][:, 0:CH], S["qrope"][:, sl], S["cosQ"][:, sl], ALU.mult),
                     [c2, ld["cosQ"]])
            t2 = vop(lambda V, sl=sl: V.tensor_tensor(S["krotb"][:, 0:CH], S["krotb"][:, 0:CH], S["sinQ2"][:, sl], ALU.mult),
                     [t1, ld["sinQ2"]])
            hi = vop(lambda V, sl=sl: V.tensor_tensor(S["qt_h"][:, sl], S["rch"][:, 0:CH], S["krotb"][:, 0:CH], ALU.add),
                     [t2] + (q_spills if q2 == 0 else []))
            t3 = vop(lambda V, sl=sl: V.tensor_tensor(S["rch"][:, 0:CH], S["rch"][:, 0:CH], S["krotb"][:, 0:CH], ALU.add), [hi])
            lo = vop(lambda V, sl=sl: V.tensor_tensor(S["qt_l"][:, sl], S["rch"][:, 0:CH], S["qt_h"][:, sl], ALU.subtract), [t3])
            ev_sp += [lo, hi]
        s1 = E.dma("scalar", dr["qh_sp"][h], S["qt_h"][:], deps=ev_sp + q_spills[-2:])
        s2 = E.dma("scalar", dr["ql_sp"][h], S["qt_l"][:], deps=ev_sp)
        q_spills = [s1, s2]
    xnq_sp = []
    for ct in range(CT):
        for nm, spn in (("xnq_h", "xnqh_sp"), ("xnq_l", "xnql_sp")):
            src = S[nm][:, ct * QB:(ct + 1) * QB].rearrange("p (b c) -> p b c", c=129)[:, :, 1:129]
            xnq_sp.append(E.dma("scalar", dr[spn][ct].rearrange("p (b c) -> p b c", c=128), src,
                                deps=q_spills if ct == 0 else []))
    A1_END = E.barrier()

    a2_rows = S["rowbuf"][0:1, 0:4 * CH]
    gate_last = []
    xl_sp_refs = []

    def a2_hi(ct):
        return S["xnh"][:, ct * T:(ct + 1) * T]

    def a2_lo(ct, hi, t2, t2buf):
        lob = S["lotile"][:, (ct % 2) * T:(ct % 2) * T + T]
        lo = vop(lambda V, lob=lob, ct=ct, t2buf=t2buf: V.tensor_tensor(
            lob, t2buf, S["xnh"][:, ct * T:(ct + 1) * T], ALU.subtract),
            [t2, hi] + (xl_sp_refs[-2:-1] if len(xl_sp_refs) >= 2 else []))
        spr = E.dma("scalar", dr["xl_sp"][ct], lob, deps=[lo])
        xl_sp_refs.append(spr)
        gm = []
        for q in range(NTC):
            sl = slice(CH * q, CH * (q + 1))
            wsl = slice(ct * 3 * H, (ct + 1) * 3 * H)
            gm.append(mm(PS[q][0:3 * H, 0:CH], S["wsp_h"][:, wsl],
                         S["xnh"][:, ct * T + CH * q:ct * T + CH * (q + 1)],
                         start=(ct == 0), stop=False, deps=[hi, wsp_h]))
            gm.append(mm(PS[q][0:3 * H, 0:CH], S["wsp_h"][:, wsl], lob[:, sl],
                         start=False, stop=False, deps=[lo]))
            gm.append(mm(PS[q][0:3 * H, 0:CH], S["wsp_l"][:, wsl],
                         S["xnh"][:, ct * T + CH * q:ct * T + CH * (q + 1)],
                         start=False, stop=(ct == CT - 1), deps=[wsp_l]))
        gate_last[:] = gm
        return [lo, spr, gm[-1]]

    ln_stream(dr["xkT"], T, NTC, CH, [S["stage"][:], S["stage2"][:]], S["x2b"][:],
              [S["stage"][:], S["stage2"][:]],
              S["rbc"][:], S["mrbc"][:], a2_rows, a2_hi, a2_lo, A1_END)
    gate_sig = []
    for q in range(NTC):
        gate_sig.append(sop(lambda Sc, q=q: Sc.activation(
            S["gates"][:, CH * q:CH * (q + 1)], PS[q][0:3 * H, 0:CH],
            AF.Sigmoid, bias=S["bfpack"][:], scale=1.0), gate_last + [ld["bfpack"]]))
    ln_ev = sop(lambda Sc: Sc.activation(S["gates"][0:H, :], S["gates"][0:H, :], AF.Ln), gate_sig)
    cpy = vop(lambda V: V.tensor_copy(S["cumB"][:], S["gates"][0:H, :]), [ln_ev, xl_sp_refs[-1]])
    cur_row = 1
    hs = cpy
    sh = 1
    while sh < T:
        srcb = S["cumB"][:] if cur_row == 1 else S["cumA"][:]
        dstb = S["cumA"][:] if cur_row == 1 else S["cumB"][:]
        h1 = vop(lambda V, srcb=srcb, dstb=dstb, sh=sh: V.tensor_tensor(
            dstb[:, sh:T], srcb[:, sh:T], srcb[:, 0:T - sh], ALU.add), [hs])
        hs = vop(lambda V, srcb=srcb, dstb=dstb, sh=sh: V.tensor_copy(dstb[:, 0:sh], srcb[:, 0:sh]), [h1])
        cur_row = 1 - cur_row
        sh *= 2
    assert cur_row == 0
    CUMR = S["cumA"][:]
    nct = [hs]
    for kt in range(TK):
        t0 = tr(PS[7][:, 0:H], CUMR[:, 128 * kt:128 * (kt + 1)], S["ident"][0:H, 0:H],
                deps=[hs] + nct[-2:])
        v1 = vop(lambda V, kt=kt: V.tensor_scalar_mul(S["negcumT"][:, H * kt:H * (kt + 1)], PS[7][:, 0:H], -1.0), [t0])
        v2 = vop(lambda V, kt=kt: V.tensor_copy(S["cumbf"][:, H * kt:H * (kt + 1)], PS[7][:, 0:H]), [t0])
        nct += [v1, v2]
    bq = []
    seld_hist = []
    for q2 in range(NCH):
        last = None
        for kt in range(TK):
            buf = kt % 2
            dsel = E.dma("sync", S["selp"][:, buf * TQ:buf * TQ + TQ], dr["selq"][kt],
                         deps=(seld_hist[-2:-1] if len(seld_hist) >= 2 else []))
            last = mm(PS[6][0:H, 0:CH], S["cumbf"][:, H * kt:H * (kt + 1)],
                      S["selp"][:, buf * TQ + CH * q2:buf * TQ + CH * (q2 + 1)],
                      start=(kt == 0), stop=(kt == TK - 1), deps=[dsel] + nct[-2:] + bq)
            seld_hist.append(last)
        bq = [vop(lambda V, q2=q2: V.tensor_scalar_add(
            S["biasQ"][:, CH * q2:CH * (q2 + 1)], PS[6][0:H, 0:CH], -float(MQ)), [last])]
    biasQ_done = bq[0]

    assert NCH == 2 and NTC == 4
    zcol = vop(lambda V: V.memset(S["y_k"][:, 0:1], 0.0), [biasQ_done])
    zcol2 = vop(lambda V: V.memset(S["dK"][:, 0:1], 0.0), [zcol])
    o_spills = []
    st = {}
    st["sc_last"] = [biasQ_done]
    st["sbank"] = {}
    st["pbank"] = {}
    st["obank"] = []
    st["lbank"] = []
    st["wfree"] = [zcol2]
    st["stagefree"] = [biasQ_done]
    st["ykfree"] = {}
    st["ropece"] = {}
    st["vgate"] = {}
    st["kgate"] = {}
    st["pvbc"] = {}
    st["pkbc"] = {}
    st["E"] = {}
    st["qld"] = {}
    st["bh"] = {}
    st["oev"] = []
    st["elast"] = {}
    st["xl_rd"] = {}
    st["trs"] = {}
    st["xln"] = 0

    def emit_wq_loads(h):
        dWv = E.dma("sync", S["stage"][:, 0:CT * 128].rearrange("p (a d) -> p a d", d=128),
                    dr["wvT"][h].rearrange("(a p) d -> p a d", p=128), deps=st["stagefree"])
        wvh = vop(lambda V: V.tensor_copy(S["wv_h"][:], S["stage"][:, 0:CT * 128]), [dWv] + st["wfree"])
        wvl = vop(lambda V: V.tensor_tensor(S["wv_l"][:], S["stage"][:, 0:CT * 128], S["wv_h"][:], ALU.subtract), [wvh])
        dWk = E.dma("sync", S["stage"][:, 0:CT * 128].rearrange("p (a d) -> p a d", d=128),
                    dr["wkT"][h].rearrange("(a p) d -> p a d", p=128), deps=[wvl])
        wkh = vop(lambda V: V.tensor_copy(S["wk_h"][:], S["stage"][:, 0:CT * 128]), [dWk])
        wkl = vop(lambda V: V.tensor_tensor(S["wk_l"][:], S["stage"][:, 0:CT * 128], S["wk_h"][:], ALU.subtract), [wkh])
        st["stagefree"] = [wkl]
        st.setdefault("W", {})[h] = (wvh, wvl, wkh, wkl)
        par = h % 2
        bh = E.dma("scalar", S["bhrow"][:, par * TQ:(par + 1) * TQ], S["biasQ"][h:h + 1, :],
                   deps=[biasQ_done] + st["sc_last"])
        st["bh"][h] = [bh]

    def emit_qload(h):
        dQh = E.dma("scalar", S["qt_h"][:], dr["qh_sp"][h], deps=st["sc_last"] + ([q_spills[0]] if h == 0 else []))
        dQl = E.dma("scalar", S["qt_l"][:], dr["ql_sp"][h], deps=([q_spills[1]] if h == 0 else []))
        st["qld"][h] = [dQh, dQl]

    def _proj_ct(h, qp, ct):
        wvh, wvl, wkh, wkl = st["W"][h]
        W2 = 2 * CH
        li = st["xln"]
        st["xln"] = li + 1
        buf = li % 4
        xlb = S["xlp"][:, buf * W2:(buf + 1) * W2]
        dxl = E.dma("sync", xlb, dr["xl_sp"][ct][:, qp * W2:(qp + 1) * W2],
                    deps=[xl_sp_refs[ct]] + st["xl_rd"].get(li - 4, []))
        wsl = slice(ct * 128, (ct + 1) * 128)
        refs = []
        for qq in range(2):
            q = 2 * qp + qq
            xsl = slice(ct * T + CH * q, ct * T + CH * (q + 1))
            lsl = slice(CH * qq, CH * (qq + 1))
            d0v = [wvh] + (st["pbank"].pop(4 + qq, []) if ct == 0 else [])
            d0k = [wkh] + (st["pbank"].pop(6 + qq, []) if ct == 0 else [])
            mm(PS[4 + qq][:, 0:CH], S["wv_h"][:, wsl], S["xnh"][:, xsl], start=(ct == 0), stop=False, deps=d0v)
            mm(PS[4 + qq][:, 0:CH], S["wv_l"][:, wsl], S["xnh"][:, xsl], start=False, stop=False, deps=[wvl])
            r1 = mm(PS[4 + qq][:, 0:CH], S["wv_h"][:, wsl], xlb[:, lsl], start=False, stop=(ct == CT - 1), deps=[dxl])
            mm(PS[6 + qq][:, 0:CH], S["wk_h"][:, wsl], S["xnh"][:, xsl], start=(ct == 0), stop=False, deps=d0k)
            mm(PS[6 + qq][:, 0:CH], S["wk_l"][:, wsl], S["xnh"][:, xsl], start=False, stop=False, deps=[wkl])
            r2 = mm(PS[6 + qq][:, 0:CH], S["wk_h"][:, wsl], xlb[:, lsl], start=False, stop=(ct == CT - 1), deps=[dxl])
            refs += [r1, r2]
        st["xl_rd"][li] = refs

    def _proj_evacs(h, qp):
        vev = []
        kev = []
        for qq in range(2):
            q = 2 * qp + qq
            lastr = st["xl_rd"][st["xln"] - 1]
            yw = st["ykfree"].pop((h - 1, q), [])
            ve = vop(lambda V, q=q: V.tensor_copy(S["y_k"][:, 1 + CH * q:1 + CH * (q + 1)], PS[4 + (q % 2)][:, 0:CH]),
                     lastr + yw + [zcol])
            ke = vop(lambda V, q=q: V.tensor_copy(S["dK"][:, 1 + CH * q:1 + CH * (q + 1)], PS[6 + (q % 2)][:, 0:CH]),
                     lastr + st["ropece"].get(h - 1, []) + [zcol2])
            vev.append(ve)
            kev.append(ke)
            st["pbank"][4 + qq] = [ve]
            st["pbank"][6 + qq] = [ke]
        st[("vev", h, qp)] = vev
        st[("kev", h, qp)] = kev

    def _pvbc(h):
        refs = {}
        mms = []
        for q in range(NTC):
            sl = slice(CH * q, CH * (q + 1))
            dep0 = st["grow_ld"] + st["pbank"].pop(7 - q, [])
            bm = mm(PS[7 - q][:, 0:CH], S["onesrf"][:], S["rowbuf"][0:1, sl], start=True, stop=True, deps=dep0)
            refs[q] = bm
            mms.append(bm)
        st["pvbc"][h] = refs
        st["grow_free"] = mms

    def make_proj_closures(h):
        cl = []
        for qp in range(2):
            for ct in range(CT):
                cl.append(lambda qp=qp, ct=ct: _proj_ct(h, qp, ct))
            cl.append(lambda qp=qp: _proj_evacs(h, qp))
        cl.append(lambda: _pvbc(h))
        cl.append(lambda: emit_growload(h, "k"))
        return cl

    def emit_growload(h, which):
        row = (2 * H + h) if which == "v" else (H + h)
        gv = E.dma("scalar", S["rowbuf"][0:1, 0:T], S["gates"][row:row + 1, :],
                   deps=gate_sig + st.get("grow_free", []))
        st["grow_ld"] = [gv]
    def emit_growload(h, which):
        row = (2 * H + h) if which == "v" else (H + h)
        gv = E.dma("scalar", S["rowbuf"][0:1, 0:T], S["gates"][row:row + 1, :],
                   deps=gate_sig + st.get("grow_free", []))
        st["grow_ld"] = [gv]

    def emit_pvbc(h):
        refs = {}
        mms = []
        for q in range(NTC):
            sl = slice(CH * q, CH * (q + 1))
            dep0 = st["grow_ld"] + st["pbank"].pop(7 - q, [])
            bm = mm(PS[7 - q][:, 0:CH], S["onesrf"][:], S["rowbuf"][0:1, sl], start=True, stop=True, deps=dep0)
            refs[q] = bm
            mms.append(bm)
        st["pvbc"][h] = refs
        st["grow_free"] = mms

    def emit_vgate(h):
        vg = {}
        prev = []
        for q in range(NTC - 1, -1, -1):
            sl = slice(CH * q, CH * (q + 1))
            dch = vop(lambda V, sl=sl: V.tensor_tensor(S["krotb"][:, 0:CH], S["y_k"][:, 1 + sl.start:1 + sl.stop],
                                                       S["y_k"][:, sl.start:sl.stop], ALU.subtract),
                      [st[("vev", h, q // 2)][q % 2], st["pvbc"][h][q]] + prev + st["ropece"].get(h - 1, []))
            km = vop(lambda V, q=q: V.tensor_tensor(S["rch2"][:, 0:CH], PS[7 - q][:, 0:CH], S["krotb"][:, 0:CH], ALU.mult), [dch])
            vgq = vop(lambda V, sl=sl: V.tensor_tensor(S["y_k"][:, 1 + sl.start:1 + sl.stop],
                                                       S["y_k"][:, 1 + sl.start:1 + sl.stop], S["rch2"][:, 0:CH], ALU.subtract), [km])
            vg[q] = vgq
            prev = [vgq]
            st["pbank"][7 - q] = [km]
        st["vgate"][h] = vg

    def emit_pkbc(h):
        refs = {}
        mms = []
        for q in range(NTC):
            sl = slice(CH * q, CH * (q + 1))
            bm = mm(PS[7 - q][:, 0:CH], S["onesrf"][:], S["rowbuf"][0:1, sl], start=True, stop=True,
                    deps=st["grow_ld"] + st["pbank"].pop(7 - q, []))
            refs[q] = bm
            mms.append(bm)
        st["pkbc"][h] = refs
        st["grow_free"] = mms

    def emit_transp(h):
        trs = {}
        prev = {}
        for stt_ in range(TK):
            bank = PS[stt_ % 2]
            t0 = tr(bank[:, 0:128], S["y_k"][:, 1 + 128 * stt_:1 + 128 * (stt_ + 1)], S["ident"][:],
                    deps=[st["vgate"][h][stt_ // (TK // NTC)]] + st["sbank"].pop(stt_ % 2, []) + prev.get(stt_ % 2, []))
            hi = sop(lambda Sc, stt_=stt_, bank=bank: Sc.copy(S["vtm_h"][:, 128 * stt_:128 * (stt_ + 1)], bank[:, 0:128]), [t0])
            lo = vop(lambda V, stt_=stt_, bank=bank: V.tensor_tensor(
                S["vtm_l"][:, 128 * stt_:128 * (stt_ + 1)], bank[:, 0:128],
                S["vtm_h"][:, 128 * stt_:128 * (stt_ + 1)], ALU.subtract), [hi])
            prev[stt_ % 2] = [lo]
            st["ykfree"][(h, stt_ // (TK // NTC))] = [t0]
            trs[stt_] = [t0, hi, lo]
        st["sbank"][0] = prev.get(0, [])
        st["sbank"][1] = prev.get(1, [])
        st["trs"][h] = [trs[TK - 1][2], trs[TK - 2][2]]

    def emit_kgate_rope(h):
        kg = {}
        prev = []
        for q in range(NTC - 1, -1, -1):
            sl = slice(CH * q, CH * (q + 1))
            dch = vop(lambda V, sl=sl: V.tensor_tensor(S["krotb"][:, 0:CH], S["dK"][:, 1 + sl.start:1 + sl.stop],
                                                       S["dK"][:, sl.start:sl.stop], ALU.subtract),
                      [st[("kev", h, q // 2)][q % 2], st["pkbc"][h][q]] + prev)
            km = vop(lambda V, q=q: V.tensor_tensor(S["rch2"][:, 0:CH], PS[7 - q][:, 0:CH], S["krotb"][:, 0:CH], ALU.mult), [dch])
            kgq = vop(lambda V, sl=sl: V.tensor_tensor(S["dK"][:, 1 + sl.start:1 + sl.stop],
                                                       S["dK"][:, 1 + sl.start:1 + sl.stop], S["rch2"][:, 0:CH], ALU.subtract), [km])
            kg[q] = kgq
            prev = [kgq]
            st["pbank"][7 - q] = [km]
        rope_refs = []
        sc_prev = st["sc_last"]
        for q in range(NTC):
            sl = slice(CH * q, CH * (q + 1))
            c1 = vop(lambda V, sl=sl: V.tensor_copy(S["krotb"][0:HLF, 0:CH], S["dK"][:, 1 + sl.start:1 + sl.stop][HLF:128, :]),
                     [kg[0]] + rope_refs[-1:])
            c2 = vop(lambda V, sl=sl: V.tensor_copy(S["krotb"][HLF:128, 0:CH], S["dK"][:, 1 + sl.start:1 + sl.stop][0:HLF, :]), [c1])
            t1 = vop(lambda V, sl=sl: V.tensor_tensor(S["rch2"][:, 0:CH], S["dK"][:, 1 + sl.start:1 + sl.stop],
                                                      S["cosK"][:, sl], ALU.mult), [c2])
            hi = vop(lambda V, sl=sl: V.scalar_tensor_tensor(S["K_h"][:, sl], S["krotb"][:, 0:CH], 1.0,
                                                             S["sinK2"][:, sl], ALU.mult, ALU.mult), [t1]) if False else None
            t2 = vop(lambda V, sl=sl: V.tensor_tensor(S["krotb"][:, 0:CH], S["krotb"][:, 0:CH], S["sinK2"][:, sl], ALU.mult), [t1])
            hi = vop(lambda V, sl=sl: V.tensor_tensor(S["K_h"][:, sl], S["rch2"][:, 0:CH], S["krotb"][:, 0:CH], ALU.add),
                     [t2] + sc_prev)
            t3 = vop(lambda V, sl=sl: V.tensor_tensor(S["rch2"][:, 0:CH], S["rch2"][:, 0:CH], S["krotb"][:, 0:CH], ALU.add), [hi])
            lo = vop(lambda V, sl=sl: V.tensor_tensor(S["K_l"][:, sl], S["rch2"][:, 0:CH], S["K_h"][:, sl], ALU.subtract), [t3])
            rope_refs = [lo]
            sc_prev = []
        st["ropece"][h] = rope_refs
        st["kgate"][h] = [kg[0]]

    def emit_scores(h, cc, fillers):
        nkt = KTPC * (cc + 1)
        csl = slice(CH * cc, CH * (cc + 1))
        par = h % 2
        l_mms = []
        last_pv = None
        ex_prev = {}
        pend = {}
        first_pv_dep = st["obank"]

        def emit_pv(kt):
            nonlocal last_pv
            ksl, ehs, els, eh_ev, el_ev = pend.pop(kt)
            m1 = mm(PS[2][:, 0:CH], S["vtm_h"][:, ksl], ehs, start=(kt == 0), stop=False,
                    deps=[eh_ev, st["trs"][h][0]] + (first_pv_dep if kt == 0 else []))
            mm(PS[2][:, 0:CH], S["vtm_l"][:, ksl], ehs, start=False, stop=False, deps=[st["trs"][h][1]])
            last_pv = mm(PS[2][:, 0:CH], S["vtm_h"][:, ksl], els, start=False, stop=(kt == nkt - 1), deps=[el_ev])
            l1 = mm(PS[3][0:1, 0:CH], S["onescb"][:], ehs, start=(kt == 0), stop=False,
                    deps=[eh_ev] + (st["lbank"] if kt == 0 else []))
            l2 = mm(PS[3][0:1, 0:CH], S["onescb"][:], els, start=False, stop=(kt == nkt - 1), deps=[el_ev])
            l_mms.append(l2)
            st["elast"][kt % 2] = [el_ev, last_pv if kt == nkt - 1 else m1]

        for kt in range(nkt):
            if fillers:
                fillers.pop(0)()
            bank = PS[kt % 2]
            ksl = slice(128 * kt, 128 * (kt + 1))
            d0 = st["ropece"][h] + st["qld"][h] + ex_prev.get(kt - 2, []) + st["sbank"].pop(kt % 2, [])
            mm(bank[:, 0:CH], S["K_h"][:, ksl], S["qt_h"][:, csl], start=True, stop=False, deps=d0)
            mm(bank[:, 0:CH], S["K_h"][:, ksl], S["qt_l"][:, csl], start=False, stop=False)
            mm(bank[:, 0:CH], S["K_l"][:, ksl], S["qt_h"][:, csl], start=False, stop=False)
            sc_mm = mm(bank[:, 0:CH], S["onesrb"][:], S["bhrow"][:, par * TQ + csl.start:par * TQ + csl.stop],
                       start=False, stop=True, deps=st["bh"][h])
            ebuf = kt % 2
            efs = S["ef"][:, ebuf * CH:(ebuf + 1) * CH]
            ehs = S["eh"][:, ebuf * CH:(ebuf + 1) * CH]
            els = S["el"][:, ebuf * CH:(ebuf + 1) * CH]
            bias_ap = S["negcumT"][:, H * kt + h:H * kt + h + 1]
            eb = st["elast"].get(ebuf, [])
            if kt >= KTPC * cc:
                mk = vop(lambda V, kt=kt, bank=bank, efs=efs: V.scalar_tensor_tensor(
                    efs, S["masks"][:, CH * kt:CH * (kt + 1)], NEG, bank[:, 0:CH], ALU.mult, ALU.add),
                    [sc_mm, ld["masks"]] + ex_prev.get(kt - 2, []) + eb)
                ex = sop(lambda Sc, efs=efs, bias_ap=bias_ap: Sc.activation(
                    efs, efs, AF.Exp, bias=bias_ap, scale=1.0), [mk] + eb)
                ex_psum_reader = mk
            else:
                ex = sop(lambda Sc, efs=efs, bank=bank, bias_ap=bias_ap: Sc.activation(
                    efs, bank[:, 0:CH], AF.Exp, bias=bias_ap, scale=1.0),
                    [sc_mm] + ex_prev.get(kt - 2, []) + eb)
                ex_psum_reader = ex
            eh_ev = sop(lambda Sc, ehs=ehs, efs=efs: Sc.copy(ehs, efs), [ex])
            el_ev = vop(lambda V, els=els, efs=efs, ehs=ehs: V.tensor_tensor(els, efs, ehs, ALU.subtract), [eh_ev])
            ex_prev[kt] = [ex_psum_reader]
            pend[kt] = (ksl, ehs, els, eh_ev, el_ev)
            if kt >= 1:
                emit_pv(kt - 1)
        emit_pv(nkt - 1)
        st["sc_last"] = [last_pv, l_mms[-1]]
        rin = vop(lambda V: V.reciprocal(S["invlrow"][:], PS[3][0:1, 0:CH]), l_mms[-1:])
        ibm = mm(PS[3][:, 0:CH], S["onesrf"][:], S["invlrow"][:], start=True, stop=True, deps=[rin])
        ibc = vop(lambda V: V.tensor_copy(S["invlbc"][:], PS[3][:, 0:CH]), [ibm])
        st["lbank"] = [ibc]
        st["sbank"][nkt % 2] = [ex_prev[nkt - 1][0]] if False else st["sbank"].get(nkt % 2, [])
        ob = cc % 2
        ohs = S["o_h"][:, ob * CH:(ob + 1) * CH]
        ols = S["o_l"][:, ob * CH:(ob + 1) * CH]
        hi = vop(lambda V, ohs=ohs: V.tensor_tensor(ohs, PS[2][:, 0:CH], S["invlbc"][:], ALU.mult),
                 [ibc, last_pv] + o_spills[-4:-2])
        tt_ = vop(lambda V: V.tensor_tensor(S["rch2"][:, 0:CH], PS[2][:, 0:CH], S["invlbc"][:], ALU.mult), [hi])
        lo = vop(lambda V, ohs=ohs, ols=ols: V.tensor_tensor(ols, S["rch2"][:, 0:CH], ohs, ALU.subtract), [tt_])
        st["obank"] = [hi, tt_]
        o_spills.append(E.dma("scalar", dr["oh_sp"][h][:, csl], ohs, deps=[hi]))
        o_spills.append(E.dma("scalar", dr["ol_sp"][h][:, csl], ols, deps=[lo]))

    st["grow_ld"] = []
    NPRE = 9
    emit_growload(0, "v")
    emit_wq_loads(0)
    carry = make_proj_closures(0)
    for it in range(H):
        if it >= 1:
            emit_scores(it - 1, 0, carry)
            emit_scores(it - 1, 1, carry)
        while carry:
            carry.pop(0)()
        emit_vgate(it)
        emit_pkbc(it)
        if it + 1 < H:
            emit_wq_loads(it + 1)
        emit_transp(it)
        emit_kgate_rope(it)
        emit_qload(it)
        if it + 1 < H:
            emit_growload(it + 1, "v")
            carry = make_proj_closures(it + 1)
            for _ in range(min(NPRE, len(carry))):
                carry.pop(0)()
        else:
            carry = []
    emit_scores(H - 1, 0, carry)
    emit_scores(H - 1, 1, carry)
    A2_END = E.barrier()

    B0_END = A2_END

    bx = []
    for ct in range(CT):
        bx.append(E.dma("sync", S["bxnq_h"][:, ct * TQ:(ct + 1) * TQ], dr["xnqh_sp"][ct], deps=(B0_END if ct == 0 else [])))
        bx.append(E.dma("sync", S["bxnq_l"][:, ct * TQ:(ct + 1) * TQ], dr["xnql_sp"][ct], deps=[]))
    conv_prev = {}
    sil_prev = []
    su_prev = []
    suf_prev = []
    last_down_mm = []
    ga_all = {}
    dbank_last = {}
    wd_conv = {}
    for hc in range(NHC):
        tiles = list(range(hc * HIDCH, min(HT, (hc + 1) * HIDCH)))
        nj = len(tiles)
        su_refs = {}
        dprev = []
        for j, ht in enumerate(tiles):
            gslot = S["wbf"][:, (j % 2) * CT * 128:(j % 2 + 1) * CT * 128]
            uslot = S["wbf"][:, (2 + j % 2) * CT * 128:(3 + j % 2) * CT * 128]
            dg = E.dma("sync", S["wint"][:].rearrange("p (a d) -> p a d", d=128),
                       dr["wgT"][ht].rearrange("(a p) d -> p a d", p=128),
                       deps=conv_prev.get("w", []) + (bx if hc == 0 and j == 0 else []))
            cg = vop(lambda V, gslot=gslot: V.tensor_scalar_add(gslot, S["wint"][:], -128.0),
                     [dg] + conv_prev.get(("g", j % 2), []) + (B0_END if hc == 0 and j == 0 else []))
            du = E.dma("sync", S["wint"][:].rearrange("p (a d) -> p a d", d=128),
                       dr["wuT"][ht].rearrange("(a p) d -> p a d", p=128), deps=[cg])
            cu = vop(lambda V, uslot=uslot: V.tensor_scalar_add(uslot, S["wint"][:], -128.0),
                     [du] + conv_prev.get(("u", j % 2), []))
            dd = E.dma("sync", S["wint"][:], dr["wdT"][ht], deps=[cu])
            wds = S["wdc"][:, j * CT * 128:(j + 1) * CT * 128]
            cd = vop(lambda V, wds=wds: V.tensor_scalar_add(wds, S["wint"][:], -128.0),
                     [dd] + wd_conv.get((hc - 1, j), []))
            wd_conv[(hc, j)] = [cd]
            conv_prev["w"] = [cd]
            g_sil = {}
            g_last = {}
            for tc in range(NCH):
                bank = PS[4 + tc % 2]
                last = None
                for ct in range(CT):
                    wsl = slice(ct * 128, (ct + 1) * 128)
                    xsl = slice(ct * TQ + CH * tc, ct * TQ + CH * (tc + 1))
                    d0 = [cg, bx[2 * ct], bx[2 * ct + 1]] + sil_prev
                    mm(bank[:, 0:CH], gslot[:, wsl], S["bxnq_h"][:, xsl], start=(ct == 0), stop=False,
                       deps=d0 if ct == 0 else [cg])
                    last = mm(bank[:, 0:CH], gslot[:, wsl], S["bxnq_l"][:, xsl], start=False, stop=(ct == CT - 1))
                gs = sop(lambda Sc, tc=tc, bank=bank, ht=ht: Sc.activation(
                    S["gsil"][:, CH * (tc % 2):CH * (tc % 2 + 1)], bank[:, 0:CH], AF.Silu,
                    scale=S["sg"][:, ht:ht + 1], bias=0.0), [last, ld["sg"]] + su_prev)
                g_sil[tc] = gs
                g_last[tc] = last
            sil_prev = [g_sil[NCH - 1]]
            conv_prev[("g", j % 2)] = [g_last[NCH - 1]]
            u_last = None
            for tc in range(NCH):
                bank = PS[6 + tc % 2]
                last = None
                for ct in range(CT):
                    wsl = slice(ct * 128, (ct + 1) * 128)
                    xsl = slice(ct * TQ + CH * tc, ct * TQ + CH * (tc + 1))
                    mm(bank[:, 0:CH], uslot[:, wsl], S["bxnq_h"][:, xsl], start=(ct == 0), stop=False,
                       deps=[cu] + (su_prev if ct == 0 else []))
                    last = mm(bank[:, 0:CH], uslot[:, wsl], S["bxnq_l"][:, xsl], start=False, stop=(ct == CT - 1))
                suf = vop(lambda V, tc=tc, bank=bank, ht=ht: V.scalar_tensor_tensor(
                    S["su_f"][:], bank[:, 0:CH], S["sup"][:, ht:ht + 1],
                    S["gsil"][:, CH * (tc % 2):CH * (tc % 2 + 1)], ALU.mult, ALU.mult),
                    [last, g_sil[tc], ld["sup"]] + suf_prev)
                shs = S["su_h"][:, j * TQ + CH * tc:j * TQ + CH * (tc + 1)]
                sls = S["su_l"][:, j * TQ + CH * tc:j * TQ + CH * (tc + 1)]
                sh1 = sop(lambda Sc, shs=shs: Sc.copy(shs, S["su_f"][:]), [suf] + last_down_mm)
                sl1 = vop(lambda V, sls=sls, shs=shs: V.tensor_tensor(sls, S["su_f"][:], shs, ALU.subtract), [sh1])
                su_refs[(j, tc)] = [sh1, sl1]
                suf_prev = [sh1, sl1]
                u_last = last
            su_prev = su_refs[(j, NCH - 1)]
            conv_prev[("u", j % 2)] = [u_last]
        for ct in range(CT):
            for tc in range(NCH):
                bank = PS[tc % 2]
                last = None
                for j in range(nj):
                    wsl = slice(j * CT * 128 + ct * 128, j * CT * 128 + (ct + 1) * 128)
                    ssl = slice(j * TQ + CH * tc, j * TQ + CH * (tc + 1))
                    d0 = su_refs[(j, tc)] + [wd_conv[(hc, j)][0]] + ((ga_all.get((hc - 1, ct, tc), []) + dbank_last.get(tc % 2, [])) if j == 0 else [])
                    mm(bank[:, 0:CH], S["wdc"][:, wsl], S["su_h"][:, ssl], start=(j == 0), stop=False, deps=d0)
                    last = mm(bank[:, 0:CH], S["wdc"][:, wsl], S["su_l"][:, ssl], start=False, stop=(j == nj - 1))
                gsl = slice(ct * TQ + CH * tc, ct * TQ + CH * (tc + 1))
                if hc == 0:
                    ga = vop(lambda V, gsl=gsl, bank=bank: V.tensor_copy(S["g_acc"][:, gsl], bank[:, 0:CH]), [last])
                else:
                    ga = vop(lambda V, gsl=gsl, bank=bank: V.tensor_tensor(
                        S["g_acc"][:, gsl], S["g_acc"][:, gsl], bank[:, 0:CH], ALU.add),
                        [last] + ga_all.get((hc - 1, ct, tc), []))
                ga_all[(hc, ct, tc)] = [ga]
                dbank_last[tc % 2] = [ga]
                last_down_mm = [last]
    B1_END = E.barrier()
    ob_ld = []
    for hh in range(H):
        ob_ld.append(E.dma("scalar", S["ob_h"][:, hh * TQ:(hh + 1) * TQ], dr["oh_sp"][hh],
                           deps=(list(suf_prev) + [o_spills[-1]]) if hh == 0 else []))
        ob_ld.append(E.dma("scalar", S["ob_l"][:, hh * TQ:(hh + 1) * TQ], dr["ol_sp"][hh], deps=[]))

    st_done = []
    v1_hist = []
    v2_hist = []
    wo_rd = {}
    wsplit = {}
    def emit_wosplit(ct):
        buf = ct % 2
        wfs = S["wo_f"][:, buf * CT * 128:(buf + 1) * CT * 128]
        hcols = CT * 128 // 2
        d0 = (B1_END if ct < 2 else []) + wsplit.get(ct - 2, [])
        dwo_a = E.dma("sync", wfs[:, 0:hcols].rearrange("p (a d) -> p a d", d=128),
                      dr["woT4"][ct, 0:CT // 2].rearrange("a p d -> p a d"), deps=d0)
        dwo_b = E.dma("gpsimd", wfs[:, hcols:2 * hcols].rearrange("p (a d) -> p a d", d=128),
                      dr["woT4"][ct, CT // 2:CT].rearrange("a p d -> p a d"), deps=d0)
        whs = S["wo_h"][:, buf * CT * 128:(buf + 1) * CT * 128]
        wls = S["wo_l"][:, buf * CT * 128:(buf + 1) * CT * 128]
        whi = vop(lambda V, whs=whs, wfs=wfs: V.tensor_copy(whs, wfs),
                  [dwo_a, dwo_b] + wo_rd.get(ct - 2, []) + (B1_END if ct < 2 else []))
        wlo = vop(lambda V, whs=whs, wls=wls, wfs=wfs: V.tensor_tensor(wls, wfs, whs, ALU.subtract), [whi])
        wsplit[ct] = [wlo]
        wo_rd[ct] = [wlo]
        return whs, wls, whi, wlo

    wosp = {0: emit_wosplit(0)}
    for ct in range(CT):
        if ct + 1 < CT:
            wosp[ct + 1] = emit_wosplit(ct + 1)
        whs, wls, whi, wlo = wosp[ct]
        for cc in range(NCH):
            csl = slice(CH * cc, CH * (cc + 1))
            bank = PS[2 + (2 * ct + cc) % 2]
            last = None
            for cp in range(CT):
                wsl = slice(cp * 128, (cp + 1) * 128)
                osl = slice(cp * TQ + CH * cc, cp * TQ + CH * (cc + 1))
                d0 = [whi, ob_ld[2 * cp], ob_ld[2 * cp + 1]] + ((v1_hist[-2:-1] if len(v1_hist) >= 2 else []) if cp == 0 else [])
                mm(bank[:, 0:CH], whs[:, wsl], S["ob_h"][:, osl], start=(cp == 0), stop=False, deps=d0)
                mm(bank[:, 0:CH], wls[:, wsl], S["ob_h"][:, osl], start=False, stop=False, deps=[wlo])
                last = mm(bank[:, 0:CH], whs[:, wsl], S["ob_l"][:, osl], start=False, stop=(cp == CT - 1))
            wo_rd[ct] = wo_rd[ct] + [last]
            gb = sop(lambda Sc, ct=ct, cc=cc, b2f=(2 * ct + cc) % 2: Sc.activation(
                S["gsil2"][:, b2f * CH:(b2f + 1) * CH],
                S["g_acc"][:, ct * TQ + CH * cc:ct * TQ + CH * (cc + 1)],
                AF.Silu, scale=S["sdn"][:, ct:ct + 1], bias=0.0),
                [ld["sdn"]] + (B1_END if ct == 0 and cc == 0 else []) + (v1_hist[-2:-1] if len(v1_hist) >= 2 else []))
            b2f = (2 * ct + cc) % 2
            dx = E.dma("scalar", S["xres"][:, b2f * CH:(b2f + 1) * CH],
                       dr["xqresT"][128 * ct:128 * (ct + 1), csl],
                       deps=(v2_hist[-2:-1] if len(v2_hist) >= 2 else []))
            v1 = vop(lambda V, b2f=b2f, bank=bank: V.tensor_tensor(
                S["outsb"][:, b2f * CH:(b2f + 1) * CH], bank[:, 0:CH],
                S["gsil2"][:, b2f * CH:(b2f + 1) * CH], ALU.mult),
                [last, gb] + (st_done[-2:-1] if len(st_done) >= 2 else []))
            v2 = vop(lambda V, b2f=b2f: V.tensor_tensor(
                S["outsb"][:, b2f * CH:(b2f + 1) * CH], S["outsb"][:, b2f * CH:(b2f + 1) * CH],
                S["xres"][:, b2f * CH:(b2f + 1) * CH], ALU.add), [v1, dx])
            st_ = E.dma("sync", outT[128 * ct:128 * (ct + 1), csl], S["outsb"][:, b2f * CH:(b2f + 1) * CH], deps=[v2])
            st_done.append(st_)
            v1_hist.append(v1)
            v2_hist.append(v2)
    tail = E.barrier() + st_done[-1:]
    E.ev("sync", lambda Eng: Eng.nop(), tail)

    E.finalize()

    import contextlib
    with contextlib.ExitStack() as ctx:
        sems = {}
        rings = {}
        for e in ENGS:
            sems[e] = ctx.enter_context(nc.semaphore(f"sm_{e}"))
            if E.ndma[e]:
                rings[e] = [ctx.enter_context(nc.semaphore(f"dq_{e}_{k}"))
                            for k in range(min(NRING, E.ndma[e]))]
        block = ctx.enter_context(nc.Block())

        @block.tensor
        def _(Eng):
            E.replay("tensor", Eng, sems, rings)

        @block.vector
        def _(Eng):
            E.replay("vector", Eng, sems, rings)

        @block.scalar
        def _(Eng):
            E.replay("scalar", Eng, sems, rings)

        @block.gpsimd
        def _(Eng):
            E.replay("gpsimd", Eng, sems, rings)

        @block.sync
        def _(Eng):
            E.replay("sync", Eng, sems, rings)

    return nc



def _prep_shared(cfg, W):
    C = cfg["C"]; T = cfg["T"]; H = cfg["H"]; HID = cfg["HID"]
    CT = C // 128
    HT = HID // 128
    f32 = np.float32
    sh = {}
    sh["wqT"] = np.ascontiguousarray(W["Wq"].T.reshape(C, H, 128).transpose(1, 0, 2)).astype(f32)
    sh["wkT"] = np.ascontiguousarray(W["Wk"].T.reshape(C, H, 128).transpose(1, 0, 2)).astype(f32)
    sh["wvT"] = np.ascontiguousarray(W["Wv"].T.reshape(C, H, 128).transpose(1, 0, 2)).astype(f32)
    sh["wsqT"] = np.ascontiguousarray(W["Wsq"].T.reshape(CT, 128, H)).astype(f32)
    wsp = np.concatenate([W["Wf"].T, W["Wsk"].T, W["Wsv"].T], axis=1)
    sh["wspT"] = np.ascontiguousarray(wsp.reshape(CT, 128, 3 * H)).astype(f32)
    bfp = np.zeros((3 * H, 1), f32)
    bfp[0:H, 0] = W["bf"]
    sh["bfpack"] = bfp
    WoT = W["Wo"].T
    sh["woT4"] = np.ascontiguousarray(WoT.reshape(CT, 128, CT, 128).transpose(2, 0, 1, 3)).astype(f32)
    sh["wgT"] = np.ascontiguousarray(W["wq_gate"].T.reshape(C, HT, 128).transpose(1, 0, 2)).astype(np.int32)
    sh["wuT"] = np.ascontiguousarray(W["wq_up"].T.reshape(C, HT, 128).transpose(1, 0, 2)).astype(np.int32)
    sh["wdT"] = np.ascontiguousarray(W["wq_down"].T.reshape(HT, 128, C)).astype(np.int32)
    sh["sg"] = np.ascontiguousarray(W["s_gate"].reshape(HT, 128).T).astype(f32)
    sh["sup"] = np.ascontiguousarray(W["s_up"].reshape(HT, 128).T).astype(f32)
    sh["sdn"] = np.ascontiguousarray(W["s_down"].reshape(CT, 128).T).astype(f32)
    cos, sin = _ref_rope_tables(cfg)
    cos = cos.T
    sin = sin.T
    sh["cosK"] = np.concatenate([cos, cos], axis=0).astype(f32)
    sh["sinK2"] = np.concatenate([sin, -sin], axis=0).astype(f32)
    sh["ident"] = np.eye(128, dtype=f32)
    sh["onescf"] = np.ones((128, 1), f32)
    sh["onesrf"] = np.ones((1, 128), f32)
    sh["onescb"] = np.ones((128, 1), BF16NP)
    sh["onesrb"] = np.ones((1, 128), BF16NP)
    return sh


def _prep_parity(cfg, h):
    T = cfg["T"]; CH = cfg["CHUNK"]
    TK = T // 128
    NB = TK // 2
    TQ = NB * 128
    D = 128
    i = np.arange(TQ)
    tok = 128 * (2 * (i // 128) + h) + (i % 128)
    cosT, sinT = _ref_rope_tables(cfg)
    cos = cosT[tok].T
    sin = sinT[tok].T
    out = {}
    out["cosQ"] = np.concatenate([cos, cos], axis=0).astype(np.float32)
    out["sinQ2"] = np.concatenate([sin, -sin], axis=0).astype(np.float32)
    sel = np.zeros((T, TQ), np.float32)
    sel[tok, np.arange(TQ)] = 1.0
    out["selq"] = np.ascontiguousarray(sel.reshape(TK, 128, TQ)).astype(BF16NP)
    masks = np.zeros((TK, 128, CH), np.uint8)
    for kt in range(TK):
        cc = kt // (2 * (CH // 128))
        tl = np.arange(CH)
        ql = CH * cc + tl
        g = 128 * (2 * (ql // 128) + h) + (ql % 128)
        srow = 128 * kt + np.arange(128)
        masks[kt] = (srow[:, None] > g[None, :]).astype(np.uint8)
    out["masks"] = masks
    return out


def _host_prep(cfg, inputs):
    C = cfg["C"]; T = cfg["T"]
    TK = T // 128
    NB = TK // 2
    TQ = NB * 128
    x = np.asarray(inputs["L0_attn_x"], np.float32)
    W = dict(
        Wq=np.asarray(inputs["L0_attn_Wq"], np.float32),
        Wsq=np.asarray(inputs["L0_attn_Wsq"], np.float32),
        Wk=np.asarray(inputs["L0_attn_Wk"], np.float32),
        Wsk=np.asarray(inputs["L0_attn_Wsk"], np.float32),
        Wv=np.asarray(inputs["L0_attn_Wv"], np.float32),
        Wsv=np.asarray(inputs["L0_attn_Wsv"], np.float32),
        Wf=np.asarray(inputs["L0_attn_Wf"], np.float32),
        bf=np.asarray(inputs["L0_attn_bf"], np.float32),
        Wo=np.asarray(inputs["L0_attn_Wo"], np.float32),
        wq_gate=np.asarray(inputs["L0_ffn_wq_gate"], np.int32),
        s_gate=np.asarray(inputs["L0_ffn_s_gate"], np.float32),
        wq_up=np.asarray(inputs["L0_ffn_wq_up"], np.int32),
        s_up=np.asarray(inputs["L0_ffn_s_up"], np.float32),
        wq_down=np.asarray(inputs["L0_ffn_wq_down"], np.int32),
        s_down=np.asarray(inputs["L0_ffn_s_down"], np.float32),
    )
    sh = _prep_shared(cfg, W)
    par = [_prep_parity(cfg, 0), _prep_parity(cfg, 1)]
    ncores = 2 * x.shape[0]
    in_maps = []
    own_tok = []
    for c in range(ncores):
        b, h = c // 2, c % 2
        xkT = np.ascontiguousarray(x[b].T)
        i = np.arange(TQ)
        tok = 128 * (2 * (i // 128) + h) + (i % 128)
        own_tok.append(tok)
        cols = np.empty((NB, 129), np.int64)
        for j in range(NB):
            g0 = 128 * (2 * j + h)
            cols[j] = np.arange(g0 - 1, g0 + 128)
        colsf = cols.reshape(-1)
        take = np.clip(colsf, 0, T - 1)
        xqT = xkT[:, take].copy()
        zm = colsf < 0
        if zm.any():
            xqT[:, zm] = 0.0
        m = dict(sh)
        m.update(par[h])
        m["xkT"] = xkT
        m["xqT"] = np.ascontiguousarray(xqT)
        m["xqresT"] = np.ascontiguousarray(xkT[:, tok])
        in_maps.append(m)
    return in_maps, own_tok, x.shape


def kernel(**inputs):
    cfg = full_cfg()
    in_maps, own_tok, xshape = _host_prep(cfg, inputs)
    nc = build_nc(cfg)
    res = run_bass_kernel_spmd(nc, in_maps, core_ids=list(range(len(in_maps))))
    B, T, C = xshape
    out = np.empty((B, T, C), np.float32)
    for c in range(len(in_maps)):
        b = c // 2
        outT = np.asarray(res.results[c]["outT"])
        out[b, own_tok[c], :] = outT.T
    return out


# version 22
# speedup vs baseline: 1.0557x; 1.0215x over previous
import numpy as np
import ml_dtypes
import concourse.bass as bass
import concourse.mybir as mybir
from concourse.bass_utils import run_bass_kernel_spmd

F32 = mybir.dt.float32
BF16 = mybir.dt.bfloat16
I32 = mybir.dt.int32
U8 = mybir.dt.uint8
AF = mybir.ActivationFunctionType
ALU = mybir.AluOpType
INVSQD = 1.0 / float(np.float32(np.sqrt(np.float64(128.0))))
BF16NP = ml_dtypes.bfloat16

ENGS = ("tensor", "vector", "scalar", "gpsimd", "sync")
NEG = -1.0e30



def _ref_rope_tables(cfg):
    T = cfg["T"]
    D = 128
    e = np.arange(0, D, 2).astype(np.float32) / np.float32(D)
    pw = np.power(np.float64(cfg["BASE"]), e.astype(np.float64)).astype(np.float32)
    inv = (np.float32(1.0) / pw).astype(np.float32)
    fr = (np.arange(T).astype(np.float32)[:, None] * inv[None, :]).astype(np.float32)
    try:
        import ctypes, ctypes.util
        libm = ctypes.CDLL(ctypes.util.find_library("m") or "libm.so.6")
        libm.cosf.restype = ctypes.c_float
        libm.cosf.argtypes = [ctypes.c_float]
        libm.sinf.restype = ctypes.c_float
        libm.sinf.argtypes = [ctypes.c_float]
        fl = fr.ravel()
        cos = np.fromiter((libm.cosf(ctypes.c_float(float(v))) for v in fl),
                          np.float32, fl.size).reshape(fr.shape)
        sin = np.fromiter((libm.sinf(ctypes.c_float(float(v))) for v in fl),
                          np.float32, fl.size).reshape(fr.shape)
        return cos, sin
    except Exception:
        return (np.cos(fr.astype(np.float64)).astype(np.float32),
                np.sin(fr.astype(np.float64)).astype(np.float32))


def full_cfg():
    return dict(C=2048, T=2048, H=16, HID=5632, CHUNK=512, MQ=30.0,
                HIDCH=4, EPS=1e-5, BASE=10000.0)


class Ref:
    __slots__ = ("eng", "idx")

    def __init__(self, eng, idx):
        self.eng = eng
        self.idx = idx


NRING = 12


class Em:

    def __init__(self):
        self.events = {e: [] for e in ENGS}
        self.ndma = {e: 0 for e in ENGS}
        self.dma_idx = {e: [] for e in ENGS}

    def ev(self, eng, fn, deps=(), dma=False):
        idx = len(self.events[eng])
        d = [r for r in deps if r is not None]
        slot = None
        if dma:
            slot = self.ndma[eng]
            self.ndma[eng] += 1
            self.dma_idx[eng].append(idx)
        self.events[eng].append((d, fn, dma, slot))
        return Ref(eng, idx)

    def dma(self, eng, out, in_, deps=()):
        def fn(E, out=out, in_=in_):
            return E.dma_start(out=out, in_=in_)
        return self.ev(eng, fn, deps, dma=True)

    def barrier(self):
        refs = [Ref(e, len(self.events[e]) - 1) for e in ENGS if self.events[e]]
        for e in ENGS:
            for i in self.dma_idx[e][-NRING:]:
                refs.append(Ref(e, i))
        return refs

    def finalize(self):
        marked = {e: set() for e in ENGS}
        for eng in ENGS:
            for deps, _, _, _ in self.events[eng]:
                for r in deps:
                    marked[r.eng].add(r.idx)
        for eng in ENGS:
            for i, (_, _, dma, _) in enumerate(self.events[eng]):
                if dma:
                    marked[eng].add(i)
        rank = {e: {} for e in ENGS}
        for eng in ENGS:
            v = 0
            for i in sorted(marked[eng]):
                if self.events[eng][i][2]:
                    continue
                v += 1
                rank[eng][i] = v
        self.rank = rank
        self.marked = marked

    def _dep_key(self, r):
        deps, fn, dma, slot = self.events[r.eng][r.idx]
        if dma:
            return (r.eng, slot % NRING), 16 * (slot // NRING + 1)
        return (r.eng, None), self.rank[r.eng][r.idx]

    def replay(self, eng, E, sems, rings):
        lastw = {}
        for i, (deps, fn, dma, slot) in enumerate(self.events[eng]):
            for r in deps:
                key, v = self._dep_key(r)
                assert v is not None, f"dep on unmarked {r.eng}:{r.idx}"
                if lastw.get(key, 0) >= v:
                    continue
                lastw[key] = v
                sem = sems[key[0]] if key[1] is None else rings[key[0]][key[1]]
                E.wait_ge(sem, v)
            if dma:
                ring_sem = rings[eng][slot % NRING]
                rnd = slot // NRING
                key = (eng, slot % NRING)
                if rnd > 0 and lastw.get(key, 0) < 16 * rnd:
                    lastw[key] = 16 * rnd
                    E.wait_ge(ring_sem, 16 * rnd)
                inst = fn(E)
                inst.then_inc(ring_sem, 16)
            else:
                inst = fn(E)
                if i in self.marked[eng]:
                    inst.then_inc(sems[eng], 1)


def align64(n):
    return ((n + 63) // 64) * 64


def build_nc(cfg):
    C = cfg["C"]; T = cfg["T"]; H = cfg["H"]; HID = cfg["HID"]
    CH = cfg["CHUNK"]; MQ = cfg["MQ"]; HIDCH = cfg["HIDCH"]; EPS = cfg["EPS"]
    CT = C // 128
    TK = T // 128
    NB = TK // 2
    TQ = NB * 128
    QB = NB * 129
    HT = HID // 128
    QTPC = CH // 128
    NCH = TQ // CH
    KTPC = 2 * QTPC
    NTC = T // CH
    NQC = 4 if QB % 4 == 0 else 2
    QCH = QB // NQC
    NHC = (HT + HIDCH - 1) // HIDCH
    HLF = 64
    assert C % 128 == 0 and T % 128 == 0 and TQ % CH == 0 and QB % NQC == 0
    assert NTC <= 4
    assert (T.bit_length() - 1) % 2 == 1, "need odd Hillis-Steele step count"

    nc = bass.Bass("TRN2", target_bir_lowering=False)

    dr = {}

    def din(name, shape, dt):
        dr[name] = nc.dram_tensor(name, list(shape), dt, kind="ExternalInput").ap()

    def dint(name, shape, dt):
        dr[name] = nc.dram_tensor(name, list(shape), dt, kind="Internal").ap()

    din("xkT", (C, T), F32)
    din("xqT", (C, QB), F32)
    din("xqresT", (C, TQ), F32)
    din("wqT", (H, C, 128), F32)
    din("wkT", (H, C, 128), F32)
    din("wvT", (H, C, 128), F32)
    din("wsqT", (CT, 128, H), F32)
    din("wspT", (CT, 128, 3 * H), F32)
    din("bfpack", (3 * H, 1), F32)
    din("woT4", (CT, CT, 128, 128), F32)
    din("wgT", (HT, C, 128), I32)
    din("wuT", (HT, C, 128), I32)
    din("wdT", (HT, 128, C), I32)
    din("sg", (128, HT), F32)
    din("sup", (128, HT), F32)
    din("sdn", (128, CT), F32)
    din("cosK", (128, T), F32)
    din("sinK2", (128, T), F32)
    din("cosQ", (128, TQ), F32)
    din("sinQ2", (128, TQ), F32)
    din("selq", (TK, 128, TQ), BF16)
    din("masks", (TK, 128, CH), U8)
    din("ident", (128, 128), F32)
    din("onescf", (128, 1), F32)
    din("onesrf", (1, 128), F32)
    din("onescb", (128, 1), BF16)
    din("onesrb", (1, 128), BF16)

    outT = nc.dram_tensor("outT", [C, TQ], F32, kind="ExternalOutput").ap()

    dint("xl_sp", (CT, 128, T), BF16)
    dint("xnqh_sp", (CT, 128, TQ), BF16)
    dint("xnql_sp", (CT, 128, TQ), BF16)
    dint("qh_sp", (H, 128, TQ), BF16)
    dint("ql_sp", (H, 128, TQ), BF16)
    dint("oh_sp", (H, 128, TQ), BF16)
    dint("ol_sp", (H, 128, TQ), BF16)

    S = {}
    OFF = {}
    _probe = nc.alloc_sbuf_tensor("_base_probe", [128, 16], mybir.dt.uint8)
    BASE0 = nc.lookup_mloc(_probe).addr + 64
    LIMIT = nc.SBUF_PARTITION_SIZE_BYTES
    cur = [BASE0]

    def sb(name, shape, dt, at=None):
        nbytes = int(np.prod(shape[1:])) * mybir.dt.size(dt)
        if at is None:
            at = cur[0]
            cur[0] = at + align64(nbytes)
        S[name] = nc.alloc_sbuf_tensor_at(name, list(shape), dt, offset=at)
        OFF[name] = at
        return at + align64(nbytes)

    sb("ident", (128, 128), F32)
    sb("onescf", (128, 1), F32)
    sb("onesrf", (1, 128), F32)
    sb("onescb", (128, 1), BF16)
    sb("onesrb", (1, 128), BF16)
    sb("bfpack", (3 * H, 1), F32)
    sb("sdn", (128, CT), F32)
    sb("sg", (128, HT), F32)
    sb("sup", (128, HT), F32)
    B_BASE = cur[0]

    sb("negcumT", (128, H * TK), F32)
    sb("cumbf", (128, H * TK), BF16)
    sb("biasQ", (H, TQ), BF16)
    sb("wsp_f", (128, CT * 3 * H), F32)
    sb("wsp_h", (128, CT * 3 * H), BF16)
    sb("wsp_l", (128, CT * 3 * H), BF16)
    sb("gates", (3 * H, T), F32)
    sb("cosK", (128, T), F32)
    sb("sinK2", (128, T), F32)
    sb("masks", (128, TK * CH), U8)
    sb("rowbuf", (1, max(T, 4 * QCH, 4 * CH)), F32)
    sb("bhrow", (1, 2 * TQ), BF16)
    R1 = cur[0]
    sb("cosQ", (128, TQ), F32)
    sb("sinQ2", (128, TQ), F32)
    R1_END = cur[0]
    e_off = sb("ef", (128, 2 * CH), F32, at=R1)
    e_off = sb("eh", (128, 2 * CH), BF16, at=e_off)
    e_off = sb("el", (128, 2 * CH), BF16, at=e_off)
    assert e_off <= R1_END
    A2_BASE = cur[0]
    sb("xnh", (128, CT * T), BF16)
    sb("xlp", (128, 2 * T), BF16)
    sb("stage", (128, T), F32)
    sb("wv_h", (128, CT * 128), BF16)
    sb("wv_l", (128, CT * 128), BF16)
    sb("wk_h", (128, CT * 128), BF16)
    sb("wk_l", (128, CT * 128), BF16)
    sb("vtm_h", (128, TK * 128), BF16)
    sb("vtm_l", (128, TK * 128), BF16)
    sb("y_k", (128, T + 4), F32)
    sb("dK", (128, T + 4), F32)
    sb("K_h", (128, T), BF16)
    sb("K_l", (128, T), BF16)
    sb("qt_h", (128, TQ), BF16)
    sb("qt_l", (128, TQ), BF16)
    sb("o_h", (128, 2 * CH), BF16)
    sb("o_l", (128, 2 * CH), BF16)
    sb("invlbc", (128, CH), F32)
    sb("krotb", (128, CH), F32)
    sb("rch2", (128, CH), F32)
    sb("invlrow", (1, CH), F32)
    A_END = cur[0]
    nc._dbg_a_end = (A_END, LIMIT)
    assert A_END <= LIMIT, f"A-phase SBUF overflow: {A_END}"

    sb("cumA", (H, T), F32, at=OFF["vtm_h"])
    sb("cumB", (H, T), F32, at=OFF["K_h"])
    sb("selp", (128, 2 * TQ), BF16, at=OFF["o_h"])
    sb("rbc", (128, T), F32, at=OFF["vtm_h"])
    sb("mrbc", (128, T), F32, at=OFF["y_k"])
    sb("x2b", (128, T), F32, at=OFF["dK"])
    sb("lotile", (128, 2 * T), BF16, at=OFF["K_h"])
    sb("stage2", (128, T), F32, at=R1)

    p = OFF["xnh"]
    p = sb("xq_st", (128, 2 * QB), F32, at=p)
    p = sb("y_q", (128, QB), F32, at=p)
    p = sb("xnq_h", (128, CT * QB), BF16, at=p)
    p = sb("xnq_l", (128, CT * QB), BF16, at=p)
    p = sb("dQ", (128, TQ), F32, at=p)
    p = sb("qrope", (128, TQ), F32, at=p)
    p = sb("rbcq", (128, QB), F32, at=p)
    p = sb("mrbcq", (128, QB), F32, at=p)
    p = sb("wq_st", (128, CT * 128), F32, at=p)
    p = sb("wq_h", (128, CT * 128), BF16, at=p)
    p = sb("wq_l", (128, CT * 128), BF16, at=p)
    p = sb("wsq_f", (128, CT * H), F32, at=p)
    p = sb("wsq_h", (128, CT * H), BF16, at=p)
    p = sb("wsq_l", (128, CT * H), BF16, at=p)
    p = sb("pq", (H, QB), F32, at=p)
    p = sb("pqrow", (1, TQ), F32, at=p)
    p = sb("rch", (128, CH), F32, at=p)
    assert p <= LIMIT, f"A1 SBUF overflow: {p}"

    p = B_BASE
    p = sb("bxnq_h", (128, CT * TQ), BF16, at=p)
    p = sb("bxnq_l", (128, CT * TQ), BF16, at=p)
    p = sb("g_acc", (128, CT * TQ), F32, at=p)
    B_AFTER_G = p
    p = sb("su_h", (128, HIDCH * TQ), BF16, at=p)
    p = sb("su_l", (128, HIDCH * TQ), BF16, at=p)
    p = sb("wint", (128, CT * 128), I32, at=p)
    p = sb("wbf", (128, 4 * CT * 128), BF16, at=p)
    p = sb("wdc", (128, HIDCH * CT * 128), BF16, at=p)
    p = sb("su_f", (128, CH), F32, at=p)
    p = sb("gsil", (128, 2 * CH), F32, at=p)
    assert p <= LIMIT, f"B1 SBUF overflow: {p}"
    p = sb("ob_h", (128, H * TQ), BF16, at=B_BASE)
    p = sb("ob_l", (128, H * TQ), BF16, at=p)
    assert p <= B_AFTER_G - (B_AFTER_G - B_BASE) + (B_AFTER_G - B_BASE), "ob overlay"
    p = B_AFTER_G
    p = sb("wo_h", (128, 2 * CT * 128), BF16, at=p)
    p = sb("wo_l", (128, 2 * CT * 128), BF16, at=p)
    p = sb("wo_f", (128, 2 * CT * 128), F32, at=p)
    p = sb("xres", (128, 2 * CH), F32, at=p)
    p = sb("outsb", (128, 2 * CH), F32, at=p)
    p = sb("gsil2", (128, 2 * CH), F32, at=p)
    assert p <= LIMIT, f"B2 SBUF overflow: {p}"

    PS = []
    for i in range(8):
        PS.append(nc.alloc_psum_tensor(f"bank{i}", [128, max(CH, 512)], F32))

    E = Em()

    def mm(out, lhsT, rhs, start, stop, deps=()):
        return E.ev("tensor", (lambda Eng, out=out, lhsT=lhsT, rhs=rhs,
                               start=start, stop=stop:
                               Eng.matmul(out, lhsT, rhs, start=start, stop=stop)), deps)

    def tr(out, in_, ident, deps=()):
        return E.ev("tensor", (lambda Eng, out=out, in_=in_, ident=ident:
                               Eng.transpose(out, in_, ident)), deps)

    def vop(fn, deps=()):
        return E.ev("vector", fn, deps)

    def sop(fn, deps=()):
        return E.ev("scalar", fn, deps)

    def gop(fn, deps=()):
        return E.ev("gpsimd", fn, deps)

    ld = {}
    for nm in ("ident", "onescf", "onesrf", "onescb", "onesrb", "bfpack",
               "sdn", "sg", "sup", "cosK", "sinK2", "cosQ", "sinQ2"):
        ld[nm] = E.dma("sync", S[nm][:], dr[nm])
    ld["masks"] = E.dma("sync", S["masks"][:].rearrange("p (a d) -> p a d", d=CH),
                        dr["masks"].rearrange("a p d -> p a d"))
    ld["wsp"] = E.dma("sync", S["wsp_f"][:].rearrange("p (a d) -> p a d", d=3 * H),
                      dr["wspT"].rearrange("a p d -> p a d"))
    ld["wsq"] = E.dma("sync", S["wsq_f"][:].rearrange("p (a d) -> p a d", d=H),
                      dr["wsqT"].rearrange("a p d -> p a d"))
    wsp_h = vop(lambda V: V.tensor_copy(S["wsp_h"][:], S["wsp_f"][:]), [ld["wsp"]])
    wsp_l = vop(lambda V: V.tensor_tensor(S["wsp_l"][:], S["wsp_f"][:], S["wsp_h"][:], ALU.subtract), [wsp_h])
    wsq_h = vop(lambda V: V.tensor_copy(S["wsq_h"][:], S["wsq_f"][:]), [ld["wsq"]])
    wsq_l = vop(lambda V: V.tensor_tensor(S["wsq_l"][:], S["wsq_f"][:], S["wsq_h"][:], ALU.subtract), [wsq_h])

    def ln_stream(src_dr, ncols, nchunk, csz, stages, x2a, t2as, rbca, mrbca,
                  rows, hi_dst, lo_sink, deps0):
        rm = rows
        L = len(stages)
        stat_last = []
        x2_prev = []
        slotdep = {}
        for ct in range(CT):
            stage = stages[ct % L]
            d = E.dma("sync", stage, src_dr[128 * ct:128 * (ct + 1), :],
                      deps=(list(deps0) if ct < L else []) + slotdep.get(ct - L, []))
            x2 = vop(lambda V, stage=stage: V.tensor_tensor(x2a, stage, stage, ALU.mult),
                     [d] + stat_last[-1:] + x2_prev)
            x2_prev = [x2]
            ms = []
            for q in range(nchunk):
                sl = slice(csz * q, csz * (q + 1))
                ms.append(mm(PS[q][0:1, 0:csz], S["onescf"][:], stage[:, sl],
                             start=(ct == 0), stop=(ct == CT - 1), deps=[d]))
                ms.append(mm(PS[4 + q][0:1, 0:csz], S["onescf"][:], x2a[:, sl],
                             start=(ct == 0), stop=(ct == CT - 1), deps=[x2]))
            stat_last = ms
            slotdep[ct] = [ms[-2], ms[-1], x2]
        bcd = []
        prev_mm = []
        for q in range(nchunk):
            sl = slice(csz * q, csz * (q + 1))
            s0 = rm[:, 0:csz]
            s1 = rm[:, csz:2 * csz]
            s2 = rm[:, 2 * csz:3 * csz]
            s3 = rm[:, 3 * csz:4 * csz]
            e1 = vop(lambda V, q=q, s0=s0: V.tensor_scalar_mul(s0, PS[q][0:1, 0:csz], 1.0 / C), stat_last + prev_mm)
            e2 = vop(lambda V, q=q, s1=s1: V.tensor_scalar_mul(s1, PS[4 + q][0:1, 0:csz], 1.0 / C), stat_last + prev_mm)
            a = vop(lambda V, s0=s0, s3=s3: V.tensor_tensor(s3, s0, s0, ALU.mult), [e1])
            a = vop(lambda V, s1=s1, s3=s3: V.tensor_tensor(s1, s1, s3, ALU.subtract), [e2, a])
            a = vop(lambda V, s1=s1, s3=s3: V.tensor_scalar_add(s3, s1, float(EPS)), [a])
            a = sop(lambda Sc, s3=s3: Sc.activation(s3, s3, AF.Sqrt), [a])
            a = vop(lambda V, s1=s1, s3=s3: V.reciprocal(s1, s3), [a])
            b = vop(lambda V, s0=s0, s1=s1, s2=s2: V.tensor_tensor(s2, s0, s1, ALU.mult), [a])
            m1 = mm(PS[q][:, 0:csz], S["onesrf"][:], s1, start=True, stop=True, deps=[a, e1] + bcd)
            v1 = vop(lambda V, q=q, sl=sl: V.tensor_copy(rbca[:, sl], PS[q][:, 0:csz]), [m1])
            m2 = mm(PS[4 + q][:, 0:csz], S["onesrf"][:], s2, start=True, stop=True, deps=[b] + bcd)
            v2 = vop(lambda V, q=q, sl=sl: V.tensor_copy(mrbca[:, sl], PS[4 + q][:, 0:csz]), [m2])
            bcd += [v1, v2]
            prev_mm = [m1, m2]
        bcd = bcd[-2 * nchunk:]
        tiles = []
        prevv = []
        slot2 = {}
        for ct in range(CT):
            stage = stages[ct % L]
            t2a = t2as[ct % L]
            d = E.dma("gpsimd", stage, src_dr[128 * ct:128 * (ct + 1), :],
                      deps=(bcd if ct < L else []) + slot2.get(ct - L, []))
            t1 = vop(lambda V, stage=stage: V.tensor_tensor(x2a, stage, rbca, ALU.mult), [d] + prevv)
            hd = hi_dst(ct)
            hi = vop(lambda V, hd=hd: V.tensor_tensor(hd, x2a, mrbca, ALU.subtract), [t1])
            t2 = vop(lambda V, t2a=t2a: V.tensor_tensor(t2a, x2a, mrbca, ALU.subtract), [t1])
            refs = lo_sink(ct, hi, t2, t2a)
            prevv = [t2, hi] + refs
            slot2[ct] = [t1, t2] + refs
            tiles.append((hi, refs))
        return tiles

    a1_rows = S["rowbuf"][0:1, 0:4 * QCH]
    stage1 = S["xq_st"][:, 0:QB]
    t2a1 = S["xq_st"][:, QB:2 * QB]

    def a1_hi(ct):
        return S["xnq_h"][:, ct * QB:(ct + 1) * QB]

    def a1_lo(ct, hi, t2, t2buf):
        lo = vop(lambda V, ct=ct, t2buf=t2buf: V.tensor_tensor(
            S["xnq_l"][:, ct * QB:(ct + 1) * QB], t2buf,
            S["xnq_h"][:, ct * QB:(ct + 1) * QB], ALU.subtract), [t2, hi])
        return [lo]

    a1_tiles = ln_stream(dr["xqT"], QB, NQC, QCH, [stage1], S["y_q"][:, 0:QB],
                         [t2a1], S["rbcq"][:], S["mrbcq"][:], a1_rows, a1_hi, a1_lo, [])

    pq_ev = None
    for q in range(NQC):
        last = None
        for ct in range(CT):
            wsl = slice(ct * H, (ct + 1) * H)
            xsl = slice(ct * QB + QCH * q, ct * QB + QCH * (q + 1))
            d0 = [wsq_h, a1_tiles[ct][0]] + ([pq_ev] if pq_ev else [])
            mm(PS[7][0:H, 0:QCH], S["wsq_h"][:, wsl], S["xnq_h"][:, xsl],
               start=(ct == 0), stop=False, deps=d0)
            mm(PS[7][0:H, 0:QCH], S["wsq_h"][:, wsl], S["xnq_l"][:, xsl],
               start=False, stop=False, deps=[a1_tiles[ct][1][0]])
            last = mm(PS[7][0:H, 0:QCH], S["wsq_l"][:, wsl], S["xnq_h"][:, xsl],
                      start=False, stop=(ct == CT - 1), deps=[wsq_l])
        pq_ev = sop(lambda Sc, q=q: Sc.activation(S["pq"][:, QCH * q:QCH * (q + 1)],
                                                  PS[7][0:H, 0:QCH], AF.Sigmoid), [last])

    QV = S["y_q"][:].rearrange("p (b c) -> p b c", c=129)
    DQV = S["dQ"][:].rearrange("p (b c) -> p b c", c=128)
    q_spills = []
    wq_free = []
    vd_prev = None
    vrow_d = {}
    km_prev = {}
    rope_prev = []
    evy_hist = {}
    for h in range(H):
        dW = E.dma("sync", S["wq_st"][:].rearrange("p (a d) -> p a d", d=128),
                   dr["wqT"][h].rearrange("(a p) d -> p a d", p=128), deps=wq_free)
        wh = vop(lambda V: V.tensor_copy(S["wq_h"][:], S["wq_st"][:]), [dW])
        wl = vop(lambda V: V.tensor_tensor(S["wq_l"][:], S["wq_st"][:], S["wq_h"][:], ALU.subtract), [wh])
        wq_free = [wl]
        PQV = S["pq"][:].rearrange("p (b c) -> p b c", c=129)
        vrow = E.dma("scalar", S["pqrow"][:].rearrange("p (b c) -> p b c", c=128),
                     PQV[h:h + 1, :, 1:129], deps=[pq_ev] + km_prev.get(h - 1, []))
        ev_y = []
        for q in range(NQC):
            pp = PS[4 + q]
            last = None
            for ct in range(CT):
                wsl = slice(ct * 128, (ct + 1) * 128)
                xsl = slice(ct * QB + QCH * q, ct * QB + QCH * (q + 1))
                d0 = [wh] + (evy_hist.get((h - 1, q), []) if ct == 0 else [])
                if ct == 0 and h == 0:
                    d0.append(pq_ev)
                if ct == 0 and q == 0 and vd_prev is not None:
                    d0.append(vd_prev)
                mm(pp[:, 0:QCH], S["wq_h"][:, wsl], S["xnq_h"][:, xsl], start=(ct == 0), stop=False, deps=d0)
                mm(pp[:, 0:QCH], S["wq_h"][:, wsl], S["xnq_l"][:, xsl], start=False, stop=False)
                last = mm(pp[:, 0:QCH], S["wq_l"][:, wsl], S["xnq_h"][:, xsl], start=False, stop=(ct == CT - 1), deps=[wl])
            ev_y.append(vop(lambda V, q=q, pp=pp: V.tensor_scalar_mul(S["y_q"][:, QCH * q:QCH * (q + 1)], pp[:, 0:QCH], INVSQD),
                            [last] + ([vd_prev] if vd_prev is not None else [])))
            evy_hist[(h, q)] = [ev_y[-1]]
        bcms = {}
        for q2 in range(NCH):
            sl = slice(CH * q2, CH * (q2 + 1))
            bcms[q2] = mm(PS[q2 % 2][:, 0:CH], S["onesrf"][:], S["pqrow"][0:1, sl], start=True, stop=True,
                          deps=[vrow] + km_prev.get((h - 1, q2), []))
        vd = vop(lambda V: V.tensor_tensor(DQV, QV[:, :, 1:129], QV[:, :, 0:128], ALU.subtract), ev_y)
        vd_prev = vd
        ev_q = []
        kms = []
        for q2 in range(NCH):
            sl = slice(CH * q2, CH * (q2 + 1))
            km = vop(lambda V, sl=sl, q2=q2: V.tensor_tensor(S["rch"][:, 0:CH], PS[q2 % 2][:, 0:CH], S["dQ"][:, sl], ALU.mult),
                     [bcms[q2], vd] + ev_q[-1:])
            km_prev[(h, q2)] = [km]
            nblk = CH // 128
            yo = QV[:, q2 * nblk:(q2 + 1) * nblk, 1:129]
            ev_q.append(vop(lambda V, sl=sl, yo=yo: V.tensor_tensor(
                S["qrope"][:, sl].rearrange("p (b c) -> p b c", c=128), yo,
                S["rch"][:, 0:CH].rearrange("p (b c) -> p b c", c=128), ALU.subtract), [km]))
            kms.append(km)
        km_prev[h] = kms
        ev_sp = []
        for q2 in range(NCH):
            sl = slice(CH * q2, CH * (q2 + 1))
            c1 = vop(lambda V, sl=sl: V.tensor_copy(S["krotb"][0:HLF, 0:CH], S["qrope"][:, sl][HLF:128, :]),
                     ev_q + ev_sp[-1:])
            c2 = vop(lambda V, sl=sl: V.tensor_copy(S["krotb"][HLF:128, 0:CH], S["qrope"][:, sl][0:HLF, :]), [c1])
            t1 = vop(lambda V, sl=sl: V.tensor_tensor(S["rch"][:, 0:CH], S["qrope"][:, sl], S["cosQ"][:, sl], ALU.mult),
                     [c2, ld["cosQ"]])
            t2 = vop(lambda V, sl=sl: V.tensor_tensor(S["krotb"][:, 0:CH], S["krotb"][:, 0:CH], S["sinQ2"][:, sl], ALU.mult),
                     [t1, ld["sinQ2"]])
            hi = vop(lambda V, sl=sl: V.tensor_tensor(S["qt_h"][:, sl], S["rch"][:, 0:CH], S["krotb"][:, 0:CH], ALU.add),
                     [t2] + (q_spills if q2 == 0 else []))
            t3 = vop(lambda V, sl=sl: V.tensor_tensor(S["rch"][:, 0:CH], S["rch"][:, 0:CH], S["krotb"][:, 0:CH], ALU.add), [hi])
            lo = vop(lambda V, sl=sl: V.tensor_tensor(S["qt_l"][:, sl], S["rch"][:, 0:CH], S["qt_h"][:, sl], ALU.subtract), [t3])
            ev_sp += [lo, hi]
        s1 = E.dma("scalar", dr["qh_sp"][h], S["qt_h"][:], deps=ev_sp + q_spills[-2:])
        s2 = E.dma("scalar", dr["ql_sp"][h], S["qt_l"][:], deps=ev_sp)
        q_spills = [s1, s2]
    xnq_sp = []
    for ct in range(CT):
        for nm, spn in (("xnq_h", "xnqh_sp"), ("xnq_l", "xnql_sp")):
            src = S[nm][:, ct * QB:(ct + 1) * QB].rearrange("p (b c) -> p b c", c=129)[:, :, 1:129]
            xnq_sp.append(E.dma("scalar", dr[spn][ct].rearrange("p (b c) -> p b c", c=128), src,
                                deps=q_spills if ct == 0 else []))
    A1_END = E.barrier()

    a2_rows = S["rowbuf"][0:1, 0:4 * CH]
    gate_last = []
    xl_sp_refs = []

    def a2_hi(ct):
        return S["xnh"][:, ct * T:(ct + 1) * T]

    def a2_lo(ct, hi, t2, t2buf):
        lob = S["lotile"][:, (ct % 2) * T:(ct % 2) * T + T]
        lo = vop(lambda V, lob=lob, ct=ct, t2buf=t2buf: V.tensor_tensor(
            lob, t2buf, S["xnh"][:, ct * T:(ct + 1) * T], ALU.subtract),
            [t2, hi] + (xl_sp_refs[-2:-1] if len(xl_sp_refs) >= 2 else []))
        spr = E.dma("scalar", dr["xl_sp"][ct], lob, deps=[lo])
        xl_sp_refs.append(spr)
        gm = []
        for q in range(NTC):
            sl = slice(CH * q, CH * (q + 1))
            wsl = slice(ct * 3 * H, (ct + 1) * 3 * H)
            gm.append(mm(PS[q][0:3 * H, 0:CH], S["wsp_h"][:, wsl],
                         S["xnh"][:, ct * T + CH * q:ct * T + CH * (q + 1)],
                         start=(ct == 0), stop=False, deps=[hi, wsp_h]))
            gm.append(mm(PS[q][0:3 * H, 0:CH], S["wsp_h"][:, wsl], lob[:, sl],
                         start=False, stop=False, deps=[lo]))
            gm.append(mm(PS[q][0:3 * H, 0:CH], S["wsp_l"][:, wsl],
                         S["xnh"][:, ct * T + CH * q:ct * T + CH * (q + 1)],
                         start=False, stop=(ct == CT - 1), deps=[wsp_l]))
        gate_last[:] = gm
        return [lo, spr, gm[-1]]

    ln_stream(dr["xkT"], T, NTC, CH, [S["stage"][:], S["stage2"][:]], S["x2b"][:],
              [S["stage"][:], S["stage2"][:]],
              S["rbc"][:], S["mrbc"][:], a2_rows, a2_hi, a2_lo, A1_END)
    gate_sig = []
    for q in range(NTC):
        gate_sig.append(sop(lambda Sc, q=q: Sc.activation(
            S["gates"][:, CH * q:CH * (q + 1)], PS[q][0:3 * H, 0:CH],
            AF.Sigmoid, bias=S["bfpack"][:], scale=1.0), gate_last + [ld["bfpack"]]))
    ln_ev = sop(lambda Sc: Sc.activation(S["gates"][0:H, :], S["gates"][0:H, :], AF.Ln), gate_sig)
    cpy = vop(lambda V: V.tensor_copy(S["cumB"][:], S["gates"][0:H, :]), [ln_ev, xl_sp_refs[-1]])
    cur_row = 1
    hs = cpy
    sh = 1
    while sh < T:
        srcb = S["cumB"][:] if cur_row == 1 else S["cumA"][:]
        dstb = S["cumA"][:] if cur_row == 1 else S["cumB"][:]
        h1 = vop(lambda V, srcb=srcb, dstb=dstb, sh=sh: V.tensor_tensor(
            dstb[:, sh:T], srcb[:, sh:T], srcb[:, 0:T - sh], ALU.add), [hs])
        hs = vop(lambda V, srcb=srcb, dstb=dstb, sh=sh: V.tensor_copy(dstb[:, 0:sh], srcb[:, 0:sh]), [h1])
        cur_row = 1 - cur_row
        sh *= 2
    assert cur_row == 0
    CUMR = S["cumA"][:]
    nct = [hs]
    for kt in range(TK):
        t0 = tr(PS[7][:, 0:H], CUMR[:, 128 * kt:128 * (kt + 1)], S["ident"][0:H, 0:H],
                deps=[hs] + nct[-2:])
        v1 = vop(lambda V, kt=kt: V.tensor_scalar_mul(S["negcumT"][:, H * kt:H * (kt + 1)], PS[7][:, 0:H], -1.0), [t0])
        v2 = vop(lambda V, kt=kt: V.tensor_copy(S["cumbf"][:, H * kt:H * (kt + 1)], PS[7][:, 0:H]), [t0])
        nct += [v1, v2]
    bq = []
    seld_hist = []
    for q2 in range(NCH):
        last = None
        for kt in range(TK):
            buf = kt % 2
            dsel = E.dma("sync", S["selp"][:, buf * TQ:buf * TQ + TQ], dr["selq"][kt],
                         deps=(seld_hist[-2:-1] if len(seld_hist) >= 2 else []))
            last = mm(PS[6][0:H, 0:CH], S["cumbf"][:, H * kt:H * (kt + 1)],
                      S["selp"][:, buf * TQ + CH * q2:buf * TQ + CH * (q2 + 1)],
                      start=(kt == 0), stop=(kt == TK - 1), deps=[dsel] + nct[-2:] + bq)
            seld_hist.append(last)
        bq = [vop(lambda V, q2=q2: V.tensor_scalar_add(
            S["biasQ"][:, CH * q2:CH * (q2 + 1)], PS[6][0:H, 0:CH], -float(MQ)), [last])]
    biasQ_done = bq[0]

    assert NCH == 2 and NTC == 4
    zcol = vop(lambda V: V.memset(S["y_k"][:, 0:1], 0.0), [biasQ_done])
    zcol2 = vop(lambda V: V.memset(S["dK"][:, 0:1], 0.0), [zcol])
    o_spills = []
    st = {}
    st["sc_last"] = [biasQ_done]
    st["sbank"] = {}
    st["pbank"] = {}
    st["obank"] = []
    st["lbank"] = []
    st["wfree"] = [zcol2]
    st["stagefree"] = [biasQ_done]
    st["ykfree"] = {}
    st["ropece"] = {}
    st["vgate"] = {}
    st["kgate"] = {}
    st["pvbc"] = {}
    st["pkbc"] = {}
    st["E"] = {}
    st["qld"] = {}
    st["bh"] = {}
    st["oev"] = []
    st["elast"] = {}
    st["xl_rd"] = {}
    st["trs"] = {}
    st["xln"] = 0

    def emit_wq_loads(h):
        dWv = E.dma("sync", S["stage"][:, 0:CT * 128].rearrange("p (a d) -> p a d", d=128),
                    dr["wvT"][h].rearrange("(a p) d -> p a d", p=128), deps=st["stagefree"])
        wvh = vop(lambda V: V.tensor_copy(S["wv_h"][:], S["stage"][:, 0:CT * 128]), [dWv] + st["wfree"])
        wvl = vop(lambda V: V.tensor_tensor(S["wv_l"][:], S["stage"][:, 0:CT * 128], S["wv_h"][:], ALU.subtract), [wvh])
        dWk = E.dma("sync", S["stage"][:, 0:CT * 128].rearrange("p (a d) -> p a d", d=128),
                    dr["wkT"][h].rearrange("(a p) d -> p a d", p=128), deps=[wvl])
        wkh = vop(lambda V: V.tensor_copy(S["wk_h"][:], S["stage"][:, 0:CT * 128]), [dWk])
        wkl = vop(lambda V: V.tensor_tensor(S["wk_l"][:], S["stage"][:, 0:CT * 128], S["wk_h"][:], ALU.subtract), [wkh])
        st["stagefree"] = [wkl]
        st.setdefault("W", {})[h] = (wvh, wvl, wkh, wkl)
        par = h % 2
        bh = E.dma("scalar", S["bhrow"][:, par * TQ:(par + 1) * TQ], S["biasQ"][h:h + 1, :],
                   deps=[biasQ_done] + st["sc_last"])
        st["bh"][h] = [bh]

    def emit_qload(h):
        dQh = E.dma("scalar", S["qt_h"][:], dr["qh_sp"][h], deps=st["sc_last"] + ([q_spills[0]] if h == 0 else []))
        dQl = E.dma("scalar", S["qt_l"][:], dr["ql_sp"][h], deps=([q_spills[1]] if h == 0 else []))
        st["qld"][h] = [dQh, dQl]

    def _proj_ct(h, qp, ct):
        wvh, wvl, wkh, wkl = st["W"][h]
        W2 = 2 * CH
        li = st["xln"]
        st["xln"] = li + 1
        buf = li % 4
        xlb = S["xlp"][:, buf * W2:(buf + 1) * W2]
        dxl = E.dma("sync", xlb, dr["xl_sp"][ct][:, qp * W2:(qp + 1) * W2],
                    deps=[xl_sp_refs[ct]] + st["xl_rd"].get(li - 4, []))
        wsl = slice(ct * 128, (ct + 1) * 128)
        refs = []
        for qq in range(2):
            q = 2 * qp + qq
            xsl = slice(ct * T + CH * q, ct * T + CH * (q + 1))
            lsl = slice(CH * qq, CH * (qq + 1))
            d0v = [wvh] + (st["pbank"].pop(4 + qq, []) if ct == 0 else [])
            d0k = [wkh] + (st["pbank"].pop(6 + qq, []) if ct == 0 else [])
            mm(PS[4 + qq][:, 0:CH], S["wv_h"][:, wsl], S["xnh"][:, xsl], start=(ct == 0), stop=False, deps=d0v)
            mm(PS[4 + qq][:, 0:CH], S["wv_l"][:, wsl], S["xnh"][:, xsl], start=False, stop=False, deps=[wvl])
            r1 = mm(PS[4 + qq][:, 0:CH], S["wv_h"][:, wsl], xlb[:, lsl], start=False, stop=(ct == CT - 1), deps=[dxl])
            mm(PS[6 + qq][:, 0:CH], S["wk_h"][:, wsl], S["xnh"][:, xsl], start=(ct == 0), stop=False, deps=d0k)
            mm(PS[6 + qq][:, 0:CH], S["wk_l"][:, wsl], S["xnh"][:, xsl], start=False, stop=False, deps=[wkl])
            r2 = mm(PS[6 + qq][:, 0:CH], S["wk_h"][:, wsl], xlb[:, lsl], start=False, stop=(ct == CT - 1), deps=[dxl])
            refs += [r1, r2]
        st["xl_rd"][li] = refs

    def _proj_evacs(h, qp):
        vev = []
        kev = []
        for qq in range(2):
            q = 2 * qp + qq
            lastr = st["xl_rd"][st["xln"] - 1]
            yw = st["ykfree"].pop((h - 1, q), [])
            ve = vop(lambda V, q=q: V.tensor_copy(S["y_k"][:, 1 + CH * q:1 + CH * (q + 1)], PS[4 + (q % 2)][:, 0:CH]),
                     lastr + yw + [zcol])
            ke = vop(lambda V, q=q: V.tensor_copy(S["dK"][:, 1 + CH * q:1 + CH * (q + 1)], PS[6 + (q % 2)][:, 0:CH]),
                     lastr + st["ropece"].get(h - 1, []) + [zcol2])
            vev.append(ve)
            kev.append(ke)
            st["pbank"][4 + qq] = [ve]
            st["pbank"][6 + qq] = [ke]
        st[("vev", h, qp)] = vev
        st[("kev", h, qp)] = kev

    def _pvbc(h):
        refs = {}
        mms = []
        for q in range(NTC):
            sl = slice(CH * q, CH * (q + 1))
            dep0 = st["grow_ld"] + st["pbank"].pop(7 - q, [])
            bm = mm(PS[7 - q][:, 0:CH], S["onesrf"][:], S["rowbuf"][0:1, sl], start=True, stop=True, deps=dep0)
            refs[q] = bm
            mms.append(bm)
        st["pvbc"][h] = refs
        st["grow_free"] = mms

    def make_proj_closures(h):
        cl = []
        for qp in range(2):
            for ct in range(CT):
                cl.append(lambda qp=qp, ct=ct: _proj_ct(h, qp, ct))
            cl.append(lambda qp=qp: _proj_evacs(h, qp))
        cl.append(lambda: _pvbc(h))
        cl.append(lambda: emit_growload(h, "k"))
        return cl

    def emit_growload(h, which):
        row = (2 * H + h) if which == "v" else (H + h)
        gv = E.dma("scalar", S["rowbuf"][0:1, 0:T], S["gates"][row:row + 1, :],
                   deps=gate_sig + st.get("grow_free", []))
        st["grow_ld"] = [gv]
    def emit_growload(h, which):
        row = (2 * H + h) if which == "v" else (H + h)
        gv = E.dma("scalar", S["rowbuf"][0:1, 0:T], S["gates"][row:row + 1, :],
                   deps=gate_sig + st.get("grow_free", []))
        st["grow_ld"] = [gv]

    def emit_pvbc(h):
        refs = {}
        mms = []
        for q in range(NTC):
            sl = slice(CH * q, CH * (q + 1))
            dep0 = st["grow_ld"] + st["pbank"].pop(7 - q, [])
            bm = mm(PS[7 - q][:, 0:CH], S["onesrf"][:], S["rowbuf"][0:1, sl], start=True, stop=True, deps=dep0)
            refs[q] = bm
            mms.append(bm)
        st["pvbc"][h] = refs
        st["grow_free"] = mms

    def emit_vgate(h):
        vg = {}
        prev = []
        for q in range(NTC - 1, -1, -1):
            sl = slice(CH * q, CH * (q + 1))
            dch = vop(lambda V, sl=sl: V.tensor_tensor(S["krotb"][:, 0:CH], S["y_k"][:, 1 + sl.start:1 + sl.stop],
                                                       S["y_k"][:, sl.start:sl.stop], ALU.subtract),
                      [st[("vev", h, q // 2)][q % 2], st["pvbc"][h][q]] + prev + st["ropece"].get(h - 1, []))
            km = vop(lambda V, q=q: V.tensor_tensor(S["rch2"][:, 0:CH], PS[7 - q][:, 0:CH], S["krotb"][:, 0:CH], ALU.mult), [dch])
            vgq = vop(lambda V, sl=sl: V.tensor_tensor(S["y_k"][:, 1 + sl.start:1 + sl.stop],
                                                       S["y_k"][:, 1 + sl.start:1 + sl.stop], S["rch2"][:, 0:CH], ALU.subtract), [km])
            vg[q] = vgq
            prev = [vgq]
            st["pbank"][7 - q] = [km]
        st["vgate"][h] = vg

    def emit_pkbc(h):
        refs = {}
        mms = []
        for q in range(NTC):
            sl = slice(CH * q, CH * (q + 1))
            bm = mm(PS[7 - q][:, 0:CH], S["onesrf"][:], S["rowbuf"][0:1, sl], start=True, stop=True,
                    deps=st["grow_ld"] + st["pbank"].pop(7 - q, []))
            refs[q] = bm
            mms.append(bm)
        st["pkbc"][h] = refs
        st["grow_free"] = mms

    def emit_transp(h):
        trs = {}
        prev = {}
        for stt_ in range(TK):
            bank = PS[stt_ % 2]
            t0 = tr(bank[:, 0:128], S["y_k"][:, 1 + 128 * stt_:1 + 128 * (stt_ + 1)], S["ident"][:],
                    deps=[st["vgate"][h][stt_ // (TK // NTC)]] + st["sbank"].pop(stt_ % 2, []) + prev.get(stt_ % 2, []))
            hi = sop(lambda Sc, stt_=stt_, bank=bank: Sc.copy(S["vtm_h"][:, 128 * stt_:128 * (stt_ + 1)], bank[:, 0:128]), [t0])
            lo = vop(lambda V, stt_=stt_, bank=bank: V.tensor_tensor(
                S["vtm_l"][:, 128 * stt_:128 * (stt_ + 1)], bank[:, 0:128],
                S["vtm_h"][:, 128 * stt_:128 * (stt_ + 1)], ALU.subtract), [hi])
            prev[stt_ % 2] = [lo]
            st["ykfree"][(h, stt_ // (TK // NTC))] = [t0]
            trs[stt_] = [t0, hi, lo]
        st["sbank"][0] = prev.get(0, [])
        st["sbank"][1] = prev.get(1, [])
        st["trs"][h] = [trs[TK - 1][2], trs[TK - 2][2]]

    def emit_kgate_rope(h):
        kg = {}
        prev = []
        for q in range(NTC - 1, -1, -1):
            sl = slice(CH * q, CH * (q + 1))
            dch = vop(lambda V, sl=sl: V.tensor_tensor(S["krotb"][:, 0:CH], S["dK"][:, 1 + sl.start:1 + sl.stop],
                                                       S["dK"][:, sl.start:sl.stop], ALU.subtract),
                      [st[("kev", h, q // 2)][q % 2], st["pkbc"][h][q]] + prev)
            km = vop(lambda V, q=q: V.tensor_tensor(S["rch2"][:, 0:CH], PS[7 - q][:, 0:CH], S["krotb"][:, 0:CH], ALU.mult), [dch])
            kgq = vop(lambda V, sl=sl: V.tensor_tensor(S["dK"][:, 1 + sl.start:1 + sl.stop],
                                                       S["dK"][:, 1 + sl.start:1 + sl.stop], S["rch2"][:, 0:CH], ALU.subtract), [km])
            kg[q] = kgq
            prev = [kgq]
            st["pbank"][7 - q] = [km]
        rope_refs = []
        sc_prev = st["sc_last"]
        for q in range(NTC):
            sl = slice(CH * q, CH * (q + 1))
            c1 = vop(lambda V, sl=sl: V.tensor_copy(S["krotb"][0:HLF, 0:CH], S["dK"][:, 1 + sl.start:1 + sl.stop][HLF:128, :]),
                     [kg[0]] + rope_refs[-1:])
            c2 = vop(lambda V, sl=sl: V.tensor_copy(S["krotb"][HLF:128, 0:CH], S["dK"][:, 1 + sl.start:1 + sl.stop][0:HLF, :]), [c1])
            t1 = vop(lambda V, sl=sl: V.tensor_tensor(S["rch2"][:, 0:CH], S["dK"][:, 1 + sl.start:1 + sl.stop],
                                                      S["cosK"][:, sl], ALU.mult), [c2])
            hi = vop(lambda V, sl=sl: V.scalar_tensor_tensor(S["K_h"][:, sl], S["krotb"][:, 0:CH], 1.0,
                                                             S["sinK2"][:, sl], ALU.mult, ALU.mult), [t1]) if False else None
            t2 = vop(lambda V, sl=sl: V.tensor_tensor(S["krotb"][:, 0:CH], S["krotb"][:, 0:CH], S["sinK2"][:, sl], ALU.mult), [t1])
            hi = vop(lambda V, sl=sl: V.tensor_tensor(S["K_h"][:, sl], S["rch2"][:, 0:CH], S["krotb"][:, 0:CH], ALU.add),
                     [t2] + sc_prev)
            t3 = vop(lambda V, sl=sl: V.tensor_tensor(S["rch2"][:, 0:CH], S["rch2"][:, 0:CH], S["krotb"][:, 0:CH], ALU.add), [hi])
            lo = vop(lambda V, sl=sl: V.tensor_tensor(S["K_l"][:, sl], S["rch2"][:, 0:CH], S["K_h"][:, sl], ALU.subtract), [t3])
            rope_refs = [lo]
            sc_prev = []
        st["ropece"][h] = rope_refs
        st["kgate"][h] = [kg[0]]

    def emit_scores(h, cc, fillers):
        nkt = KTPC * (cc + 1)
        csl = slice(CH * cc, CH * (cc + 1))
        par = h % 2
        l_mms = []
        last_pv = None
        ex_prev = {}
        pend = {}
        first_pv_dep = st["obank"]

        def emit_pv(kt):
            nonlocal last_pv
            ksl, ehs, els, eh_ev, el_ev = pend.pop(kt)
            m1 = mm(PS[2][:, 0:CH], S["vtm_h"][:, ksl], ehs, start=(kt == 0), stop=False,
                    deps=[eh_ev, st["trs"][h][0]] + (first_pv_dep if kt == 0 else []))
            mm(PS[2][:, 0:CH], S["vtm_l"][:, ksl], ehs, start=False, stop=False, deps=[st["trs"][h][1]])
            last_pv = mm(PS[2][:, 0:CH], S["vtm_h"][:, ksl], els, start=False, stop=(kt == nkt - 1), deps=[el_ev])
            l1 = mm(PS[3][0:1, 0:CH], S["onescb"][:], ehs, start=(kt == 0), stop=False,
                    deps=[eh_ev] + (st["lbank"] if kt == 0 else []))
            l2 = mm(PS[3][0:1, 0:CH], S["onescb"][:], els, start=False, stop=(kt == nkt - 1), deps=[el_ev])
            l_mms.append(l2)
            st["elast"][kt % 2] = [el_ev, last_pv if kt == nkt - 1 else m1]

        for kt in range(nkt):
            if fillers:
                fillers.pop(0)()
            bank = PS[kt % 2]
            ksl = slice(128 * kt, 128 * (kt + 1))
            d0 = st["ropece"][h] + st["qld"][h] + ex_prev.get(kt - 2, []) + st["sbank"].pop(kt % 2, [])
            mm(bank[:, 0:CH], S["K_h"][:, ksl], S["qt_h"][:, csl], start=True, stop=False, deps=d0)
            mm(bank[:, 0:CH], S["K_h"][:, ksl], S["qt_l"][:, csl], start=False, stop=False)
            mm(bank[:, 0:CH], S["K_l"][:, ksl], S["qt_h"][:, csl], start=False, stop=False)
            sc_mm = mm(bank[:, 0:CH], S["onesrb"][:], S["bhrow"][:, par * TQ + csl.start:par * TQ + csl.stop],
                       start=False, stop=True, deps=st["bh"][h])
            ebuf = kt % 2
            efs = S["ef"][:, ebuf * CH:(ebuf + 1) * CH]
            ehs = S["eh"][:, ebuf * CH:(ebuf + 1) * CH]
            els = S["el"][:, ebuf * CH:(ebuf + 1) * CH]
            bias_ap = S["negcumT"][:, H * kt + h:H * kt + h + 1]
            eb = st["elast"].get(ebuf, [])
            if kt >= KTPC * cc:
                mk = vop(lambda V, kt=kt, bank=bank, efs=efs: V.scalar_tensor_tensor(
                    efs, S["masks"][:, CH * kt:CH * (kt + 1)], NEG, bank[:, 0:CH], ALU.mult, ALU.add),
                    [sc_mm, ld["masks"]] + ex_prev.get(kt - 2, []) + eb)
                ex = sop(lambda Sc, efs=efs, bias_ap=bias_ap: Sc.activation(
                    efs, efs, AF.Exp, bias=bias_ap, scale=1.0), [mk] + eb)
                ex_psum_reader = mk
            else:
                ex = sop(lambda Sc, efs=efs, bank=bank, bias_ap=bias_ap: Sc.activation(
                    efs, bank[:, 0:CH], AF.Exp, bias=bias_ap, scale=1.0),
                    [sc_mm] + ex_prev.get(kt - 2, []) + eb)
                ex_psum_reader = ex
            eh_ev = sop(lambda Sc, ehs=ehs, efs=efs: Sc.copy(ehs, efs), [ex])
            el_ev = vop(lambda V, els=els, efs=efs, ehs=ehs: V.tensor_tensor(els, efs, ehs, ALU.subtract), [eh_ev])
            ex_prev[kt] = [ex_psum_reader]
            pend[kt] = (ksl, ehs, els, eh_ev, el_ev)
            if kt >= 1:
                emit_pv(kt - 1)
        emit_pv(nkt - 1)
        st["sc_last"] = [last_pv, l_mms[-1]]
        rin = vop(lambda V: V.reciprocal(S["invlrow"][:], PS[3][0:1, 0:CH]), l_mms[-1:])
        ibm = mm(PS[3][:, 0:CH], S["onesrf"][:], S["invlrow"][:], start=True, stop=True, deps=[rin])
        ibc = vop(lambda V: V.tensor_copy(S["invlbc"][:], PS[3][:, 0:CH]), [ibm])
        st["lbank"] = [ibc]
        st["sbank"][nkt % 2] = [ex_prev[nkt - 1][0]] if False else st["sbank"].get(nkt % 2, [])
        ob = cc % 2
        ohs = S["o_h"][:, ob * CH:(ob + 1) * CH]
        ols = S["o_l"][:, ob * CH:(ob + 1) * CH]
        hi = vop(lambda V, ohs=ohs: V.tensor_tensor(ohs, PS[2][:, 0:CH], S["invlbc"][:], ALU.mult),
                 [ibc, last_pv] + o_spills[-4:-2])
        tt_ = vop(lambda V: V.tensor_tensor(S["rch2"][:, 0:CH], PS[2][:, 0:CH], S["invlbc"][:], ALU.mult), [hi])
        lo = vop(lambda V, ohs=ohs, ols=ols: V.tensor_tensor(ols, S["rch2"][:, 0:CH], ohs, ALU.subtract), [tt_])
        st["obank"] = [hi, tt_]
        o_spills.append(E.dma("scalar", dr["oh_sp"][h][:, csl], ohs, deps=[hi]))
        o_spills.append(E.dma("scalar", dr["ol_sp"][h][:, csl], ols, deps=[lo]))

    st["grow_ld"] = []
    NPRE = 9
    emit_growload(0, "v")
    emit_wq_loads(0)
    carry = make_proj_closures(0)
    for it in range(H):
        if it >= 1:
            emit_scores(it - 1, 0, carry)
            emit_scores(it - 1, 1, carry)
        while carry:
            carry.pop(0)()
        emit_vgate(it)
        emit_pkbc(it)
        if it + 1 < H:
            emit_wq_loads(it + 1)
        emit_transp(it)
        emit_kgate_rope(it)
        emit_qload(it)
        if it + 1 < H:
            emit_growload(it + 1, "v")
            carry = make_proj_closures(it + 1)
            for _ in range(min(NPRE, len(carry))):
                carry.pop(0)()
        else:
            carry = []
    emit_scores(H - 1, 0, carry)
    emit_scores(H - 1, 1, carry)
    A2_END = E.barrier()

    B0_END = A2_END

    bx = []
    for ct in range(CT):
        bx.append(E.dma("sync", S["bxnq_h"][:, ct * TQ:(ct + 1) * TQ], dr["xnqh_sp"][ct], deps=(B0_END if ct == 0 else [])))
        bx.append(E.dma("sync", S["bxnq_l"][:, ct * TQ:(ct + 1) * TQ], dr["xnql_sp"][ct], deps=[]))
    conv_prev = {}
    sil_prev = []
    su_prev = []
    suf_prev = []
    last_down_mm = []
    ga_all = {}
    dbank_last = {}
    wd_conv = {}
    for hc in range(NHC):
        tiles = list(range(hc * HIDCH, min(HT, (hc + 1) * HIDCH)))
        nj = len(tiles)
        su_refs = {}
        dprev = []
        for j, ht in enumerate(tiles):
            gslot = S["wbf"][:, (j % 2) * CT * 128:(j % 2 + 1) * CT * 128]
            uslot = S["wbf"][:, (2 + j % 2) * CT * 128:(3 + j % 2) * CT * 128]
            dg = E.dma("sync", S["wint"][:].rearrange("p (a d) -> p a d", d=128),
                       dr["wgT"][ht].rearrange("(a p) d -> p a d", p=128),
                       deps=conv_prev.get("w", []) + (bx if hc == 0 and j == 0 else []))
            cg = vop(lambda V, gslot=gslot: V.tensor_scalar_add(gslot, S["wint"][:], -128.0),
                     [dg] + conv_prev.get(("g", j % 2), []) + (B0_END if hc == 0 and j == 0 else []))
            du = E.dma("sync", S["wint"][:].rearrange("p (a d) -> p a d", d=128),
                       dr["wuT"][ht].rearrange("(a p) d -> p a d", p=128), deps=[cg])
            cu = vop(lambda V, uslot=uslot: V.tensor_scalar_add(uslot, S["wint"][:], -128.0),
                     [du] + conv_prev.get(("u", j % 2), []))
            dd = E.dma("sync", S["wint"][:], dr["wdT"][ht], deps=[cu])
            wds = S["wdc"][:, j * CT * 128:(j + 1) * CT * 128]
            cd = vop(lambda V, wds=wds: V.tensor_scalar_add(wds, S["wint"][:], -128.0),
                     [dd] + wd_conv.get((hc - 1, j), []))
            wd_conv[(hc, j)] = [cd]
            conv_prev["w"] = [cd]
            g_sil = {}
            g_last = {}
            for tc in range(NCH):
                bank = PS[4 + tc % 2]
                last = None
                for ct in range(CT):
                    wsl = slice(ct * 128, (ct + 1) * 128)
                    xsl = slice(ct * TQ + CH * tc, ct * TQ + CH * (tc + 1))
                    d0 = [cg, bx[2 * ct], bx[2 * ct + 1]] + sil_prev
                    mm(bank[:, 0:CH], gslot[:, wsl], S["bxnq_h"][:, xsl], start=(ct == 0), stop=False,
                       deps=d0 if ct == 0 else [cg])
                    last = mm(bank[:, 0:CH], gslot[:, wsl], S["bxnq_l"][:, xsl], start=False, stop=(ct == CT - 1))
                gs = sop(lambda Sc, tc=tc, bank=bank, ht=ht: Sc.activation(
                    S["gsil"][:, CH * (tc % 2):CH * (tc % 2 + 1)], bank[:, 0:CH], AF.Silu,
                    scale=S["sg"][:, ht:ht + 1], bias=0.0), [last, ld["sg"]] + su_prev)
                g_sil[tc] = gs
                g_last[tc] = last
            sil_prev = [g_sil[NCH - 1]]
            conv_prev[("g", j % 2)] = [g_last[NCH - 1]]
            u_last = None
            for tc in range(NCH):
                bank = PS[6 + tc % 2]
                last = None
                for ct in range(CT):
                    wsl = slice(ct * 128, (ct + 1) * 128)
                    xsl = slice(ct * TQ + CH * tc, ct * TQ + CH * (tc + 1))
                    mm(bank[:, 0:CH], uslot[:, wsl], S["bxnq_h"][:, xsl], start=(ct == 0), stop=False,
                       deps=[cu] + (su_prev if ct == 0 else []))
                    last = mm(bank[:, 0:CH], uslot[:, wsl], S["bxnq_l"][:, xsl], start=False, stop=(ct == CT - 1))
                suf = vop(lambda V, tc=tc, bank=bank, ht=ht: V.scalar_tensor_tensor(
                    S["su_f"][:], bank[:, 0:CH], S["sup"][:, ht:ht + 1],
                    S["gsil"][:, CH * (tc % 2):CH * (tc % 2 + 1)], ALU.mult, ALU.mult),
                    [last, g_sil[tc], ld["sup"]] + suf_prev)
                shs = S["su_h"][:, j * TQ + CH * tc:j * TQ + CH * (tc + 1)]
                sls = S["su_l"][:, j * TQ + CH * tc:j * TQ + CH * (tc + 1)]
                sh1 = sop(lambda Sc, shs=shs: Sc.copy(shs, S["su_f"][:]), [suf] + last_down_mm)
                sl1 = vop(lambda V, sls=sls, shs=shs: V.tensor_tensor(sls, S["su_f"][:], shs, ALU.subtract), [sh1])
                su_refs[(j, tc)] = [sh1, sl1]
                suf_prev = [sh1, sl1]
                u_last = last
            su_prev = su_refs[(j, NCH - 1)]
            conv_prev[("u", j % 2)] = [u_last]
        for ct in range(CT):
            for tc in range(NCH):
                bank = PS[tc % 2]
                last = None
                for j in range(nj):
                    wsl = slice(j * CT * 128 + ct * 128, j * CT * 128 + (ct + 1) * 128)
                    ssl = slice(j * TQ + CH * tc, j * TQ + CH * (tc + 1))
                    d0 = su_refs[(j, tc)] + [wd_conv[(hc, j)][0]] + ((ga_all.get((hc - 1, ct, tc), []) + dbank_last.get(tc % 2, [])) if j == 0 else [])
                    mm(bank[:, 0:CH], S["wdc"][:, wsl], S["su_h"][:, ssl], start=(j == 0), stop=False, deps=d0)
                    last = mm(bank[:, 0:CH], S["wdc"][:, wsl], S["su_l"][:, ssl], start=False, stop=(j == nj - 1))
                gsl = slice(ct * TQ + CH * tc, ct * TQ + CH * (tc + 1))
                if hc == 0:
                    ga = vop(lambda V, gsl=gsl, bank=bank: V.tensor_copy(S["g_acc"][:, gsl], bank[:, 0:CH]), [last])
                else:
                    ga = vop(lambda V, gsl=gsl, bank=bank: V.tensor_tensor(
                        S["g_acc"][:, gsl], S["g_acc"][:, gsl], bank[:, 0:CH], ALU.add),
                        [last] + ga_all.get((hc - 1, ct, tc), []))
                ga_all[(hc, ct, tc)] = [ga]
                dbank_last[tc % 2] = [ga]
                last_down_mm = [last]
    B1_END = E.barrier()
    ob_ld = []
    for hh in range(H):
        ob_ld.append(E.dma("scalar", S["ob_h"][:, hh * TQ:(hh + 1) * TQ], dr["oh_sp"][hh],
                           deps=(list(suf_prev) + [o_spills[-1]]) if hh == 0 else []))
        ob_ld.append(E.dma("scalar", S["ob_l"][:, hh * TQ:(hh + 1) * TQ], dr["ol_sp"][hh], deps=[]))

    st_done = []
    v1_hist = []
    v2_hist = []
    wo_rd = {}
    wsplit = {}
    def emit_wosplit(ct):
        buf = ct % 2
        wfs = S["wo_f"][:, buf * CT * 128:(buf + 1) * CT * 128]
        hcols = CT * 128 // 2
        d0 = (B1_END if ct < 2 else []) + wsplit.get(ct - 2, [])
        dwo_a = E.dma("sync", wfs[:, 0:hcols].rearrange("p (a d) -> p a d", d=128),
                      dr["woT4"][ct, 0:CT // 2].rearrange("a p d -> p a d"), deps=d0)
        dwo_b = E.dma("gpsimd", wfs[:, hcols:2 * hcols].rearrange("p (a d) -> p a d", d=128),
                      dr["woT4"][ct, CT // 2:CT].rearrange("a p d -> p a d"), deps=d0)
        whs = S["wo_h"][:, buf * CT * 128:(buf + 1) * CT * 128]
        wls = S["wo_l"][:, buf * CT * 128:(buf + 1) * CT * 128]
        whi = vop(lambda V, whs=whs, wfs=wfs: V.tensor_copy(whs, wfs),
                  [dwo_a, dwo_b] + wo_rd.get(ct - 2, []) + (B1_END if ct < 2 else []))
        wlo = vop(lambda V, whs=whs, wls=wls, wfs=wfs: V.tensor_tensor(wls, wfs, whs, ALU.subtract), [whi])
        wsplit[ct] = [wlo]
        wo_rd[ct] = [wlo]
        return whs, wls, whi, wlo

    wosp = {0: emit_wosplit(0)}
    for ct in range(CT):
        if ct + 1 < CT:
            wosp[ct + 1] = emit_wosplit(ct + 1)
        whs, wls, whi, wlo = wosp[ct]
        for cc in range(NCH):
            csl = slice(CH * cc, CH * (cc + 1))
            bank = PS[2 + (2 * ct + cc) % 2]
            last = None
            for cp in range(CT):
                wsl = slice(cp * 128, (cp + 1) * 128)
                osl = slice(cp * TQ + CH * cc, cp * TQ + CH * (cc + 1))
                d0 = [whi, ob_ld[2 * cp], ob_ld[2 * cp + 1]] + ((v1_hist[-2:-1] if len(v1_hist) >= 2 else []) if cp == 0 else [])
                mm(bank[:, 0:CH], whs[:, wsl], S["ob_h"][:, osl], start=(cp == 0), stop=False, deps=d0)
                mm(bank[:, 0:CH], wls[:, wsl], S["ob_h"][:, osl], start=False, stop=False, deps=[wlo])
                last = mm(bank[:, 0:CH], whs[:, wsl], S["ob_l"][:, osl], start=False, stop=(cp == CT - 1))
            wo_rd[ct] = wo_rd[ct] + [last]
            gb = sop(lambda Sc, ct=ct, cc=cc, b2f=(2 * ct + cc) % 2: Sc.activation(
                S["gsil2"][:, b2f * CH:(b2f + 1) * CH],
                S["g_acc"][:, ct * TQ + CH * cc:ct * TQ + CH * (cc + 1)],
                AF.Silu, scale=S["sdn"][:, ct:ct + 1], bias=0.0),
                [ld["sdn"]] + (B1_END if ct == 0 and cc == 0 else []) + (v1_hist[-2:-1] if len(v1_hist) >= 2 else []))
            b2f = (2 * ct + cc) % 2
            dx = E.dma("scalar", S["xres"][:, b2f * CH:(b2f + 1) * CH],
                       dr["xqresT"][128 * ct:128 * (ct + 1), csl],
                       deps=(v2_hist[-2:-1] if len(v2_hist) >= 2 else []))
            v1 = vop(lambda V, b2f=b2f, bank=bank: V.tensor_tensor(
                S["outsb"][:, b2f * CH:(b2f + 1) * CH], bank[:, 0:CH],
                S["gsil2"][:, b2f * CH:(b2f + 1) * CH], ALU.mult),
                [last, gb] + (st_done[-2:-1] if len(st_done) >= 2 else []))
            v2 = vop(lambda V, b2f=b2f: V.tensor_tensor(
                S["outsb"][:, b2f * CH:(b2f + 1) * CH], S["outsb"][:, b2f * CH:(b2f + 1) * CH],
                S["xres"][:, b2f * CH:(b2f + 1) * CH], ALU.add), [v1, dx])
            st_ = E.dma("sync", outT[128 * ct:128 * (ct + 1), csl], S["outsb"][:, b2f * CH:(b2f + 1) * CH], deps=[v2])
            st_done.append(st_)
            v1_hist.append(v1)
            v2_hist.append(v2)
    tail = E.barrier() + st_done[-1:]
    E.ev("sync", lambda Eng: Eng.nop(), tail)

    E.finalize()

    import contextlib
    with contextlib.ExitStack() as ctx:
        sems = {}
        rings = {}
        for e in ENGS:
            sems[e] = ctx.enter_context(nc.semaphore(f"sm_{e}"))
            if E.ndma[e]:
                rings[e] = [ctx.enter_context(nc.semaphore(f"dq_{e}_{k}"))
                            for k in range(min(NRING, E.ndma[e]))]
        block = ctx.enter_context(nc.Block())

        @block.tensor
        def _(Eng):
            E.replay("tensor", Eng, sems, rings)

        @block.vector
        def _(Eng):
            E.replay("vector", Eng, sems, rings)

        @block.scalar
        def _(Eng):
            E.replay("scalar", Eng, sems, rings)

        @block.gpsimd
        def _(Eng):
            E.replay("gpsimd", Eng, sems, rings)

        @block.sync
        def _(Eng):
            E.replay("sync", Eng, sems, rings)

    return nc



def _prep_shared(cfg, W):
    C = cfg["C"]; T = cfg["T"]; H = cfg["H"]; HID = cfg["HID"]
    CT = C // 128
    HT = HID // 128
    f32 = np.float32
    sh = {}
    sh["wqT"] = np.ascontiguousarray(W["Wq"].T.reshape(C, H, 128).transpose(1, 0, 2)).astype(f32)
    sh["wkT"] = np.ascontiguousarray(W["Wk"].T.reshape(C, H, 128).transpose(1, 0, 2)).astype(f32)
    sh["wvT"] = np.ascontiguousarray(W["Wv"].T.reshape(C, H, 128).transpose(1, 0, 2)).astype(f32)
    sh["wsqT"] = np.ascontiguousarray(W["Wsq"].T.reshape(CT, 128, H)).astype(f32)
    wsp = np.concatenate([W["Wf"].T, W["Wsk"].T, W["Wsv"].T], axis=1)
    sh["wspT"] = np.ascontiguousarray(wsp.reshape(CT, 128, 3 * H)).astype(f32)
    bfp = np.zeros((3 * H, 1), f32)
    bfp[0:H, 0] = W["bf"]
    sh["bfpack"] = bfp
    WoT = W["Wo"].T
    sh["woT4"] = np.ascontiguousarray(WoT.reshape(CT, 128, CT, 128).transpose(2, 0, 1, 3)).astype(f32)
    sh["wgT"] = np.ascontiguousarray(W["wq_gate"].T.reshape(C, HT, 128).transpose(1, 0, 2)).astype(np.int32)
    sh["wuT"] = np.ascontiguousarray(W["wq_up"].T.reshape(C, HT, 128).transpose(1, 0, 2)).astype(np.int32)
    sh["wdT"] = np.ascontiguousarray(W["wq_down"].T.reshape(HT, 128, C)).astype(np.int32)
    sh["sg"] = np.ascontiguousarray(W["s_gate"].reshape(HT, 128).T).astype(f32)
    sh["sup"] = np.ascontiguousarray(W["s_up"].reshape(HT, 128).T).astype(f32)
    sh["sdn"] = np.ascontiguousarray(W["s_down"].reshape(CT, 128).T).astype(f32)
    cos, sin = _ref_rope_tables(cfg)
    cos = cos.T
    sin = sin.T
    sh["cosK"] = np.concatenate([cos, cos], axis=0).astype(f32)
    sh["sinK2"] = np.concatenate([sin, -sin], axis=0).astype(f32)
    sh["ident"] = np.eye(128, dtype=f32)
    sh["onescf"] = np.ones((128, 1), f32)
    sh["onesrf"] = np.ones((1, 128), f32)
    sh["onescb"] = np.ones((128, 1), BF16NP)
    sh["onesrb"] = np.ones((1, 128), BF16NP)
    return sh


def _prep_parity(cfg, h):
    T = cfg["T"]; CH = cfg["CHUNK"]
    TK = T // 128
    NB = TK // 2
    TQ = NB * 128
    D = 128
    i = np.arange(TQ)
    tok = 128 * (2 * (i // 128) + h) + (i % 128)
    cosT, sinT = _ref_rope_tables(cfg)
    cos = cosT[tok].T
    sin = sinT[tok].T
    out = {}
    out["cosQ"] = np.concatenate([cos, cos], axis=0).astype(np.float32)
    out["sinQ2"] = np.concatenate([sin, -sin], axis=0).astype(np.float32)
    sel = np.zeros((T, TQ), np.float32)
    sel[tok, np.arange(TQ)] = 1.0
    out["selq"] = np.ascontiguousarray(sel.reshape(TK, 128, TQ)).astype(BF16NP)
    masks = np.zeros((TK, 128, CH), np.uint8)
    for kt in range(TK):
        cc = kt // (2 * (CH // 128))
        tl = np.arange(CH)
        ql = CH * cc + tl
        g = 128 * (2 * (ql // 128) + h) + (ql % 128)
        srow = 128 * kt + np.arange(128)
        masks[kt] = (srow[:, None] > g[None, :]).astype(np.uint8)
    out["masks"] = masks
    return out


def _host_prep(cfg, inputs):
    C = cfg["C"]; T = cfg["T"]
    TK = T // 128
    NB = TK // 2
    TQ = NB * 128
    x = np.asarray(inputs["L0_attn_x"], np.float32)
    W = dict(
        Wq=np.asarray(inputs["L0_attn_Wq"], np.float32),
        Wsq=np.asarray(inputs["L0_attn_Wsq"], np.float32),
        Wk=np.asarray(inputs["L0_attn_Wk"], np.float32),
        Wsk=np.asarray(inputs["L0_attn_Wsk"], np.float32),
        Wv=np.asarray(inputs["L0_attn_Wv"], np.float32),
        Wsv=np.asarray(inputs["L0_attn_Wsv"], np.float32),
        Wf=np.asarray(inputs["L0_attn_Wf"], np.float32),
        bf=np.asarray(inputs["L0_attn_bf"], np.float32),
        Wo=np.asarray(inputs["L0_attn_Wo"], np.float32),
        wq_gate=np.asarray(inputs["L0_ffn_wq_gate"], np.int32),
        s_gate=np.asarray(inputs["L0_ffn_s_gate"], np.float32),
        wq_up=np.asarray(inputs["L0_ffn_wq_up"], np.int32),
        s_up=np.asarray(inputs["L0_ffn_s_up"], np.float32),
        wq_down=np.asarray(inputs["L0_ffn_wq_down"], np.int32),
        s_down=np.asarray(inputs["L0_ffn_s_down"], np.float32),
    )
    sh = _prep_shared(cfg, W)
    par = [_prep_parity(cfg, 0), _prep_parity(cfg, 1)]
    ncores = 2 * x.shape[0]
    in_maps = []
    own_tok = []
    for c in range(ncores):
        b, h = c // 2, c % 2
        xkT = np.ascontiguousarray(x[b].T)
        i = np.arange(TQ)
        tok = 128 * (2 * (i // 128) + h) + (i % 128)
        own_tok.append(tok)
        cols = np.empty((NB, 129), np.int64)
        for j in range(NB):
            g0 = 128 * (2 * j + h)
            cols[j] = np.arange(g0 - 1, g0 + 128)
        colsf = cols.reshape(-1)
        take = np.clip(colsf, 0, T - 1)
        xqT = xkT[:, take].copy()
        zm = colsf < 0
        if zm.any():
            xqT[:, zm] = 0.0
        m = dict(sh)
        m.update(par[h])
        m["xkT"] = xkT
        m["xqT"] = np.ascontiguousarray(xqT)
        m["xqresT"] = np.ascontiguousarray(xkT[:, tok])
        in_maps.append(m)
    return in_maps, own_tok, x.shape


def kernel(**inputs):
    cfg = full_cfg()
    in_maps, own_tok, xshape = _host_prep(cfg, inputs)
    nc = build_nc(cfg)
    res = run_bass_kernel_spmd(nc, in_maps, core_ids=list(range(len(in_maps))))
    B, T, C = xshape
    out = np.empty((B, T, C), np.float32)
    for c in range(len(in_maps)):
        b = c // 2
        outT = np.asarray(res.results[c]["outT"])
        out[b, own_tok[c], :] = outT.T
    return out
